# version 01
# baseline (speedup 1.0000x reference)
import math
import numpy as np
from contextlib import ExitStack

import concourse.bass as bass
import concourse.mybir as mybir
from concourse.bass_utils import run_bass_kernel_spmd

F32 = mybir.dt.float32
BF16 = mybir.dt.bfloat16
AF = mybir.ActivationFunctionType
ALU = mybir.AluOpType
AX = mybir.AxisListType

L = 2048
DM = 1024
H = 32
P = 64
N = 64
DI = H * P
NCH = 16
Q = 128
DPROJ = 5312
E = 8
DFF = 4096
TWO_PI = 2.0 * math.pi
MAGIC = 1.5 * (2.0 ** 23)
PI = math.pi
PIC = math.pi * (1.0 - 1e-6)

ZC0, XC0, BCC0, DLC0, THC0 = 0, 2048, 4096, 4224, 4288


class Sem:
    def __init__(self, handle):
        self.h = handle
        self.count = 0


class Emitter:
    def __init__(self, name):
        self.name = name
        self.items = []
        self._waited = {}

    def wait(self, sem, val=None):
        v = sem.count if val is None else val
        if self._waited.get(id(sem), -1) >= v or v <= 0:
            return
        self._waited[id(sem)] = v
        self.items.append(lambda e, s=sem, v=v: e.wait_ge(s.h, v))

    def op(self, fn, inc=None, amt=1):
        if inc is not None:
            inc.count += amt
            def run(e, fn=fn, inc=inc, amt=amt):
                fn(e).then_inc(inc.h, amt)
        else:
            def run(e, fn=fn):
                fn(e)
        self.items.append(run)

    def replay(self, eng):
        for it in self.items:
            it(eng)


def build_program(upto=None, taps=()):
    nc = bass.Bass("TRN2", target_bir_lowering=False)
    dt = nc.dram_tensor

    u_d = dt("u", [L, DM], F32, kind="ExternalInput").ap()
    winT_d = dt("winT", [DM, DPROJ], F32, kind="ExternalInput").ap()
    woutT_d = dt("woutT", [DI, DM], F32, kind="ExternalInput").ap()
    wr_d = dt("wr", [DM, E], F32, kind="ExternalInput").ap()
    br_d = dt("br", [E, 1], F32, kind="ExternalInput").ap()
    w1_d = dt("w1", [E, 32, 8, 128, 128], F32, kind="ExternalInput").ap()
    b1_d = dt("b1c", [E, 128, 32], F32, kind="ExternalInput").ap()
    w2_d = dt("w2", [E, 33, 128, DM], F32, kind="ExternalInput").ap()
    dtb_d = dt("dtb", [H, 1], F32, kind="ExternalInput").ap()
    alog_d = dt("alog", [H, 1], F32, kind="ExternalInput").ap()
    dsk_d = dt("dsk4", [128, 4, 8], F32, kind="ExternalInput").ap()
    bnw_d = dt("bnw", [N, 1], F32, kind="ExternalInput").ap()
    cnw_d = dt("cnw", [N, 1], F32, kind="ExternalInput").ap()
    bbv_d = dt("bbv", [2, 128, 1024], F32, kind="ExternalInput").ap()
    cbv_d = dt("cbv", [2, 128, 1024], F32, kind="ExternalInput").ap()
    nwrep_d = dt("nwrep", [128, DI], F32, kind="ExternalInput").ap()
    ident_d = dt("ident", [128, 128], F32, kind="ExternalInput").ap()
    mneg_d = dt("mneg", [128, 128], F32, kind="ExternalInput").ap()
    repl1_d = dt("repl1", [N, 128], F32, kind="ExternalInput").ap()
    repl2_d = dt("repl2", [N, 128], F32, kind="ExternalInput").ap()
    repl4_d = dt("repl4", [H, 128], F32, kind="ExternalInput").ap()
    selj8_d = dt("selj8", [H, 8], F32, kind="ExternalInput").ap()
    out_d = dt("out", [L, DM], F32, kind="ExternalOutput").ap()

    winTbf_d = dt("winTbf", [DM, DPROJ], BF16, kind="Internal").ap()
    woutTbf_d = dt("woutTbf", [DI, DM], BF16, kind="Internal").ap()
    r1scr_d = dt("r1scr", [L, DM], F32, kind="Internal").ap()


    st = ExitStack()
    SBUF = {}

    def sb(n, sh, d=F32):
        t = st.enter_context(nc.sbuf_tensor(n, sh, d))
        SBUF[n] = t
        return t

    ps = lambda n, sh: st.enter_context(nc.psum_tensor(n, sh, F32))
    mksem = lambda n: Sem(st.enter_context(nc.semaphore(n)))

    with st:
        xT = sb("xT", [128, 8 * L], BF16)
        acc = sb("acc", [128, 4 * DM], BF16)
        h1T = sb("h1T", [128, 33 * 512], BF16)
        zsb = h1T[:, 0:4096]
        Xsb = h1T[:, 4096:8192]
        tcV = h1T[:, 8192:12288].bitcast(F32)
        bpp = h1T[:, 12288:14336]
        sil = h1T[:, 14336:16384]
        bcT = sb("bcT", [128, 2 * 128], F32)
        dtT = sb("dtT", [32, 2 * 128], F32)
        lamT = sb("lamT", [32, 2 * 128], F32)
        g_t = sb("g_t", [32, 2 * 128], F32)
        q_t = sb("q_t", [32, 2 * 128], F32)
        wgb = [sb(f"wgb{i}", [128, 8 * 512], BF16) for i in range(2)]
        wob = [sb(f"wob{i}", [128, DM], BF16) for i in range(2)]
        usb = [sb(f"usb{i}", [128, DM], F32) for i in range(2)]

        ghat = sb("ghat", [32, 128], F32)
        s_t = sb("s_t", [32, 128], F32)
        S_T = sb("S_T", [32, 128], F32)
        lng = sb("lng", [32, 128], F32)
        wT = sb("wT", [32, 128], F32)
        expd = sb("expd", [32, 128], F32)
        pack = sb("pack", [128, 128], F32)
        packN = sb("packN", [128, 128], F32)
        E_nat = sb("E_nat", [128, 32], F32)
        dec_c = sb("dec_c", [32, 1], F32)
        dec4i = sb("dec4i", [32, 8], F32)
        dec4 = sb("dec4", [128, 8], F32)
        wvsel = sb("wvsel", [32, 8 * 128], BF16)
        state = sb("state", [128, 1024], F32)
        state_bf = sb("state_bf", [128, 1024], BF16)
        sqT = sb("sqT", [128, 128], F32)
        invB = sb("invB", [1, 128], F32)
        invC = sb("invC", [1, 128], F32)
        brawn = sb("brawn", [64, 128], BF16)
        brawn8 = sb("brawn8", [64, 8 * 128], BF16)
        crawn = sb("crawn", [64, 128], BF16)
        crawn8 = sb("crawn8", [64, 8 * 128], BF16)
        vB1 = sb("vB1", [128, 1024], BF16)
        vB2 = sb("vB2", [128, 1024], BF16)
        vC1 = sb("vC1", [128, 1024], BF16)
        vC2 = sb("vC2", [128, 1024], BF16)
        rta = sb("rta", [128, 1024], BF16)
        rtb = sb("rtb", [128, 1024], BF16)
        vBW1 = rta
        vBW2 = rtb
        rmod = sb("rmod", [128, 1024], F32)
        cosV = sb("cosV", [128, 1024], BF16)
        sinV = sb("sinV", [128, 1024], BF16)
        pre_t = [sb(f"pre{i}", [128, 128], F32) for i in range(2)]
        D2 = [sb(f"D2_{i}", [128, 128], F32) for i in range(2)]
        G_bf = [sb(f"G_{i}", [128, 128], BF16) for i in range(2)]
        ysc = sb("ysc", [128, 512], F32)
        yg = sb("yg", [128, DI], BF16)
        I32 = sb("I32", [128, 128], F32)
        Ibf = sb("Ibf", [128, 128], BF16)
        Mneg = sb("Mneg", [128, 128], F32)
        rp1 = sb("rp1", [64, 128], F32)
        rp2 = sb("rp2", [64, 128], F32)
        repl1w = sb("repl1w", [64, 128], BF16)
        repl2w = sb("repl2w", [64, 128], BF16)
        repl1wc = sb("repl1wc", [64, 128], BF16)
        repl2wc = sb("repl2wc", [64, 128], BF16)
        repl4s = sb("repl4s", [32, 128], F32)
        repl4bf = sb("repl4bf", [32, 128], BF16)
        selj8s = sb("selj8s", [32, 8], F32)
        bbv_s = sb("bbv_s", [128, 2 * 1024], BF16)
        cbv_s = sb("cbv_s", [128, 2 * 1024], BF16)
        nwrep_s = sb("nwrep_s", [128, DI], BF16)
        onec = sb("onec", [128, 128], F32)
        a_col = sb("a_col", [32, 1], F32)
        acol_t = sb("acol_t", [32, 1], F32)
        dtb_s = sb("dtb_s", [32, 1], F32)
        dsk_s = sb("dsk_s", [128, 32], F32)
        bnw_s = sb("bnw_s", [64, 1], F32)
        cnw_s = sb("cnw_s", [64, 1], F32)
        wr_s = sb("wr_s", [128, 8 * 8], F32)
        br_s = sb("br_s", [8, 1], F32)
        zcol = sb("zcol", [32, 1], F32)
        rln = sb("rln", [128, DM], F32)
        xln = sb("xln", [128, DM], F32)
        carg = rln
        atmp = xln
        xsq = rmod
        cst32 = [rln, xln]
        cstbf_t = None
        xsq = sb("xsq", [128, DM], F32)
        xt32 = sb("xt32", [128, 128], F32)
        logT = sb("logT", [8, 128], F32)
        cols = sb("cols", [128, 16], F32)
        cns = sb("cns", [128, 6], F32)
        gsel = sb("gsel", [128, 8 * 16], F32)
        w1st = [sb(f"w1st{i}", [128, 512], F32) for i in range(4)]
        w1bf = [sb(f"w1bf{i}", [128, 1024], BF16) for i in range(2)]
        w2st = [sb(f"w2st{i}", [128, 512], F32) for i in range(4)]
        w2bf = [sb(f"w2bf{i}", [128, DM], BF16) for i in range(2)]
        b1c_s = [sb(f"b1c{i}", [128, 32], F32) for i in range(2)]
        xo32 = [xln, xln]

        pb = [ps(f"pb{i}", [128, 512]) for i in range(8)]

        for _n, _a in (("zsb", zsb), ("Xsb", Xsb), ("tcV", tcV),
                       ("bpp", bpp), ("sil", sil)):
            SBUF[_n] = _a
        sy = mksem("sy")
        sr = mksem("sr")
        gq = mksem("gq")
        pe = mksem("pe")
        ve = mksem("ve")
        se = mksem("se")
        so = mksem("so")

        SY = Emitter("sync"); GQ = Emitter("gpsimd"); PE = Emitter("tensor")
        V = Emitter("vector"); S = Emitter("scalar")

        for a, b in [(I32, ident_d), (Mneg, mneg_d), (repl4s, repl4_d),
                     (selj8s, selj8_d), (a_col, alog_d), (dtb_s, dtb_d),
                     (dsk_s, dsk_d.rearrange("p a b -> p (a b)")), (bnw_s, bnw_d), (cnw_s, cnw_d),
                     (br_s, br_d), (rp1, repl1_d),
                     (rp2, repl2_d)]:
            SY.op(lambda e, a=a, b=b: e.dma_start(out=a[:], in_=b), sy, 16)
        SY.op(lambda e: e.dma_start(out=wr_s[:].rearrange("p (k c) -> p k c", k=8),
                                    in_=wr_d.rearrange("(k p) c -> p k c", p=128)), sy, 16)

        V.wait(sy)
        V.op(lambda e: e.memset(onec[:], 1.0))
        V.op(lambda e: e.tensor_copy(out=Ibf[:], in_=I32[:]))
        V.op(lambda e: e.tensor_copy(out=repl4bf[:], in_=repl4s[:]))
        V.op(lambda e: e.tensor_scalar(out=repl1w[:], in0=rp1[:], scalar1=bnw_s[:],
                                       scalar2=None, op0=ALU.mult))
        V.op(lambda e: e.tensor_scalar(out=repl2w[:], in0=rp2[:], scalar1=bnw_s[:],
                                       scalar2=None, op0=ALU.mult))
        V.op(lambda e: e.tensor_scalar(out=repl1wc[:], in0=rp1[:], scalar1=cnw_s[:],
                                       scalar2=None, op0=ALU.mult))
        V.op(lambda e: e.tensor_scalar(out=repl2wc[:], in0=rp2[:], scalar1=cnw_s[:],
                                       scalar2=None, op0=ALU.mult))
        V.op(lambda e: e.memset(zcol[:], 0.0))
        V.op(lambda e: e.memset(cns[:, 0:1], 1e-5))
        V.op(lambda e: e.memset(cns[:, 1:2], 1.0))
        V.op(lambda e: e.memset(cns[:, 2:3], 1.0 / DI))
        V.op(lambda e: e.memset(cns[:, 3:4], 1.0 / DM))
        V.op(lambda e: e.memset(cns[:, 4:5], -1.0))
        V.op(lambda e: e.memset(state[:], 0.0))
        V.op(lambda e: e.memset(state_bf[:], 0.0))
        V.op(lambda e: e.memset(h1T[:, 32 * 512:33 * 512], 0.0))
        V.op(lambda e: e.memset(h1T[0:1, 32 * 512:32 * 512 + 512], 1.0), ve)
        S.wait(sy)
        S.op(lambda e: e.activation(out=acol_t[:], in_=a_col[:], func=AF.Exp), se)
        V.wait(se)
        V.op(lambda e: e.tensor_scalar(out=a_col[:], in0=acol_t[:], scalar1=-1.0,
                                       scalar2=None, op0=ALU.mult), ve)

        cast_jobs = []
        wpieces = [(i * 1024, 1024) for i in range(5)] + [(5120, 192)]
        for k in range(8):
            for s0, w in wpieces:
                cast_jobs.append((winT_d[k * 128:(k + 1) * 128, s0:s0 + w],
                                  winTbf_d[k * 128:(k + 1) * 128, s0:s0 + w],
                                  None, w))
        for k in range(16):
            cast_jobs.append((woutT_d[k * 128:(k + 1) * 128, :],
                              woutTbf_d[k * 128:(k + 1) * 128, :], None, DM))
        for half in range(2):
            cast_jobs.append((bbv_d[half], None,
                              bbv_s[:, half * 1024:(half + 1) * 1024], 1024))
            cast_jobs.append((cbv_d[half], None,
                              cbv_s[:, half * 1024:(half + 1) * 1024], 1024))
            cast_jobs.append((nwrep_d[:, half * 1024:(half + 1) * 1024], None,
                              nwrep_s[:, half * 1024:(half + 1) * 1024], 1024))
        for idx, (srcap, dstap, sdst, w) in enumerate(cast_jobs):
            par = idx % 2
            GQ.wait(ve)
            GQ.op(lambda e, s=srcap, w=w, par=par: e.dma_start(
                out=cst32[par][:, 0:w], in_=s), gq, 16)
            V.wait(gq)
            if sdst is None:
                V.op(lambda e, w=w, par=par: e.tensor_copy(
                    out=yg[:, par * 1024:par * 1024 + w],
                    in_=cst32[par][:, 0:w]), ve)
                GQ.wait(ve)
                GQ.op(lambda e, d=dstap, w=w, par=par: e.dma_start(
                    out=d, in_=yg[:, par * 1024:par * 1024 + w]), gq, 16)
            else:
                V.op(lambda e, sdst=sdst, w=w, par=par: e.tensor_copy(
                    out=sdst, in_=cst32[par][:, 0:w]), ve)
        scratch_done = gq.count

        pe_tile_end = {}
        for t in range(16):
            par = t % 2
            if t >= 2:
                SY.wait(pe, pe_tile_end[t - 2])
            SY.op(lambda e, t=t, par=par: e.dma_start(
                out=usb[par][:], in_=u_d[t * 128:(t + 1) * 128, :]), sy, 16)
            PE.wait(sy)
            for k in range(8):
                bank = pb[k % 2]
                PE.wait(ve, ve.count - 1)
                PE.op(lambda e, k=k, par=par, bank=bank: e.transpose(
                    bank[:, 0:128], usb[par][:, k * 128:(k + 1) * 128], I32[:]), pe)
                V.wait(pe)
                V.op(lambda e, k=k, t=t, bank=bank: e.tensor_copy(
                    out=xT[:, k * L + t * 128: k * L + (t + 1) * 128],
                    in_=bank[:, 0:128]), ve)
            pe_tile_end[t] = pe.count

        PE.wait(gq, scratch_done)
        GQ.wait(gq, scratch_done)

        nat_groups = [(ZC0 + 512 * i, 512) for i in range(4)] + \
                     [(XC0 + 512 * i, 512) for i in range(4)]
        tt_groups = [(BCC0, 128), (DLC0, 64)] + [(THC0 + 128 * j, 128) for j in range(8)]

        def proj_stage(c):
            par = c % 2
            xcols = slice(c * 128, (c + 1) * 128)
            for gi, (c0, w) in enumerate(nat_groups + tt_groups):
                is_nat = gi < len(nat_groups)
                bank = pb[gi % 2]
                buf = wgb[gi % 2]
                SY.wait(pe)
                SY.op(lambda e, c0=c0, w=w, buf=buf: e.dma_start(
                    out=buf[:, 0:8 * w].rearrange("p (k c) -> p k c", k=8),
                    in_=winTbf_d[:, c0:c0 + w].rearrange("(k p) c -> p k c", p=128)),
                    sy, 16)
                PE.wait(sy)
                PE.wait(ve)
                for k in range(8):
                    if is_nat:
                        PE.op(lambda e, k=k, w=w, buf=buf, bank=bank, xcols=xcols:
                              e.matmul(bank[:, 0:w],
                                       xT[:, k * L:(k + 1) * L][:, xcols],
                                       buf[:, k * w:(k + 1) * w],
                                       start=(k == 0), stop=(k == 7)),
                              pe if k == 7 else None)
                    else:
                        PE.op(lambda e, k=k, w=w, buf=buf, bank=bank, xcols=xcols:
                              e.matmul(bank[0:w, 0:128],
                                       buf[:, k * w:(k + 1) * w],
                                       xT[:, k * L:(k + 1) * L][:, xcols],
                                       start=(k == 0), stop=(k == 7)),
                              pe if k == 7 else None)
                V.wait(pe)
                if is_nat:
                    if c0 < XC0:
                        off = c0 - ZC0
                        V.op(lambda e, off=off, w=w, bank=bank, par=par: e.tensor_copy(
                            out=zsb[:, par * DI + off: par * DI + off + w],
                            in_=bank[:, 0:w]), ve)
                    else:
                        off = c0 - XC0
                        V.op(lambda e, off=off, w=w, bank=bank, par=par: e.tensor_copy(
                            out=Xsb[:, par * DI + off: par * DI + off + w],
                            in_=bank[:, 0:w]), ve)
                elif c0 == BCC0:
                    V.op(lambda e, bank=bank, par=par: e.tensor_copy(
                        out=bcT[:, par * 128:(par + 1) * 128], in_=bank[:, 0:128]), ve)
                elif c0 == DLC0:
                    V.op(lambda e, bank=bank, par=par: e.tensor_copy(
                        out=dtT[:, par * 128:(par + 1) * 128], in_=bank[0:32, 0:128]))
                    V.op(lambda e, bank=bank, par=par: e.tensor_copy(
                        out=lamT[:, par * 128:(par + 1) * 128], in_=bank[32:64, 0:128]),
                        ve)
                else:
                    j = (c0 - THC0) // 128
                    if c == 0:
                        ini = 0.0
                    else:
                        prev = (c - 1) % 2
                        ini = tcV[:, prev * 1024 + j * 128 + 127:
                                  prev * 1024 + j * 128 + 128]
                    V.op(lambda e, bank=bank, par=par, j=j, ini=ini:
                         e.tensor_tensor_scan(
                             out=tcV[:, par * 1024 + j * 128: par * 1024 + (j + 1) * 128],
                             data0=onec[:], data1=bank[:, 0:128], initial=ini,
                             op0=ALU.mult, op1=ALU.add), ve)
            S.wait(ve)
            S.op(lambda e, par=par: e.activation(
                out=s_t[:], in_=dtT[:, par * 128:(par + 1) * 128], func=AF.Exp,
                bias=dtb_s[:]))
            S.op(lambda e, par=par: e.activation(
                out=dtT[:, par * 128:(par + 1) * 128], in_=s_t[:], func=AF.Ln,
                bias=cns[0:32, 1:2]))
            S.op(lambda e, par=par: e.activation(
                out=lamT[:, par * 128:(par + 1) * 128],
                in_=lamT[:, par * 128:(par + 1) * 128], func=AF.Sigmoid), se)
            V.wait(se)
            V.op(lambda e, par=par: e.tensor_tensor(
                out=g_t[:, par * 128:(par + 1) * 128],
                in0=lamT[:, par * 128:(par + 1) * 128],
                in1=dtT[:, par * 128:(par + 1) * 128], op=ALU.mult))
            V.op(lambda e, par=par: e.tensor_tensor(
                out=q_t[:, par * 128:(par + 1) * 128],
                in0=dtT[:, par * 128:(par + 1) * 128],
                in1=g_t[:, par * 128:(par + 1) * 128], op=ALU.subtract), ve)

        def ssd_stage(c):
            par = c % 2
            npar = (c + 1) % 2
            S.wait(ve)
            S.op(lambda e, par=par: e.activation(
                out=sil, in_=zsb[:, par * DI:(par + 1) * DI], func=AF.Silu), se)
            V.op(lambda e, par=par: e.tensor_tensor(
                out=ghat[:, 0:127], in0=g_t[:, par * 128:par * 128 + 127],
                in1=q_t[:, par * 128 + 1:par * 128 + 128], op=ALU.add))
            qn = zcol[:] if c == NCH - 1 else q_t[:, npar * 128:npar * 128 + 1]
            V.op(lambda e, qn=qn, par=par: e.tensor_tensor(
                out=ghat[:, 127:128], in0=g_t[:, par * 128 + 127:par * 128 + 128],
                in1=qn, op=ALU.add))
            V.op(lambda e, par=par: e.tensor_scalar(
                out=s_t[:], in0=dtT[:, par * 128:(par + 1) * 128], scalar1=a_col[:],
                scalar2=None, op0=ALU.mult))
            V.op(lambda e: e.tensor_tensor_scan(
                out=S_T[:], data0=onec[0:32, :], data1=s_t[:], initial=0.0,
                op0=ALU.mult, op1=ALU.add), ve)
            S.wait(ve)
            S.op(lambda e: e.activation(out=lng[:], in_=ghat[:], func=AF.Ln))
            S.op(lambda e: e.activation(out=expd[:], in_=S_T[:], func=AF.Exp,
                                        bias=S_T[:, 127:128], scale=-1.0))
            S.op(lambda e: e.activation(out=dec_c[:], in_=S_T[:, 127:128],
                                        func=AF.Exp), se)
            V.wait(se)
            V.op(lambda e: e.tensor_tensor(out=pack[0:32, :], in0=lng[:], in1=S_T[:],
                                           op=ALU.subtract))
            V.op(lambda e, par=par: e.tensor_tensor(
                out=pack[32:64, :], in0=g_t[:, par * 128:(par + 1) * 128],
                in1=ghat[:], op=ALU.subtract))
            V.op(lambda e: e.tensor_copy(out=pack[64:96, :], in_=S_T[:]))
            V.op(lambda e: e.tensor_tensor(out=wT[:], in0=ghat[:], in1=expd[:],
                                           op=ALU.mult))
            V.op(lambda e: e.tensor_scalar(out=dec4i[:], in0=selj8s[:],
                                           scalar1=dec_c[:], scalar2=None,
                                           op0=ALU.mult), ve)
            PE.wait(ve)
            PE.op(lambda e: e.transpose(pb[2][:, 0:128], pack[:], I32[:]), pe)
            PE.op(lambda e: e.matmul(pb[3][:, 0:8], repl4s[:], dec4i[:],
                                     start=True, stop=True), pe)
            V.wait(pe)
            V.op(lambda e: e.tensor_copy(out=packN[:], in_=pb[2][:, 0:128]))
            V.op(lambda e: e.tensor_copy(out=dec4[:], in_=pb[3][:, 0:8]), ve)
            S.wait(ve)
            S.op(lambda e: e.activation(out=E_nat[:], in_=packN[:, 64:96],
                                        func=AF.Exp), se)

            tcs = tcV[:, par * 1024:(par + 1) * 1024]
            V.op(lambda e, tcs=tcs: e.tensor_scalar(
                out=atmp[:], in0=tcs, scalar1=1.0 / TWO_PI, scalar2=MAGIC,
                op0=ALU.mult, op1=ALU.add))
            V.op(lambda e: e.tensor_scalar(
                out=atmp[:], in0=atmp[:], scalar1=MAGIC, scalar2=None,
                op0=ALU.subtract))
            V.op(lambda e, tcs=tcs: e.scalar_tensor_tensor(
                out=rmod[:], in0=atmp[:], scalar=-TWO_PI, in1=tcs,
                op0=ALU.mult, op1=ALU.add))
            V.op(lambda e: e.tensor_scalar(
                out=rmod[:], in0=rmod[:], scalar1=-PIC, scalar2=PIC,
                op0=ALU.max, op1=ALU.min))
            V.op(lambda e: e.tensor_scalar(
                out=atmp[:], in0=rmod[:], scalar1=PI / 2, scalar2=None, op0=ALU.is_gt))
            V.op(lambda e: e.tensor_scalar(
                out=carg[:], in0=rmod[:], scalar1=PI / 2, scalar2=None, op0=ALU.add))
            V.op(lambda e: e.scalar_tensor_tensor(
                out=carg[:], in0=atmp[:], scalar=-TWO_PI, in1=carg[:],
                op0=ALU.mult, op1=ALU.add))
            V.op(lambda e: e.tensor_scalar(
                out=carg[:], in0=carg[:], scalar1=-PIC, scalar2=PIC,
                op0=ALU.max, op1=ALU.min), ve)
            S.wait(ve)
            S.op(lambda e: e.activation(out=rmod[:], in_=rmod[:], func=AF.Sin))
            S.op(lambda e: e.activation(out=carg[:], in_=carg[:], func=AF.Sin), se)
            V.wait(se)
            V.op(lambda e: e.tensor_copy(out=sinV[:], in_=rmod[:]))
            V.op(lambda e: e.tensor_copy(out=cosV[:], in_=carg[:]), ve)

            bct = bcT[:, par * 128:(par + 1) * 128]
            V.op(lambda e, bct=bct: e.tensor_tensor(out=sqT[:], in0=bct, in1=bct,
                                                    op=ALU.mult), ve)
            PE.wait(ve)
            PE.op(lambda e: e.matmul(pb[2][0:1, 0:128], onec[0:64, 0:1], sqT[0:64, :],
                                     start=True, stop=True), pe)
            PE.op(lambda e: e.matmul(pb[3][0:1, 0:128], onec[64:128, 0:1],
                                     sqT[64:128, :], start=True, stop=True,
                                     tile_position=(64, 0)), pe)
            S.wait(pe)
            S.op(lambda e: e.activation(out=invB[:], in_=pb[2][0:1, 0:128],
                                        func=AF.Sqrt, scale=1.0 / 64.0, bias=cns[0:1, 0:1]))
            S.op(lambda e: e.activation(out=invC[:], in_=pb[3][0:1, 0:128],
                                        func=AF.Sqrt, scale=1.0 / 64.0, bias=cns[0:1, 0:1]), se)
            V.wait(se)
            V.op(lambda e: e.reciprocal(out=invB[:], in_=invB[:]))
            V.op(lambda e: e.reciprocal(out=invC[:], in_=invC[:]), ve)
            PE.wait(ve)
            PE.op(lambda e: e.matmul(pb[2][0:64, 0:128], onec[0:1, 0:64], invB[:],
                                     start=True, stop=True), pe)
            PE.op(lambda e: e.matmul(pb[3][0:64, 0:128], onec[0:1, 0:64], invC[:],
                                     start=True, stop=True), pe)
            V.wait(pe)
            V.op(lambda e, bct=bct: e.tensor_tensor(out=brawn[:], in0=bct[0:64, :],
                                                    in1=pb[2][0:64, 0:128],
                                                    op=ALU.mult))
            V.op(lambda e, bct=bct: e.tensor_tensor(out=crawn[:], in0=bct[64:128, :],
                                                    in1=pb[3][0:64, 0:128],
                                                    op=ALU.mult))
            V.op(lambda e: e.tensor_copy(
                out=brawn8[:].rearrange("p (r c) -> p r c", r=8),
                in_=brawn[:].unsqueeze(1).broadcast_to([64, 8, 128])))
            V.op(lambda e: e.tensor_copy(
                out=crawn8[:].rearrange("p (r c) -> p r c", r=8),
                in_=crawn[:].unsqueeze(1).broadcast_to([64, 8, 128])), ve)
            PE.wait(ve)
            for half in range(2):
                rw = repl1w if half == 0 else repl2w
                rwc = repl1wc if half == 0 else repl2wc
                dst_b = vB1 if half == 0 else vB2
                dst_c = vC1 if half == 0 else vC2
                for piece in range(2):
                    PE.wait(ve)
                    PE.op(lambda e, rw=rw, piece=piece: e.matmul(
                        pb[2][:, 0:512], rw[:],
                        brawn8[:, piece * 512:(piece + 1) * 512],
                        start=True, stop=True), pe)
                    PE.op(lambda e, rwc=rwc, piece=piece: e.matmul(
                        pb[3][:, 0:512], rwc[:],
                        crawn8[:, piece * 512:(piece + 1) * 512],
                        start=True, stop=True), pe)
                    V.wait(pe)
                    V.op(lambda e, dst_b=dst_b, piece=piece, half=half: e.tensor_tensor(
                        out=dst_b[:, piece * 512:(piece + 1) * 512],
                        in0=bbv_s[:, half * 1024 + piece * 512:
                                  half * 1024 + (piece + 1) * 512],
                        in1=pb[2][:, 0:512], op=ALU.add))
                    V.op(lambda e, dst_c=dst_c, piece=piece, half=half: e.tensor_tensor(
                        out=dst_c[:, piece * 512:(piece + 1) * 512],
                        in0=cbv_s[:, half * 1024 + piece * 512:
                                  half * 1024 + (piece + 1) * 512],
                        in1=pb[3][:, 0:512], op=ALU.add), ve)
            V.wait(se)
            V.wait(pe)
            for (v1, v2) in ((vB1, vB2), (vC1, vC2)):
                V.op(lambda e, v1=v1: e.tensor_tensor(out=rta[:], in0=v1[:],
                                                      in1=cosV[:], op=ALU.mult))
                V.op(lambda e, v2=v2: e.tensor_tensor(out=rtb[:], in0=v2[:],
                                                      in1=sinV[:], op=ALU.mult))
                V.op(lambda e, v2=v2: e.tensor_tensor(out=v2[:], in0=v2[:],
                                                      in1=cosV[:], op=ALU.mult))
                V.op(lambda e, v1=v1: e.tensor_tensor(out=v1[:], in0=v1[:],
                                                      in1=sinV[:], op=ALU.mult))
                V.op(lambda e, v1=v1, v2=v2: e.tensor_tensor(out=v2[:], in0=v2[:],
                                                             in1=v1[:], op=ALU.add))
                V.op(lambda e, v1=v1: e.tensor_tensor(out=v1[:], in0=rta[:],
                                                      in1=rtb[:], op=ALU.subtract), ve)
            V.op(lambda e: e.tensor_tensor(
                out=wvsel[:].rearrange("p (r c) -> p r c", r=8),
                in0=wT[:].unsqueeze(1).broadcast_to([32, 8, 128]),
                in1=selj8s[:].unsqueeze(2).broadcast_to([32, 8, 128]),
                op=ALU.mult), ve)
            PE.wait(ve)
            for piece in range(2):
                PE.op(lambda e, piece=piece: e.matmul(
                    pb[2 + piece][:, 0:512], repl4bf[:],
                    wvsel[:, piece * 512:(piece + 1) * 512],
                    start=True, stop=True), pe)
            V.wait(pe)
            for piece in range(2):
                V.op(lambda e, piece=piece: e.tensor_tensor(
                    out=vBW1[:, piece * 512:(piece + 1) * 512],
                    in0=vB1[:, piece * 512:(piece + 1) * 512],
                    in1=pb[2 + piece][:, 0:512], op=ALU.mult))
                V.op(lambda e, piece=piece: e.tensor_tensor(
                    out=vBW2[:, piece * 512:(piece + 1) * 512],
                    in0=vB2[:, piece * 512:(piece + 1) * 512],
                    in1=pb[2 + piece][:, 0:512], op=ALU.mult), ve)
            for j in range(8):
                for half in range(2):
                    src = vBW1 if half == 0 else vBW2
                    bank = pb[2 + ((j * 2 + half) % 2)]
                    PE.wait(ve)
                    PE.op(lambda e, src=src, j=j, bank=bank: e.matmul(
                        bank[:, 0:128], src[:, j * 128:(j + 1) * 128], Ibf[:],
                        start=True, stop=True), pe)
                    V.wait(pe)
                    V.op(lambda e, j=j, half=half, bank=bank: e.tensor_copy(
                        out=bpp.rearrange("p (jj mm n) -> p jj mm n", jj=8, mm=4)
                            [:, j, :, half * 32:half * 32 + 32],
                        in_=bank[:, 0:128].rearrange("p (m k) -> p m k", m=4)), ve)

            for m in range(4):
                pbYx = pb[4 + (m % 2)]
                pbYs = pb[6 + (m % 2)]
                pm = slice(32 * m, 32 * m + 32)
                for jj in range(8):
                    h = 4 * jj + m
                    j = jj
                    dpar = jj % 2
                    cbank = pb[jj % 2]
                    V.wait(pe)
                    V.op(lambda e, h=h, dpar=dpar: e.tensor_tensor(
                        out=pre_t[dpar][:], in0=Mneg[:],
                        in1=packN[:, 64 + h:65 + h].broadcast_to([128, 128]),
                        op=ALU.add), ve)
                    PE.wait(ve)
                    PE.wait(se)
                    PE.op(lambda e, dpar=dpar: e.transpose(
                        pb[2 + dpar][:, 0:128], pre_t[dpar][:], I32[:]), pe)
                    PE.op(lambda e, pm=pm, j=j, cbank=cbank, m=m: e.matmul(
                        cbank[:, 0:128], vB1[pm, j * 128:(j + 1) * 128],
                        vC1[pm, j * 128:(j + 1) * 128], start=True, stop=False,
                        tile_position=(32 * m, 0)))
                    PE.op(lambda e, pm=pm, j=j, cbank=cbank, m=m: e.matmul(
                        cbank[:, 0:128], vB2[pm, j * 128:(j + 1) * 128],
                        vC2[pm, j * 128:(j + 1) * 128], start=False, stop=True,
                        tile_position=(32 * m, 0)), pe)
                    S.wait(ve)
                    S.wait(pe)
                    S.op(lambda e, dpar=dpar, h=h: e.activation(
                        out=D2[dpar][:], in_=pb[2 + dpar][:, 0:128], func=AF.Exp,
                        bias=packN[:, h:h + 1]), se)
                    V.wait(se)
                    V.op(lambda e, dpar=dpar, h=h: e.scalar_tensor_tensor(
                        out=D2[dpar][:], in0=I32[:], scalar=packN[:, 32 + h:33 + h],
                        in1=D2[dpar][:], op0=ALU.mult, op1=ALU.add))
                    V.wait(pe)
                    V.op(lambda e, dpar=dpar, cbank=cbank: e.tensor_tensor(
                        out=G_bf[dpar][:], in0=D2[dpar][:], in1=cbank[:, 0:128],
                        op=ALU.mult), ve)
                    PE.wait(ve)
                    PE.op(lambda e, dpar=dpar, h=h, jj=jj, pbYx=pbYx, par=par:
                          e.matmul(pbYx[:, jj * 64:(jj + 1) * 64], G_bf[dpar][:],
                                   Xsb[:, par * DI + h * 64: par * DI + (h + 1) * 64],
                                   start=True, stop=True, skip_group_check=True),
                          pe if jj == 7 else None)
                    PE.op(lambda e, pm=pm, j=j, jj=jj, pbYs=pbYs, m=m: e.matmul(
                        pbYs[:, jj * 64:(jj + 1) * 64],
                        vC1[pm, j * 128:(j + 1) * 128],
                        state_bf[pm, j * 128:j * 128 + 64], start=True,
                        stop=False, tile_position=(32 * m, 0),
                        skip_group_check=True))
                    PE.op(lambda e, pm=pm, j=j, jj=jj, pbYs=pbYs, m=m: e.matmul(
                        pbYs[:, jj * 64:(jj + 1) * 64],
                        vC2[pm, j * 128:(j + 1) * 128],
                        state_bf[pm, j * 128 + 64:(j + 1) * 128], start=False,
                        stop=True, tile_position=(32 * m, 0),
                        skip_group_check=True), pe)
                    S.wait(pe)
                    if jj == 0:
                        S.wait(ve)
                    S.op(lambda e, jj=jj, h=h, pbYs=pbYs: e.activation(
                        out=ysc[:, jj * 64:(jj + 1) * 64],
                        in_=pbYs[:, jj * 64:(jj + 1) * 64],
                        func=AF.Identity, scale=E_nat[:, h:h + 1]), se)
                V.wait(se)
                V.op(lambda e, pbYx=pbYx: e.tensor_tensor(
                    out=ysc[:], in0=ysc[:], in1=pbYx[:, 0:512], op=ALU.add))
                V.op(lambda e, m=m, par=par: e.tensor_tensor(
                    out=rmod[:, 0:512].rearrange("p (j q) -> p j q", j=8),
                    in0=Xsb[:, par * DI:(par + 1) * DI].rearrange(
                        "p (j hm q) -> p j hm q", j=8, hm=4)[:, :, m, :],
                    in1=dsk_s[:].rearrange("p (a b) -> p a b", a=4)[:, m, :]
                        .unsqueeze(2).broadcast_to([128, 8, 64]), op=ALU.mult))
                V.op(lambda e: e.tensor_tensor(
                    out=ysc[:], in0=ysc[:], in1=rmod[:, 0:512], op=ALU.add))
                V.op(lambda e, m=m: e.tensor_copy(
                    out=yg[:].rearrange("p (j hm q) -> p j hm q", j=8, hm=4)[:, :, m, :],
                    in_=ysc[:].rearrange("p (j c) -> p j c", j=8)), ve)

            V.op(lambda e: e.tensor_tensor(out=yg[:], in0=yg[:], in1=sil,
                                           op=ALU.mult), ve)
            PE.wait(ve)
            for m in range(4):
                pm = slice(32 * m, 32 * m + 32)
                for j in range(8):
                    h = 4 * j + m
                    bank = pb[6] if j < 4 else pb[7]
                    jo = (j % 4) * 128
                    PE.op(lambda e, h=h, pm=pm, bank=bank, jo=jo, m=m, par=par:
                          e.matmul(bank[pm, jo:jo + 64],
                                   bpp[:, h * 64:h * 64 + 32],
                                   Xsb[:, par * DI + h * 64: par * DI + (h + 1) * 64],
                                   start=True, stop=True,
                                   tile_position=(0, 32 * m),
                                   skip_group_check=True))
                    PE.op(lambda e, h=h, pm=pm, bank=bank, jo=jo, m=m, par=par:
                          e.matmul(bank[pm, jo + 64:jo + 128],
                                   bpp[:, h * 64 + 32:(h + 1) * 64],
                                   Xsb[:, par * DI + h * 64: par * DI + (h + 1) * 64],
                                   start=True, stop=True,
                                   tile_position=(0, 32 * m),
                                   skip_group_check=True),
                          pe if (j in (3, 7)) else None)
            V.wait(pe)
            V.op(lambda e: e.tensor_tensor(
                out=state[:].rearrange("p (j c) -> p j c", j=8),
                in0=state[:].rearrange("p (j c) -> p j c", j=8),
                in1=dec4[:].unsqueeze(2).broadcast_to([128, 8, 128]), op=ALU.mult))
            V.op(lambda e: e.tensor_tensor(out=state[:, 0:512], in0=state[:, 0:512],
                                           in1=pb[6][:, 0:512], op=ALU.add))
            V.op(lambda e: e.tensor_tensor(out=state[:, 512:1024],
                                           in0=state[:, 512:1024],
                                           in1=pb[7][:, 0:512], op=ALU.add))
            V.op(lambda e: e.tensor_copy(out=state_bf[:], in_=state[:]), ve)

            V.op(lambda e: e.tensor_tensor(out=rmod[:], in0=yg[:, 0:1024],
                                           in1=yg[:, 0:1024], op=ALU.mult))
            V.op(lambda e: e.tensor_tensor(out=rln[:], in0=yg[:, 1024:2048],
                                           in1=yg[:, 1024:2048], op=ALU.mult))
            V.op(lambda e: e.tensor_reduce(out=cols[:, 0:1], in_=rmod[:], axis=AX.X,
                                           op=ALU.add))
            V.op(lambda e: e.tensor_reduce(out=cols[:, 1:2], in_=rln[:], axis=AX.X,
                                           op=ALU.add), ve)
            S.wait(ve)
            S.op(lambda e: e.activation(out=cols[:, 12:13], in_=cols[:, 0:1],
                                        func=AF.Identity, bias=cols[:, 1:2]))
            S.op(lambda e: e.activation(out=cols[:, 1:2], in_=cols[:, 12:13],
                                        func=AF.Sqrt, scale=1.0 / DI,
                                        bias=cns[:, 0:1]), se)
            V.wait(se)
            V.op(lambda e: e.reciprocal(out=cols[:, 13:14], in_=cols[:, 1:2]), ve)
            S.wait(ve)
            S.op(lambda e: e.activation(out=cols[:, 2:3], in_=cols[:, 13:14],
                                        func=AF.Identity), se)
            V.wait(se)
            V.op(lambda e: e.tensor_tensor(
                out=yg[:], in0=yg[:],
                in1=cols[:, 2:3].broadcast_to([128, DI]), op=ALU.mult))
            V.op(lambda e: e.tensor_tensor(out=yg[:], in0=yg[:], in1=nwrep_s[:],
                                           op=ALU.mult), ve)
            for k in range(16):
                bank = pb[2 + (k % 2)]
                ytd = vB1 if k < 8 else vB2
                ko = k % 8
                PE.wait(ve)
                PE.wait(se)
                PE.op(lambda e, k=k, bank=bank: e.matmul(
                    bank[:, 0:128], yg[:, k * 128:(k + 1) * 128], Ibf[:],
                    start=True, stop=True), pe)
                V.wait(pe)
                V.op(lambda e, ko=ko, bank=bank, ytd=ytd: e.tensor_copy(
                    out=ytd[:, ko * 128:(ko + 1) * 128],
                    in_=bank[:, 0:128]), ve)
            for k in range(16):
                ytd = vB1 if k < 8 else vB2
                ko = k % 8
                GQ.wait(pe)
                GQ.op(lambda e, k=k: e.dma_start(
                    out=wob[k % 2][:], in_=woutTbf_d[k * 128:(k + 1) * 128, :]),
                    gq, 16)
                PE.wait(gq)
                PE.wait(ve)
                for nh in range(2):
                    PE.op(lambda e, ko=ko, nh=nh, k=k, ytd=ytd: e.matmul(
                        pb[nh][:, 0:512], ytd[:, ko * 128:(ko + 1) * 128],
                        wob[k % 2][:, nh * 512:(nh + 1) * 512],
                        start=(k == 0), stop=(k == 15)), pe)
            SY.wait(ve)
            SY.op(lambda e, c=c, par=par: e.dma_start(
                out=usb[par][:], in_=u_d[c * 128:(c + 1) * 128, :]), sy, 16)
            V.wait(sy)
            V.wait(pe)
            V.op(lambda e, par=par: e.tensor_tensor(
                out=rln[:, 0:512], in0=usb[par][:, 0:512], in1=pb[0][:, 0:512],
                op=ALU.add))
            V.op(lambda e, par=par: e.tensor_tensor(
                out=rln[:, 512:1024], in0=usb[par][:, 512:1024], in1=pb[1][:, 0:512],
                op=ALU.add))
            V.op(lambda e: e.tensor_reduce(out=cols[:, 2:3], in_=rln[:], axis=AX.X,
                                           op=ALU.add), ve)
            S.wait(ve)
            S.op(lambda e: e.activation(out=cols[:, 4:5], in_=cols[:, 2:3],
                                        func=AF.Identity, scale=1.0 / DM), se)
            V.wait(se)
            V.op(lambda e: e.tensor_tensor(
                out=xsq[:], in0=rln[:],
                in1=cols[:, 4:5].broadcast_to([128, DM]), op=ALU.subtract))
            V.wait(sr)
            V.op(lambda e: e.tensor_tensor(out=xln[:], in0=xsq[:], in1=xsq[:],
                                           op=ALU.mult))
            V.op(lambda e: e.tensor_reduce(out=cols[:, 5:6], in_=xln[:], axis=AX.X,
                                           op=ALU.add), ve)
            S.wait(ve)
            S.op(lambda e: e.activation(out=cols[:, 8:9], in_=cols[:, 5:6],
                                        func=AF.Sqrt, scale=1.0 / DM,
                                        bias=cns[:, 0:1]), se)
            V.wait(se)
            V.op(lambda e: e.reciprocal(out=cols[:, 9:10], in_=cols[:, 8:9]), ve)
            S.wait(ve)
            S.op(lambda e: e.activation(out=cols[:, 10:11], in_=cols[:, 9:10],
                                        func=AF.Identity), se)
            V.wait(se)
            V.op(lambda e: e.tensor_tensor(
                out=xln[:], in0=xsq[:],
                in1=cols[:, 10:11].broadcast_to([128, DM]), op=ALU.mult), ve)
            SY.wait(ve)
            SY.op(lambda e, c=c: e.dma_start(
                out=r1scr_d[c * 128:(c + 1) * 128, :], in_=xln[:]), sr, 16)
            for k in range(8):
                bank = pb[2 + (k % 2)]
                PE.wait(ve)
                PE.op(lambda e, k=k, bank=bank: e.transpose(
                    bank[:, 0:128], xln[:, k * 128:(k + 1) * 128], I32[:]), pe)
                V.wait(pe)
                V.op(lambda e, bank=bank: e.tensor_copy(out=xt32[:],
                                                        in_=bank[:, 0:128]))
                V.op(lambda e, k=k, c=c, bank=bank: e.tensor_copy(
                    out=xT[:, k * L + c * 128:k * L + (c + 1) * 128],
                    in_=bank[:, 0:128]), ve)
                PE.wait(ve)
                PE.op(lambda e, k=k: e.matmul(
                    pb[4][0:8, 0:128], wr_s[:, k * 8:(k + 1) * 8], xt32[:],
                    start=(k == 0), stop=(k == 7)), pe if k == 7 else None)
            S.wait(pe)
            S.op(lambda e: e.activation(out=logT[:], in_=pb[4][0:8, 0:128],
                                        func=AF.Identity, bias=br_s[:]), se)
            PE.wait(se)
            PE.wait(ve)
            PE.op(lambda e: e.transpose(pb[5][:, 0:8], logT[:], I32[0:8, 0:8]), pe)
            V.wait(pe)
            V.op(lambda e: e.tensor_reduce(out=cols[:, 14:15], in_=pb[5][:, 0:8],
                                           axis=AX.X, op=ALU.max), ve)
            S.wait(ve)
            S.op(lambda e: e.activation(out=cols[:, 13:14], in_=cols[:, 14:15],
                                        func=AF.Identity), se)
            V.wait(se)
            V.op(lambda e: e.tensor_tensor(
                out=xt32[:, 8:16], in0=pb[5][:, 0:8],
                in1=cols[:, 13:14].broadcast_to([128, 8]), op=ALU.is_ge))
            V.op(lambda e: e.tensor_tensor(out=cols[:, 15:16], in0=cols[:, 13:14],
                                           in1=cns[:, 4:5], op=ALU.mult), ve)
            S.wait(ve)
            S.op(lambda e: e.activation(out=xt32[:, 0:8], in_=pb[5][:, 0:8],
                                        func=AF.Exp, bias=cols[:, 15:16],
                                        accum_out=cols[:, 11:12]), se)
            V.wait(se)
            V.op(lambda e: e.reciprocal(out=cols[:, 12:13], in_=cols[:, 11:12]), ve)
            S.wait(ve)
            S.op(lambda e: e.activation(out=cols[:, 10:11], in_=cols[:, 12:13],
                                        func=AF.Identity), se)
            V.wait(se)
            V.op(lambda e, c=c: e.tensor_tensor(
                out=gsel[:, c * 8:(c + 1) * 8], in0=xt32[:, 8:16],
                in1=cols[:, 10:11].broadcast_to([128, 8]), op=ALU.mult), ve)

        proj_stage(0)
        nssd = NCH if upto is None else min(NCH, upto)
        for c in range(nssd):
            if c + 1 < NCH:
                proj_stage(c + 1)
            ssd_stage(c)

        if upto is None:
            for tg in range(4):
                V.op(lambda e: e.memset(acc[:], 0.0), ve)
                for e_i in range(E):
                    par_e = e_i % 2
                    SY.wait(se)
                    SY.op(lambda e, e_i=e_i, par_e=par_e: e.dma_start(
                        out=b1c_s[par_e][:], in_=b1_d[e_i]), sy, 16)
                    for ft in range(32):
                        par = ft % 2
                        for pc in range(2):
                            GQ.wait(ve)
                            GQ.op(lambda e, e_i=e_i, ft=ft, par=par, pc=pc:
                                  e.dma_start(
                                      out=w1st[par * 2 + pc][:].rearrange(
                                          "p (k c) -> p k c", k=4),
                                      in_=w1_d[e_i, ft, pc * 4:(pc + 1) * 4]
                                      .transpose([1, 0, 2])),
                                  gq, 16)
                            V.wait(gq)
                            V.wait(pe)
                            V.op(lambda e, par=par, pc=pc: e.tensor_copy(
                                out=w1bf[par][:, pc * 512:(pc + 1) * 512],
                                in_=w1st[par * 2 + pc][:]), ve)
                        bank = pb[ft % 4]
                        PE.wait(ve)
                        PE.wait(se)
                        for k in range(8):
                            PE.op(lambda e, k=k, par=par, bank=bank, tg=tg: e.matmul(
                                bank[:, 0:512], w1bf[par][:, k * 128:(k + 1) * 128],
                                xT[:, k * L + tg * 512:k * L + (tg + 1) * 512],
                                start=(k == 0), stop=(k == 7)),
                                pe if k == 7 else None)
                        S.wait(pe)
                        S.wait(sy)
                        S.op(lambda e, ft=ft, bank=bank, par_e=par_e: e.activation(
                            out=h1T[:, ft * 512:(ft + 1) * 512], in_=bank[:, 0:512],
                            func=AF.Relu, bias=b1c_s[par_e][:, ft:ft + 1]), se)
                    for fs in range(33):
                        par = fs % 2
                        for pc in range(2):
                            GQ.wait(ve)
                            GQ.op(lambda e, e_i=e_i, fs=fs, par=par, pc=pc:
                                  e.dma_start(
                                      out=w2st[par * 2 + pc][:],
                                      in_=w2_d[e_i, fs, :, pc * 512:(pc + 1) * 512]),
                                  gq, 16)
                            V.wait(gq)
                            V.wait(pe)
                            V.op(lambda e, par=par, pc=pc: e.tensor_copy(
                                out=w2bf[par][:, pc * 512:(pc + 1) * 512],
                                in_=w2st[par * 2 + pc][:]), ve)
                        PE.wait(ve)
                        if fs == 0:
                            PE.wait(se)
                        for tt in range(4):
                            for nh in range(2):
                                PE.op(lambda e, fs=fs, tt=tt, nh=nh, par=par: e.matmul(
                                    pb[tt * 2 + nh][:, 0:512],
                                    h1T[:, fs * 512 + tt * 128:
                                        fs * 512 + (tt + 1) * 128],
                                    w2bf[par][:, nh * 512:(nh + 1) * 512],
                                    start=(fs == 0), stop=(fs == 32)),
                                    pe if (fs == 32 and tt == 3 and nh == 1) else None)
                    V.wait(pe)
                    for tt in range(4):
                        ci = tg * 4 + tt
                        for nh in range(2):
                            V.op(lambda e, tt=tt, nh=nh, ci=ci, e_i=e_i:
                                 e.scalar_tensor_tensor(
                                     out=acc[:, tt * DM + nh * 512:
                                            tt * DM + (nh + 1) * 512],
                                     in0=pb[tt * 2 + nh][:, 0:512],
                                     scalar=gsel[:, ci * 8 + e_i:ci * 8 + e_i + 1],
                                     in1=acc[:, tt * DM + nh * 512:
                                            tt * DM + (nh + 1) * 512],
                                     op0=ALU.mult, op1=ALU.add), ve)
                if e_i == E - 1:
                    for tt in range(4):
                        ci = tg * 4 + tt
                        par = tt % 2
                        SY.wait(ve)
                        SY.op(lambda e, ci=ci, par=par: e.dma_start(
                            out=usb[par][:], in_=r1scr_d[ci * 128:(ci + 1) * 128, :]),
                            sy, 16)
                        V.wait(sy)
                        V.op(lambda e, tt=tt, par=par: e.tensor_tensor(
                            out=rln[:], in0=usb[par][:],
                            in1=acc[:, tt * DM:(tt + 1) * DM], op=ALU.add))
                        V.op(lambda e: e.tensor_reduce(
                            out=cols[:, 2:3], in_=rln[:], axis=AX.X, op=ALU.add), ve)
                        S.wait(ve)
                        S.op(lambda e: e.activation(
                            out=cols[:, 4:5], in_=cols[:, 2:3], func=AF.Identity,
                            scale=1.0 / DM), se)
                        V.wait(se)
                        V.op(lambda e: e.tensor_tensor(
                            out=xsq[:], in0=rln[:],
                            in1=cols[:, 4:5].broadcast_to([128, DM]),
                            op=ALU.subtract))
                        V.op(lambda e: e.tensor_tensor(
                            out=w1st[0][:], in0=xsq[:, 0:512], in1=xsq[:, 0:512],
                            op=ALU.mult))
                        V.op(lambda e: e.tensor_tensor(
                            out=w1st[1][:], in0=xsq[:, 512:1024],
                            in1=xsq[:, 512:1024], op=ALU.mult))
                        V.op(lambda e: e.tensor_reduce(
                            out=cols[:, 5:6], in_=w1st[0][:], axis=AX.X, op=ALU.add))
                        V.op(lambda e: e.tensor_reduce(
                            out=cols[:, 6:7], in_=w1st[1][:], axis=AX.X, op=ALU.add),
                            ve)
                        S.wait(ve)
                        S.op(lambda e: e.activation(
                            out=cols[:, 7:8], in_=cols[:, 5:6], func=AF.Identity,
                            bias=cols[:, 6:7]))
                        S.op(lambda e: e.activation(
                            out=cols[:, 10:11], in_=cols[:, 7:8], func=AF.Sqrt,
                            scale=1.0 / DM, bias=cns[:, 0:1]), se)
                        V.wait(se)
                        V.op(lambda e: e.reciprocal(out=cols[:, 11:12],
                                                    in_=cols[:, 10:11]), ve)
                        S.wait(ve)
                        S.op(lambda e: e.activation(
                            out=cols[:, 12:13], in_=cols[:, 11:12],
                            func=AF.Identity), se)
                        V.wait(se)
                        V.wait(so)
                        V.op(lambda e, par=par: e.tensor_tensor(
                            out=xo32[par][:], in0=xsq[:],
                            in1=cols[:, 12:13].broadcast_to([128, DM]),
                            op=ALU.mult), ve)
                        SY.wait(ve)
                        SY.op(lambda e, ci=ci, par=par: e.dma_start(
                            out=out_d[ci * 128:(ci + 1) * 128, :], in_=xo32[par][:]),
                            so, 16)
            SY.wait(so)

        if taps:
            SY.wait(ve); SY.wait(se); SY.wait(pe); SY.wait(gq, gq.count)
            for tname, sname, shape in taps:
                t = SBUF[sname]
                td = dt("tap_" + tname, list(shape), t.dtype,
                        kind="ExternalOutput").ap()
                SY.op(lambda e, t=t, td=td, shape=shape: e.dma_start(
                    out=td, in_=t[0:shape[0], 0:shape[1]]), so, 16)
            SY.wait(so)

        with nc.Block() as block:
            @block.sync
            def _(eng):
                SY.replay(eng)

            @block.gpsimd
            def _(eng):
                GQ.replay(eng)

            @block.tensor
            def _(eng):
                PE.replay(eng)

            @block.vector
            def _(eng):
                V.replay(eng)

            @block.scalar
            def _(eng):
                S.replay(eng)

    return nc


def _host_inputs(inputs):
    u = np.asarray(inputs["L0_attn_u"], np.float32)
    ipw = np.asarray(inputs["L0_attn_in_proj_w"], np.float32)
    perm = np.concatenate([
        np.arange(0, 2048), np.arange(2048, 4096), np.arange(4096, 4224),
        np.arange(4224, 4256), np.arange(5280, 5312), np.arange(4256, 5280)])
    winT = np.ascontiguousarray(ipw[perm].T)
    woutT = np.ascontiguousarray(
        np.asarray(inputs["L0_attn_out_proj_w"], np.float32).T)
    W1 = np.asarray(inputs["L0_ffn_W1"], np.float32)
    w1r = np.ascontiguousarray(
        W1.reshape(E, 8, 128, 32, 128).transpose(0, 3, 1, 2, 4))
    W2 = np.asarray(inputs["L0_ffn_W2"], np.float32)
    b2 = np.asarray(inputs["L0_ffn_b2"], np.float32)
    w2p = np.zeros((E, 33, 128, DM), np.float32)
    w2p[:, :32] = W2.reshape(E, 32, 128, DM)
    w2p[:, 32, 0, :] = b2
    b1 = np.asarray(inputs["L0_ffn_b1"], np.float32)
    b1c = np.ascontiguousarray(b1.reshape(E, 32, 128).transpose(0, 2, 1))
    Bb = np.asarray(inputs["L0_attn_B_bias"], np.float32)
    Cb = np.asarray(inputs["L0_attn_C_bias"], np.float32)

    def vint(bias):
        outs = np.zeros((2, 128, 1024), np.float32)
        m2g, kg = np.divmod(np.arange(128), 32)
        for half in range(2):
            for j in range(8):
                outs[half, :, 128 * j:128 * (j + 1)] = \
                    bias[4 * j + m2g, half * 32 + kg][:, None]
        return outs

    nwrep = np.ascontiguousarray(np.broadcast_to(
        np.asarray(inputs["L0_attn_norm_w"], np.float32)[None, :], (128, DI)))
    ii, jjx = np.meshgrid(np.arange(128), np.arange(128), indexing="ij")
    mneg = np.where(ii < jjx, -1e30, 0.0).astype(np.float32)
    kk = np.arange(128) % 32
    shared = dict(
        winT=winT, woutT=woutT,
        wr=np.ascontiguousarray(np.asarray(inputs["L0_ffn_Wr"], np.float32)),
        br=np.asarray(inputs["L0_ffn_br"], np.float32).reshape(E, 1),
        w1=w1r, b1c=b1c, w2=w2p,
        dtb=np.asarray(inputs["L0_attn_dt_bias"], np.float32).reshape(H, 1),
        alog=np.asarray(inputs["L0_attn_A_log"], np.float32).reshape(H, 1),
        dsk4=np.ascontiguousarray(np.broadcast_to(
            np.asarray(inputs["L0_attn_D"], np.float32).reshape(-1)[
                (4 * np.arange(8)[None, :] + np.arange(4)[:, None])][None, :, :],
            (128, 4, 8))),
        bnw=np.asarray(inputs["L0_attn_B_norm_w"], np.float32).reshape(N, 1),
        cnw=np.asarray(inputs["L0_attn_C_norm_w"], np.float32).reshape(N, 1),
        bbv=vint(Bb), cbv=vint(Cb), nwrep=nwrep,
        ident=np.eye(128, dtype=np.float32), mneg=mneg,
        repl1=(np.arange(N)[:, None] == kk[None, :]).astype(np.float32),
        repl2=(np.arange(N)[:, None] == (32 + kk)[None, :]).astype(np.float32),
        repl4=((np.arange(H)[:, None] % 4) ==
               (np.arange(128)[None, :] // 32)).astype(np.float32),
        selj8=((np.arange(H)[:, None] // 4) ==
               np.arange(8)[None, :]).astype(np.float32),
    )
    in_maps = []
    for core in range(8):
        d = dict(shared)
        d["u"] = np.ascontiguousarray(u[core])
        in_maps.append(d)
    return in_maps


def kernel(**inputs):
    in_maps = _host_inputs(inputs)
    nc = build_program()
    res = run_bass_kernel_spmd(nc, in_maps, core_ids=list(range(8)))
    out = np.stack([np.asarray(r["out"], np.float32) for r in res.results], axis=0)
    return out


# version 02
# speedup vs baseline: 1.6156x; 1.6156x over previous
import math
import numpy as np
from contextlib import ExitStack

import concourse.bass as bass
import concourse.mybir as mybir
from concourse.bass_utils import run_bass_kernel_spmd

F32 = mybir.dt.float32
BF16 = mybir.dt.bfloat16
AF = mybir.ActivationFunctionType
ALU = mybir.AluOpType
AX = mybir.AxisListType

L = 2048
DM = 1024
H = 32
P = 64
N = 64
DI = H * P
NCH = 16
Q = 128
DPROJ = 5312
E = 8
DFF = 4096
TWO_PI = 2.0 * math.pi
MAGIC = 1.5 * (2.0 ** 23)
PI = math.pi
PIC = math.pi * (1.0 - 1e-6)

ZC0, XC0, BCC0, DLC0, THC0 = 0, 2048, 4096, 4224, 4288


class Sem:
    def __init__(self, handle):
        self.h = handle
        self.count = 0


class Emitter:
    def __init__(self, name):
        self.name = name
        self.items = []
        self._waited = {}

    def wait(self, sem, val=None):
        v = sem.count if val is None else val
        if self._waited.get(id(sem), -1) >= v or v <= 0:
            return
        self._waited[id(sem)] = v
        self.items.append(lambda e, s=sem, v=v: e.wait_ge(s.h, v))

    def op(self, fn, inc=None, amt=1):
        if inc is not None:
            inc.count += amt
            def run(e, fn=fn, inc=inc, amt=amt):
                fn(e).then_inc(inc.h, amt)
        else:
            def run(e, fn=fn):
                fn(e)
        self.items.append(run)

    def replay(self, eng):
        for it in self.items:
            it(eng)


def build_program(upto=None, taps=()):
    nc = bass.Bass("TRN2", target_bir_lowering=False)
    dt = nc.dram_tensor

    u_d = dt("u", [L, DM], F32, kind="ExternalInput").ap()
    winT_d = dt("winT", [DM, DPROJ], F32, kind="ExternalInput").ap()
    woutT_d = dt("woutT", [DI, DM], F32, kind="ExternalInput").ap()
    wr_d = dt("wr", [DM, E], F32, kind="ExternalInput").ap()
    br_d = dt("br", [E, 1], F32, kind="ExternalInput").ap()
    w1_d = dt("w1", [E, 32, 8, 128, 128], F32, kind="ExternalInput").ap()
    b1_d = dt("b1c", [E, 128, 32], F32, kind="ExternalInput").ap()
    w2_d = dt("w2", [E, 33, 128, DM], F32, kind="ExternalInput").ap()
    dtb_d = dt("dtb", [H, 1], F32, kind="ExternalInput").ap()
    alog_d = dt("alog", [H, 1], F32, kind="ExternalInput").ap()
    dsk_d = dt("dsk4", [128, 4, 8], F32, kind="ExternalInput").ap()
    bnw_d = dt("bnw", [N, 1], F32, kind="ExternalInput").ap()
    cnw_d = dt("cnw", [N, 1], F32, kind="ExternalInput").ap()
    bbv_d = dt("bbv", [2, 128, 1024], F32, kind="ExternalInput").ap()
    cbv_d = dt("cbv", [2, 128, 1024], F32, kind="ExternalInput").ap()
    nwrep_d = dt("nwrep", [128, DI], F32, kind="ExternalInput").ap()
    ident_d = dt("ident", [128, 128], F32, kind="ExternalInput").ap()
    mneg_d = dt("mneg", [128, 128], F32, kind="ExternalInput").ap()
    repl1_d = dt("repl1", [N, 128], F32, kind="ExternalInput").ap()
    repl2_d = dt("repl2", [N, 128], F32, kind="ExternalInput").ap()
    repl4_d = dt("repl4", [H, 128], F32, kind="ExternalInput").ap()
    selj8_d = dt("selj8", [H, 8], F32, kind="ExternalInput").ap()
    out_d = dt("out", [L, DM], F32, kind="ExternalOutput").ap()

    winTbf_d = dt("winTbf", [DM, DPROJ], BF16, kind="Internal").ap()
    woutTbf_d = dt("woutTbf", [DI, DM], BF16, kind="Internal").ap()
    r1scr_d = dt("r1scr", [L, DM], F32, kind="Internal").ap()
    w1bf_d = dt("w1bfd", [E, 32, 128, 1024], BF16, kind="Internal").ap()
    w2bf_d = dt("w2bfd", [E, 33, 128, DM], BF16, kind="Internal").ap()


    st = ExitStack()
    SBUF = {}

    def sb(n, sh, d=F32):
        t = st.enter_context(nc.sbuf_tensor(n, sh, d))
        SBUF[n] = t
        return t

    ps = lambda n, sh: st.enter_context(nc.psum_tensor(n, sh, F32))
    mksem = lambda n: Sem(st.enter_context(nc.semaphore(n)))

    with st:
        xT = sb("xT", [128, 8 * L], BF16)
        acc = sb("acc", [128, 4 * DM], BF16)
        h1T = sb("h1T", [128, 33 * 512], BF16)
        zsb = h1T[:, 0:4096]
        Xsb = h1T[:, 4096:8192]
        tcV = h1T[:, 8192:12288].bitcast(F32)
        bpp = h1T[:, 12288:14336]
        sil = h1T[:, 14336:16384]
        bcT = sb("bcT", [128, 2 * 128], F32)
        dtT = sb("dtT", [32, 2 * 128], F32)
        lamT = sb("lamT", [32, 2 * 128], F32)
        g_t = sb("g_t", [32, 2 * 128], F32)
        q_t = sb("q_t", [32, 2 * 128], F32)
        wgb = [sb(f"wgb{i}", [128, 8 * 512], BF16) for i in range(2)]
        wob = [sb(f"wob{i}", [128, DM], BF16) for i in range(2)]
        usb = [sb(f"usb{i}", [128, DM], F32) for i in range(2)]

        ghat = sb("ghat", [32, 128], F32)
        s_t = sb("s_t", [32, 128], F32)
        S_T = sb("S_T", [32, 128], F32)
        lng = sb("lng", [32, 128], F32)
        wT = sb("wT", [32, 128], F32)
        expd = sb("expd", [32, 128], F32)
        pack = sb("pack", [128, 128], F32)
        packN = sb("packN", [128, 128], F32)
        E_nat = sb("E_nat", [128, 32], F32)
        dec_c = sb("dec_c", [32, 1], F32)
        dec4i = sb("dec4i", [32, 8], F32)
        dec4 = sb("dec4", [128, 8], F32)
        wvsel = sb("wvsel", [32, 8 * 128], BF16)
        state = sb("state", [128, 1024], F32)
        state_bf = sb("state_bf", [128, 1024], BF16)
        sqT = sb("sqT", [128, 128], F32)
        invB = sb("invB", [1, 128], F32)
        invC = sb("invC", [1, 128], F32)
        brawn = sb("brawn", [64, 128], BF16)
        brawn8 = sb("brawn8", [64, 8 * 128], BF16)
        crawn = sb("crawn", [64, 128], BF16)
        crawn8 = sb("crawn8", [64, 8 * 128], BF16)
        vB1 = sb("vB1", [128, 1024], BF16)
        vB2 = sb("vB2", [128, 1024], BF16)
        vC1 = sb("vC1", [128, 1024], BF16)
        vC2 = sb("vC2", [128, 1024], BF16)
        rta = sb("rta", [128, 1024], BF16)
        rtb = sb("rtb", [128, 1024], BF16)
        vBW1 = rta
        vBW2 = rtb
        rmod = sb("rmod", [128, 1024], F32)
        cosV = sb("cosV", [128, 1024], BF16)
        sinV = sb("sinV", [128, 1024], BF16)
        pre_t = [sb(f"pre{i}", [128, 128], F32) for i in range(2)]
        D2 = [sb(f"D2_{i}", [128, 128], F32) for i in range(2)]
        G_bf = [sb(f"G_{i}", [128, 128], BF16) for i in range(2)]
        ysc = sb("ysc", [128, 512], F32)
        yg = sb("yg", [128, DI], BF16)
        I32 = sb("I32", [128, 128], F32)
        Ibf = sb("Ibf", [128, 128], BF16)
        Mneg = sb("Mneg", [128, 128], F32)
        rp1 = sb("rp1", [64, 128], F32)
        rp2 = sb("rp2", [64, 128], F32)
        repl1w = sb("repl1w", [64, 128], BF16)
        repl2w = sb("repl2w", [64, 128], BF16)
        repl1wc = sb("repl1wc", [64, 128], BF16)
        repl2wc = sb("repl2wc", [64, 128], BF16)
        repl4s = sb("repl4s", [32, 128], F32)
        repl4bf = sb("repl4bf", [32, 128], BF16)
        selj8s = sb("selj8s", [32, 8], F32)
        bbv_s = sb("bbv_s", [128, 2 * 1024], BF16)
        cbv_s = sb("cbv_s", [128, 2 * 1024], BF16)
        nwrep_s = sb("nwrep_s", [128, DI], BF16)
        onec = sb("onec", [128, 128], F32)
        a_col = sb("a_col", [32, 1], F32)
        acol_t = sb("acol_t", [32, 1], F32)
        dtb_s = sb("dtb_s", [32, 1], F32)
        dsk_s = sb("dsk_s", [128, 32], F32)
        bnw_s = sb("bnw_s", [64, 1], F32)
        cnw_s = sb("cnw_s", [64, 1], F32)
        wr_s = sb("wr_s", [128, 8 * 8], F32)
        br_s = sb("br_s", [8, 1], F32)
        zcol = sb("zcol", [32, 1], F32)
        rln = sb("rln", [128, DM], F32)
        xln = sb("xln", [128, DM], F32)
        carg = rln
        atmp = xln
        xsq = rmod
        cst32 = [rln, xln]
        cstbf_t = None
        xsq = sb("xsq", [128, DM], F32)
        xt32 = sb("xt32", [128, 128], F32)
        logT = sb("logT", [8, 128], F32)
        cols = sb("cols", [128, 16], F32)
        cns = sb("cns", [128, 6], F32)
        gsel = sb("gsel", [128, 8 * 16], F32)
        w1st = [sb(f"w1st{i}", [128, 512], F32) for i in range(4)]
        w1bf = [sb(f"w1bf{i}", [128, 1024], BF16) for i in range(2)]
        w1bf4 = [sb(f"w1bf4_{i}", [128, 1024], BF16) for i in range(4)]
        w2bf4 = [w1bf4[0], w1bf4[1], w1bf4[2], w1bf4[3]]
        b1c_s = [sb(f"b1c{i}", [128, 32], F32) for i in range(2)]
        xo32 = [xln, xln]

        pb = [ps(f"pb{i}", [128, 512]) for i in range(8)]

        for _n, _a in (("zsb", zsb), ("Xsb", Xsb), ("tcV", tcV),
                       ("bpp", bpp), ("sil", sil)):
            SBUF[_n] = _a
        sy = mksem("sy")
        sr = mksem("sr")
        gq = mksem("gq")
        pe = mksem("pe")
        ve = mksem("ve")
        se = mksem("se")
        so = mksem("so")

        SY = Emitter("sync"); GQ = Emitter("gpsimd"); PE = Emitter("tensor")
        V = Emitter("vector"); S = Emitter("scalar")

        for a, b in [(I32, ident_d), (Mneg, mneg_d), (repl4s, repl4_d),
                     (selj8s, selj8_d), (a_col, alog_d), (dtb_s, dtb_d),
                     (dsk_s, dsk_d.rearrange("p a b -> p (a b)")), (bnw_s, bnw_d), (cnw_s, cnw_d),
                     (br_s, br_d), (rp1, repl1_d),
                     (rp2, repl2_d)]:
            SY.op(lambda e, a=a, b=b: e.dma_start(out=a[:], in_=b), sy, 16)
        SY.op(lambda e: e.dma_start(out=wr_s[:].rearrange("p (k c) -> p k c", k=8),
                                    in_=wr_d.rearrange("(k p) c -> p k c", p=128)), sy, 16)

        V.wait(sy)
        V.op(lambda e: e.memset(onec[:], 1.0))
        V.op(lambda e: e.tensor_copy(out=Ibf[:], in_=I32[:]))
        V.op(lambda e: e.tensor_copy(out=repl4bf[:], in_=repl4s[:]))
        V.op(lambda e: e.tensor_scalar(out=repl1w[:], in0=rp1[:], scalar1=bnw_s[:],
                                       scalar2=None, op0=ALU.mult))
        V.op(lambda e: e.tensor_scalar(out=repl2w[:], in0=rp2[:], scalar1=bnw_s[:],
                                       scalar2=None, op0=ALU.mult))
        V.op(lambda e: e.tensor_scalar(out=repl1wc[:], in0=rp1[:], scalar1=cnw_s[:],
                                       scalar2=None, op0=ALU.mult))
        V.op(lambda e: e.tensor_scalar(out=repl2wc[:], in0=rp2[:], scalar1=cnw_s[:],
                                       scalar2=None, op0=ALU.mult))
        V.op(lambda e: e.memset(zcol[:], 0.0))
        V.op(lambda e: e.memset(cns[:, 0:1], 1e-5))
        V.op(lambda e: e.memset(cns[:, 1:2], 1.0))
        V.op(lambda e: e.memset(cns[:, 2:3], 1.0 / DI))
        V.op(lambda e: e.memset(cns[:, 3:4], 1.0 / DM))
        V.op(lambda e: e.memset(cns[:, 4:5], -1.0))
        V.op(lambda e: e.memset(state[:], 0.0))
        V.op(lambda e: e.memset(state_bf[:], 0.0))
        V.op(lambda e: e.memset(h1T[:, 32 * 512:33 * 512], 0.0))
        V.op(lambda e: e.memset(h1T[0:1, 32 * 512:32 * 512 + 512], 1.0), ve)
        S.wait(sy)
        S.op(lambda e: e.activation(out=acol_t[:], in_=a_col[:], func=AF.Exp), se)
        V.wait(se)
        V.op(lambda e: e.tensor_scalar(out=a_col[:], in0=acol_t[:], scalar1=-1.0,
                                       scalar2=None, op0=ALU.mult), ve)

        cast_jobs = []
        wpieces = [(i * 1024, 1024) for i in range(5)] + [(5120, 192)]
        for k in range(8):
            for s0, w in wpieces:
                cast_jobs.append((winT_d[k * 128:(k + 1) * 128, s0:s0 + w],
                                  winTbf_d[k * 128:(k + 1) * 128, s0:s0 + w],
                                  None, w))
        for k in range(16):
            cast_jobs.append((woutT_d[k * 128:(k + 1) * 128, :],
                              woutTbf_d[k * 128:(k + 1) * 128, :], None, DM))
        for half in range(2):
            cast_jobs.append((bbv_d[half], None,
                              bbv_s[:, half * 1024:(half + 1) * 1024], 1024))
            cast_jobs.append((cbv_d[half], None,
                              cbv_s[:, half * 1024:(half + 1) * 1024], 1024))
            cast_jobs.append((nwrep_d[:, half * 1024:(half + 1) * 1024], None,
                              nwrep_s[:, half * 1024:(half + 1) * 1024], 1024))

        def emit_cast_job(srcap, dstap, sdst, w, par):
            GQ.wait(ve)
            GQ.op(lambda e, s=srcap, w=w, par=par: e.dma_start(
                out=cst32[par][:, 0:w], in_=s), gq, 16)
            V.wait(gq)
            if sdst is None:
                V.op(lambda e, w=w, par=par: e.tensor_copy(
                    out=yg[:, par * 1024:par * 1024 + w],
                    in_=cst32[par][:, 0:w]), ve)
                GQ.wait(ve)
                GQ.op(lambda e, d=dstap, w=w, par=par: e.dma_start(
                    out=d, in_=yg[:, par * 1024:par * 1024 + w]), gq, 16)
            else:
                V.op(lambda e, sdst=sdst, w=w, par=par: e.tensor_copy(
                    out=sdst, in_=cst32[par][:, 0:w]), ve)

        for idx, job in enumerate(cast_jobs):
            emit_cast_job(*job, idx % 2)
        scratch_done = gq.count

        wcast_jobs = []
        for e_i in range(E):
            for ft in range(32):
                wcast_jobs.append((
                    ("w1", w1_d[e_i, ft].transpose([1, 0, 2])),
                    w1bf_d[e_i, ft], 1024))
            for fs in range(33):
                wcast_jobs.append((("w2", w2_d[e_i, fs]), w2bf_d[e_i, fs], 1024))
        wcast_pos = [0]

        def emit_wcasts(count):
            for _ in range(count):
                if wcast_pos[0] >= len(wcast_jobs):
                    return
                (kind, srcap), dstap, w = wcast_jobs[wcast_pos[0]]
                par = wcast_pos[0] % 2
                wcast_pos[0] += 1
                if kind == "w1":
                    h0 = srcap[:, 0:4, :]
                    h1 = srcap[:, 4:8, :]
                else:
                    h0 = srcap[:, 0:512]
                    h1 = srcap[:, 512:1024]
                GQ.wait(ve)
                GQ.op(lambda e, s=h0, par=par: e.dma_start(
                    out=w1st[2 * par][:], in_=s), gq, 16)
                GQ.op(lambda e, s=h1, par=par: e.dma_start(
                    out=w1st[2 * par + 1][:], in_=s), gq, 16)
                V.wait(gq)
                V.op(lambda e, par=par: e.tensor_copy(
                    out=w1bf[par][:, 0:512], in_=w1st[2 * par][:]))
                V.op(lambda e, par=par: e.tensor_copy(
                    out=w1bf[par][:, 512:1024], in_=w1st[2 * par + 1][:]), ve)
                GQ.wait(ve)
                GQ.op(lambda e, d=dstap, par=par: e.dma_start(
                    out=d, in_=w1bf[par][:]), gq, 16)

        pe_tile_end = {}
        for t in range(16):
            par = t % 2
            if t >= 2:
                SY.wait(pe, pe_tile_end[t - 2])
            SY.op(lambda e, t=t, par=par: e.dma_start(
                out=usb[par][:], in_=u_d[t * 128:(t + 1) * 128, :]), sy, 16)
            PE.wait(sy)
            for k in range(8):
                bank = pb[k % 2]
                PE.wait(ve, ve.count - 1)
                PE.op(lambda e, k=k, par=par, bank=bank: e.transpose(
                    bank[:, 0:128], usb[par][:, k * 128:(k + 1) * 128], I32[:]), pe)
                V.wait(pe)
                V.op(lambda e, k=k, t=t, bank=bank: e.tensor_copy(
                    out=xT[:, k * L + t * 128: k * L + (t + 1) * 128],
                    in_=bank[:, 0:128]), ve)
            pe_tile_end[t] = pe.count

        PE.wait(gq, scratch_done)
        GQ.wait(gq, scratch_done)

        nat_groups = [(ZC0 + 512 * i, 512) for i in range(4)] + \
                     [(XC0 + 512 * i, 512) for i in range(4)]
        tt_groups = [(BCC0, 128), (DLC0, 64)] + [(THC0 + 128 * j, 128) for j in range(8)]

        def proj_stage(c):
            par = c % 2
            xcols = slice(c * 128, (c + 1) * 128)
            for gi, (c0, w) in enumerate(nat_groups + tt_groups):
                is_nat = gi < len(nat_groups)
                bank = pb[gi % 2]
                buf = wgb[gi % 2]
                SY.wait(pe)
                SY.op(lambda e, c0=c0, w=w, buf=buf: e.dma_start(
                    out=buf[:, 0:8 * w].rearrange("p (k c) -> p k c", k=8),
                    in_=winTbf_d[:, c0:c0 + w].rearrange("(k p) c -> p k c", p=128)),
                    sy, 16)
                PE.wait(sy)
                PE.wait(ve)
                for k in range(8):
                    if is_nat:
                        PE.op(lambda e, k=k, w=w, buf=buf, bank=bank, xcols=xcols:
                              e.matmul(bank[:, 0:w],
                                       xT[:, k * L:(k + 1) * L][:, xcols],
                                       buf[:, k * w:(k + 1) * w],
                                       start=(k == 0), stop=(k == 7)),
                              pe if k == 7 else None)
                    else:
                        PE.op(lambda e, k=k, w=w, buf=buf, bank=bank, xcols=xcols:
                              e.matmul(bank[0:w, 0:128],
                                       buf[:, k * w:(k + 1) * w],
                                       xT[:, k * L:(k + 1) * L][:, xcols],
                                       start=(k == 0), stop=(k == 7)),
                              pe if k == 7 else None)
                V.wait(pe)
                if is_nat:
                    if c0 < XC0:
                        off = c0 - ZC0
                        V.op(lambda e, off=off, w=w, bank=bank, par=par: e.tensor_copy(
                            out=zsb[:, par * DI + off: par * DI + off + w],
                            in_=bank[:, 0:w]), ve)
                    else:
                        off = c0 - XC0
                        V.op(lambda e, off=off, w=w, bank=bank, par=par: e.tensor_copy(
                            out=Xsb[:, par * DI + off: par * DI + off + w],
                            in_=bank[:, 0:w]), ve)
                elif c0 == BCC0:
                    V.op(lambda e, bank=bank, par=par: e.tensor_copy(
                        out=bcT[:, par * 128:(par + 1) * 128], in_=bank[:, 0:128]), ve)
                elif c0 == DLC0:
                    V.op(lambda e, bank=bank, par=par: e.tensor_copy(
                        out=dtT[:, par * 128:(par + 1) * 128], in_=bank[0:32, 0:128]))
                    V.op(lambda e, bank=bank, par=par: e.tensor_copy(
                        out=lamT[:, par * 128:(par + 1) * 128], in_=bank[32:64, 0:128]),
                        ve)
                else:
                    j = (c0 - THC0) // 128
                    if c == 0:
                        ini = 0.0
                    else:
                        prev = (c - 1) % 2
                        ini = tcV[:, prev * 1024 + j * 128 + 127:
                                  prev * 1024 + j * 128 + 128]
                    V.op(lambda e, bank=bank, par=par, j=j, ini=ini:
                         e.tensor_tensor_scan(
                             out=tcV[:, par * 1024 + j * 128: par * 1024 + (j + 1) * 128],
                             data0=onec[:], data1=bank[:, 0:128], initial=ini,
                             op0=ALU.mult, op1=ALU.add), ve)
            S.wait(ve)
            S.op(lambda e, par=par: e.activation(
                out=s_t[:], in_=dtT[:, par * 128:(par + 1) * 128], func=AF.Exp,
                bias=dtb_s[:]))
            S.op(lambda e, par=par: e.activation(
                out=dtT[:, par * 128:(par + 1) * 128], in_=s_t[:], func=AF.Ln,
                bias=cns[0:32, 1:2]))
            S.op(lambda e, par=par: e.activation(
                out=lamT[:, par * 128:(par + 1) * 128],
                in_=lamT[:, par * 128:(par + 1) * 128], func=AF.Sigmoid), se)
            V.wait(se)
            V.op(lambda e, par=par: e.tensor_tensor(
                out=g_t[:, par * 128:(par + 1) * 128],
                in0=lamT[:, par * 128:(par + 1) * 128],
                in1=dtT[:, par * 128:(par + 1) * 128], op=ALU.mult))
            V.op(lambda e, par=par: e.tensor_tensor(
                out=q_t[:, par * 128:(par + 1) * 128],
                in0=dtT[:, par * 128:(par + 1) * 128],
                in1=g_t[:, par * 128:(par + 1) * 128], op=ALU.subtract), ve)

        def ssd_stage(c):
            par = c % 2
            npar = (c + 1) % 2
            S.wait(ve)
            S.op(lambda e, par=par: e.activation(
                out=sil, in_=zsb[:, par * DI:(par + 1) * DI], func=AF.Silu), se)
            V.op(lambda e, par=par: e.tensor_tensor(
                out=ghat[:, 0:127], in0=g_t[:, par * 128:par * 128 + 127],
                in1=q_t[:, par * 128 + 1:par * 128 + 128], op=ALU.add))
            qn = zcol[:] if c == NCH - 1 else q_t[:, npar * 128:npar * 128 + 1]
            V.op(lambda e, qn=qn, par=par: e.tensor_tensor(
                out=ghat[:, 127:128], in0=g_t[:, par * 128 + 127:par * 128 + 128],
                in1=qn, op=ALU.add))
            V.op(lambda e, par=par: e.tensor_scalar(
                out=s_t[:], in0=dtT[:, par * 128:(par + 1) * 128], scalar1=a_col[:],
                scalar2=None, op0=ALU.mult))
            V.op(lambda e: e.tensor_tensor_scan(
                out=S_T[:], data0=onec[0:32, :], data1=s_t[:], initial=0.0,
                op0=ALU.mult, op1=ALU.add), ve)
            S.wait(ve)
            S.op(lambda e: e.activation(out=lng[:], in_=ghat[:], func=AF.Ln))
            S.op(lambda e: e.activation(out=expd[:], in_=S_T[:], func=AF.Exp,
                                        bias=S_T[:, 127:128], scale=-1.0))
            S.op(lambda e: e.activation(out=dec_c[:], in_=S_T[:, 127:128],
                                        func=AF.Exp), se)
            V.wait(se)
            V.op(lambda e: e.tensor_tensor(out=pack[0:32, :], in0=lng[:], in1=S_T[:],
                                           op=ALU.subtract))
            V.op(lambda e, par=par: e.tensor_tensor(
                out=pack[32:64, :], in0=g_t[:, par * 128:(par + 1) * 128],
                in1=ghat[:], op=ALU.subtract))
            V.op(lambda e: e.tensor_copy(out=pack[64:96, :], in_=S_T[:]))
            V.op(lambda e: e.tensor_tensor(out=wT[:], in0=ghat[:], in1=expd[:],
                                           op=ALU.mult))
            V.op(lambda e: e.tensor_scalar(out=dec4i[:], in0=selj8s[:],
                                           scalar1=dec_c[:], scalar2=None,
                                           op0=ALU.mult), ve)
            PE.wait(ve)
            PE.op(lambda e: e.transpose(pb[2][:, 0:128], pack[:], I32[:]), pe)
            PE.op(lambda e: e.matmul(pb[3][:, 0:8], repl4s[:], dec4i[:],
                                     start=True, stop=True), pe)
            V.wait(pe)
            V.op(lambda e: e.tensor_copy(out=packN[:], in_=pb[2][:, 0:128]))
            V.op(lambda e: e.tensor_copy(out=dec4[:], in_=pb[3][:, 0:8]), ve)
            S.wait(ve)
            S.op(lambda e: e.activation(out=E_nat[:], in_=packN[:, 64:96],
                                        func=AF.Exp), se)

            tcs = tcV[:, par * 1024:(par + 1) * 1024]
            V.op(lambda e, tcs=tcs: e.tensor_scalar(
                out=atmp[:], in0=tcs, scalar1=1.0 / TWO_PI, scalar2=MAGIC,
                op0=ALU.mult, op1=ALU.add))
            V.op(lambda e: e.tensor_scalar(
                out=atmp[:], in0=atmp[:], scalar1=MAGIC, scalar2=None,
                op0=ALU.subtract))
            V.op(lambda e, tcs=tcs: e.scalar_tensor_tensor(
                out=rmod[:], in0=atmp[:], scalar=-TWO_PI, in1=tcs,
                op0=ALU.mult, op1=ALU.add))
            V.op(lambda e: e.tensor_scalar(
                out=rmod[:], in0=rmod[:], scalar1=-PIC, scalar2=PIC,
                op0=ALU.max, op1=ALU.min))
            V.op(lambda e: e.tensor_scalar(
                out=atmp[:], in0=rmod[:], scalar1=PI / 2, scalar2=None, op0=ALU.is_gt))
            V.op(lambda e: e.tensor_scalar(
                out=carg[:], in0=rmod[:], scalar1=PI / 2, scalar2=None, op0=ALU.add))
            V.op(lambda e: e.scalar_tensor_tensor(
                out=carg[:], in0=atmp[:], scalar=-TWO_PI, in1=carg[:],
                op0=ALU.mult, op1=ALU.add))
            V.op(lambda e: e.tensor_scalar(
                out=carg[:], in0=carg[:], scalar1=-PIC, scalar2=PIC,
                op0=ALU.max, op1=ALU.min), ve)
            S.wait(ve)
            S.op(lambda e: e.activation(out=rmod[:], in_=rmod[:], func=AF.Sin))
            S.op(lambda e: e.activation(out=carg[:], in_=carg[:], func=AF.Sin), se)
            V.wait(se)
            V.op(lambda e: e.tensor_copy(out=sinV[:], in_=rmod[:]))
            V.op(lambda e: e.tensor_copy(out=cosV[:], in_=carg[:]), ve)

            bct = bcT[:, par * 128:(par + 1) * 128]
            V.op(lambda e, bct=bct: e.tensor_tensor(out=sqT[:], in0=bct, in1=bct,
                                                    op=ALU.mult), ve)
            PE.wait(ve)
            PE.op(lambda e: e.matmul(pb[2][0:1, 0:128], onec[0:64, 0:1], sqT[0:64, :],
                                     start=True, stop=True), pe)
            PE.op(lambda e: e.matmul(pb[3][0:1, 0:128], onec[64:128, 0:1],
                                     sqT[64:128, :], start=True, stop=True,
                                     tile_position=(64, 0)), pe)
            S.wait(pe)
            S.op(lambda e: e.activation(out=invB[:], in_=pb[2][0:1, 0:128],
                                        func=AF.Sqrt, scale=1.0 / 64.0, bias=cns[0:1, 0:1]))
            S.op(lambda e: e.activation(out=invC[:], in_=pb[3][0:1, 0:128],
                                        func=AF.Sqrt, scale=1.0 / 64.0, bias=cns[0:1, 0:1]), se)
            V.wait(se)
            V.op(lambda e: e.reciprocal(out=invB[:], in_=invB[:]))
            V.op(lambda e: e.reciprocal(out=invC[:], in_=invC[:]), ve)
            PE.wait(ve)
            PE.op(lambda e: e.matmul(pb[2][0:64, 0:128], onec[0:1, 0:64], invB[:],
                                     start=True, stop=True), pe)
            PE.op(lambda e: e.matmul(pb[3][0:64, 0:128], onec[0:1, 0:64], invC[:],
                                     start=True, stop=True), pe)
            V.wait(pe)
            V.op(lambda e, bct=bct: e.tensor_tensor(out=brawn[:], in0=bct[0:64, :],
                                                    in1=pb[2][0:64, 0:128],
                                                    op=ALU.mult))
            V.op(lambda e, bct=bct: e.tensor_tensor(out=crawn[:], in0=bct[64:128, :],
                                                    in1=pb[3][0:64, 0:128],
                                                    op=ALU.mult))
            V.op(lambda e: e.tensor_copy(
                out=brawn8[:].rearrange("p (r c) -> p r c", r=8),
                in_=brawn[:].unsqueeze(1).broadcast_to([64, 8, 128])))
            V.op(lambda e: e.tensor_copy(
                out=crawn8[:].rearrange("p (r c) -> p r c", r=8),
                in_=crawn[:].unsqueeze(1).broadcast_to([64, 8, 128])), ve)
            PE.wait(ve)
            for half in range(2):
                rw = repl1w if half == 0 else repl2w
                rwc = repl1wc if half == 0 else repl2wc
                dst_b = vB1 if half == 0 else vB2
                dst_c = vC1 if half == 0 else vC2
                for piece in range(2):
                    PE.wait(ve)
                    PE.op(lambda e, rw=rw, piece=piece: e.matmul(
                        pb[2][:, 0:512], rw[:],
                        brawn8[:, piece * 512:(piece + 1) * 512],
                        start=True, stop=True), pe)
                    PE.op(lambda e, rwc=rwc, piece=piece: e.matmul(
                        pb[3][:, 0:512], rwc[:],
                        crawn8[:, piece * 512:(piece + 1) * 512],
                        start=True, stop=True), pe)
                    V.wait(pe)
                    V.op(lambda e, dst_b=dst_b, piece=piece, half=half: e.tensor_tensor(
                        out=dst_b[:, piece * 512:(piece + 1) * 512],
                        in0=bbv_s[:, half * 1024 + piece * 512:
                                  half * 1024 + (piece + 1) * 512],
                        in1=pb[2][:, 0:512], op=ALU.add))
                    V.op(lambda e, dst_c=dst_c, piece=piece, half=half: e.tensor_tensor(
                        out=dst_c[:, piece * 512:(piece + 1) * 512],
                        in0=cbv_s[:, half * 1024 + piece * 512:
                                  half * 1024 + (piece + 1) * 512],
                        in1=pb[3][:, 0:512], op=ALU.add), ve)
            V.wait(se)
            V.wait(pe)
            for (v1, v2) in ((vB1, vB2), (vC1, vC2)):
                V.op(lambda e, v1=v1: e.tensor_tensor(out=rta[:], in0=v1[:],
                                                      in1=cosV[:], op=ALU.mult))
                V.op(lambda e, v2=v2: e.tensor_tensor(out=rtb[:], in0=v2[:],
                                                      in1=sinV[:], op=ALU.mult))
                V.op(lambda e, v2=v2: e.tensor_tensor(out=v2[:], in0=v2[:],
                                                      in1=cosV[:], op=ALU.mult))
                V.op(lambda e, v1=v1: e.tensor_tensor(out=v1[:], in0=v1[:],
                                                      in1=sinV[:], op=ALU.mult))
                V.op(lambda e, v1=v1, v2=v2: e.tensor_tensor(out=v2[:], in0=v2[:],
                                                             in1=v1[:], op=ALU.add))
                V.op(lambda e, v1=v1: e.tensor_tensor(out=v1[:], in0=rta[:],
                                                      in1=rtb[:], op=ALU.subtract), ve)
            V.op(lambda e: e.tensor_tensor(
                out=wvsel[:].rearrange("p (r c) -> p r c", r=8),
                in0=wT[:].unsqueeze(1).broadcast_to([32, 8, 128]),
                in1=selj8s[:].unsqueeze(2).broadcast_to([32, 8, 128]),
                op=ALU.mult), ve)
            PE.wait(ve)
            for piece in range(2):
                PE.op(lambda e, piece=piece: e.matmul(
                    pb[2 + piece][:, 0:512], repl4bf[:],
                    wvsel[:, piece * 512:(piece + 1) * 512],
                    start=True, stop=True), pe)
            V.wait(pe)
            for piece in range(2):
                V.op(lambda e, piece=piece: e.tensor_tensor(
                    out=vBW1[:, piece * 512:(piece + 1) * 512],
                    in0=vB1[:, piece * 512:(piece + 1) * 512],
                    in1=pb[2 + piece][:, 0:512], op=ALU.mult))
                V.op(lambda e, piece=piece: e.tensor_tensor(
                    out=vBW2[:, piece * 512:(piece + 1) * 512],
                    in0=vB2[:, piece * 512:(piece + 1) * 512],
                    in1=pb[2 + piece][:, 0:512], op=ALU.mult), ve)
            for j in range(8):
                for half in range(2):
                    src = vBW1 if half == 0 else vBW2
                    bank = pb[2 + ((j * 2 + half) % 2)]
                    PE.wait(ve)
                    PE.op(lambda e, src=src, j=j, bank=bank: e.matmul(
                        bank[:, 0:128], src[:, j * 128:(j + 1) * 128], Ibf[:],
                        start=True, stop=True), pe)
                    V.wait(pe)
                    V.op(lambda e, j=j, half=half, bank=bank: e.tensor_copy(
                        out=bpp.rearrange("p (jj mm n) -> p jj mm n", jj=8, mm=4)
                            [:, j, :, half * 32:half * 32 + 32],
                        in_=bank[:, 0:128].rearrange("p (m k) -> p m k", m=4)), ve)

            for m in range(4):
                pbYx = pb[4 + (m % 2)]
                pbYs = pb[6 + (m % 2)]
                pm = slice(32 * m, 32 * m + 32)
                for jj in range(8):
                    h = 4 * jj + m
                    j = jj
                    dpar = jj % 2
                    cbank = pb[jj % 2]
                    V.wait(pe)
                    V.op(lambda e, h=h, dpar=dpar: e.tensor_tensor(
                        out=pre_t[dpar][:], in0=Mneg[:],
                        in1=packN[:, 64 + h:65 + h].broadcast_to([128, 128]),
                        op=ALU.add), ve)
                    PE.wait(ve)
                    PE.wait(se)
                    PE.op(lambda e, dpar=dpar: e.transpose(
                        pb[2 + dpar][:, 0:128], pre_t[dpar][:], I32[:]), pe)
                    PE.op(lambda e, pm=pm, j=j, cbank=cbank, m=m: e.matmul(
                        cbank[:, 0:128], vB1[pm, j * 128:(j + 1) * 128],
                        vC1[pm, j * 128:(j + 1) * 128], start=True, stop=False,
                        tile_position=(32 * m, 0)))
                    PE.op(lambda e, pm=pm, j=j, cbank=cbank, m=m: e.matmul(
                        cbank[:, 0:128], vB2[pm, j * 128:(j + 1) * 128],
                        vC2[pm, j * 128:(j + 1) * 128], start=False, stop=True,
                        tile_position=(32 * m, 0)), pe)
                    S.wait(ve)
                    S.wait(pe)
                    S.op(lambda e, dpar=dpar, h=h: e.activation(
                        out=D2[dpar][:], in_=pb[2 + dpar][:, 0:128], func=AF.Exp,
                        bias=packN[:, h:h + 1]), se)
                    V.wait(se)
                    V.op(lambda e, dpar=dpar, h=h: e.scalar_tensor_tensor(
                        out=D2[dpar][:], in0=I32[:], scalar=packN[:, 32 + h:33 + h],
                        in1=D2[dpar][:], op0=ALU.mult, op1=ALU.add))
                    V.wait(pe)
                    V.op(lambda e, dpar=dpar, cbank=cbank: e.tensor_tensor(
                        out=G_bf[dpar][:], in0=D2[dpar][:], in1=cbank[:, 0:128],
                        op=ALU.mult), ve)
                    PE.wait(ve)
                    PE.op(lambda e, dpar=dpar, h=h, jj=jj, pbYx=pbYx, par=par:
                          e.matmul(pbYx[:, jj * 64:(jj + 1) * 64], G_bf[dpar][:],
                                   Xsb[:, par * DI + h * 64: par * DI + (h + 1) * 64],
                                   start=True, stop=True, skip_group_check=True),
                          pe if jj == 7 else None)
                    PE.op(lambda e, pm=pm, j=j, jj=jj, pbYs=pbYs, m=m: e.matmul(
                        pbYs[:, jj * 64:(jj + 1) * 64],
                        vC1[pm, j * 128:(j + 1) * 128],
                        state_bf[pm, j * 128:j * 128 + 64], start=True,
                        stop=False, tile_position=(32 * m, 0),
                        skip_group_check=True))
                    PE.op(lambda e, pm=pm, j=j, jj=jj, pbYs=pbYs, m=m: e.matmul(
                        pbYs[:, jj * 64:(jj + 1) * 64],
                        vC2[pm, j * 128:(j + 1) * 128],
                        state_bf[pm, j * 128 + 64:(j + 1) * 128], start=False,
                        stop=True, tile_position=(32 * m, 0),
                        skip_group_check=True), pe)
                    S.wait(pe)
                    if jj == 0:
                        S.wait(ve)
                    S.op(lambda e, jj=jj, h=h, pbYs=pbYs: e.activation(
                        out=ysc[:, jj * 64:(jj + 1) * 64],
                        in_=pbYs[:, jj * 64:(jj + 1) * 64],
                        func=AF.Identity, scale=E_nat[:, h:h + 1]), se)
                V.wait(se)
                V.op(lambda e, pbYx=pbYx: e.tensor_tensor(
                    out=ysc[:], in0=ysc[:], in1=pbYx[:, 0:512], op=ALU.add))
                V.op(lambda e, m=m, par=par: e.tensor_tensor(
                    out=rmod[:, 0:512].rearrange("p (j q) -> p j q", j=8),
                    in0=Xsb[:, par * DI:(par + 1) * DI].rearrange(
                        "p (j hm q) -> p j hm q", j=8, hm=4)[:, :, m, :],
                    in1=dsk_s[:].rearrange("p (a b) -> p a b", a=4)[:, m, :]
                        .unsqueeze(2).broadcast_to([128, 8, 64]), op=ALU.mult))
                V.op(lambda e: e.tensor_tensor(
                    out=ysc[:], in0=ysc[:], in1=rmod[:, 0:512], op=ALU.add))
                V.op(lambda e, m=m: e.tensor_copy(
                    out=yg[:].rearrange("p (j hm q) -> p j hm q", j=8, hm=4)[:, :, m, :],
                    in_=ysc[:].rearrange("p (j c) -> p j c", j=8)), ve)

            V.op(lambda e: e.tensor_tensor(out=yg[:], in0=yg[:], in1=sil,
                                           op=ALU.mult), ve)
            PE.wait(ve)
            for m in range(4):
                pm = slice(32 * m, 32 * m + 32)
                for j in range(8):
                    h = 4 * j + m
                    bank = pb[6] if j < 4 else pb[7]
                    jo = (j % 4) * 128
                    PE.op(lambda e, h=h, pm=pm, bank=bank, jo=jo, m=m, par=par:
                          e.matmul(bank[pm, jo:jo + 64],
                                   bpp[:, h * 64:h * 64 + 32],
                                   Xsb[:, par * DI + h * 64: par * DI + (h + 1) * 64],
                                   start=True, stop=True,
                                   tile_position=(0, 32 * m),
                                   skip_group_check=True))
                    PE.op(lambda e, h=h, pm=pm, bank=bank, jo=jo, m=m, par=par:
                          e.matmul(bank[pm, jo + 64:jo + 128],
                                   bpp[:, h * 64 + 32:(h + 1) * 64],
                                   Xsb[:, par * DI + h * 64: par * DI + (h + 1) * 64],
                                   start=True, stop=True,
                                   tile_position=(0, 32 * m),
                                   skip_group_check=True),
                          pe if (j in (3, 7)) else None)
            V.wait(pe)
            V.op(lambda e: e.tensor_tensor(
                out=state[:].rearrange("p (j c) -> p j c", j=8),
                in0=state[:].rearrange("p (j c) -> p j c", j=8),
                in1=dec4[:].unsqueeze(2).broadcast_to([128, 8, 128]), op=ALU.mult))
            V.op(lambda e: e.tensor_tensor(out=state[:, 0:512], in0=state[:, 0:512],
                                           in1=pb[6][:, 0:512], op=ALU.add))
            V.op(lambda e: e.tensor_tensor(out=state[:, 512:1024],
                                           in0=state[:, 512:1024],
                                           in1=pb[7][:, 0:512], op=ALU.add))
            V.op(lambda e: e.tensor_copy(out=state_bf[:], in_=state[:]), ve)

            V.op(lambda e: e.tensor_tensor(out=rmod[:], in0=yg[:, 0:1024],
                                           in1=yg[:, 0:1024], op=ALU.mult))
            V.op(lambda e: e.tensor_tensor(out=rln[:], in0=yg[:, 1024:2048],
                                           in1=yg[:, 1024:2048], op=ALU.mult))
            V.op(lambda e: e.tensor_reduce(out=cols[:, 0:1], in_=rmod[:], axis=AX.X,
                                           op=ALU.add))
            V.op(lambda e: e.tensor_reduce(out=cols[:, 1:2], in_=rln[:], axis=AX.X,
                                           op=ALU.add), ve)
            S.wait(ve)
            S.op(lambda e: e.activation(out=cols[:, 12:13], in_=cols[:, 0:1],
                                        func=AF.Identity, bias=cols[:, 1:2]))
            S.op(lambda e: e.activation(out=cols[:, 1:2], in_=cols[:, 12:13],
                                        func=AF.Sqrt, scale=1.0 / DI,
                                        bias=cns[:, 0:1]), se)
            V.wait(se)
            V.op(lambda e: e.reciprocal(out=cols[:, 13:14], in_=cols[:, 1:2]), ve)
            S.wait(ve)
            S.op(lambda e: e.activation(out=cols[:, 2:3], in_=cols[:, 13:14],
                                        func=AF.Identity), se)
            V.wait(se)
            V.op(lambda e: e.tensor_tensor(
                out=yg[:], in0=yg[:],
                in1=cols[:, 2:3].broadcast_to([128, DI]), op=ALU.mult))
            V.op(lambda e: e.tensor_tensor(out=yg[:], in0=yg[:], in1=nwrep_s[:],
                                           op=ALU.mult), ve)
            for k in range(16):
                bank = pb[2 + (k % 2)]
                ytd = vB1 if k < 8 else vB2
                ko = k % 8
                PE.wait(ve)
                PE.wait(se)
                PE.op(lambda e, k=k, bank=bank: e.matmul(
                    bank[:, 0:128], yg[:, k * 128:(k + 1) * 128], Ibf[:],
                    start=True, stop=True), pe)
                V.wait(pe)
                V.op(lambda e, ko=ko, bank=bank, ytd=ytd: e.tensor_copy(
                    out=ytd[:, ko * 128:(ko + 1) * 128],
                    in_=bank[:, 0:128]), ve)
            for k in range(16):
                ytd = vB1 if k < 8 else vB2
                ko = k % 8
                GQ.wait(pe)
                GQ.op(lambda e, k=k: e.dma_start(
                    out=wob[k % 2][:], in_=woutTbf_d[k * 128:(k + 1) * 128, :]),
                    gq, 16)
                PE.wait(gq)
                PE.wait(ve)
                for nh in range(2):
                    PE.op(lambda e, ko=ko, nh=nh, k=k, ytd=ytd: e.matmul(
                        pb[nh][:, 0:512], ytd[:, ko * 128:(ko + 1) * 128],
                        wob[k % 2][:, nh * 512:(nh + 1) * 512],
                        start=(k == 0), stop=(k == 15)), pe)
            SY.wait(ve)
            SY.op(lambda e, c=c, par=par: e.dma_start(
                out=usb[par][:], in_=u_d[c * 128:(c + 1) * 128, :]), sy, 16)
            V.wait(sy)
            V.wait(pe)
            V.op(lambda e, par=par: e.tensor_tensor(
                out=rln[:, 0:512], in0=usb[par][:, 0:512], in1=pb[0][:, 0:512],
                op=ALU.add))
            V.op(lambda e, par=par: e.tensor_tensor(
                out=rln[:, 512:1024], in0=usb[par][:, 512:1024], in1=pb[1][:, 0:512],
                op=ALU.add))
            V.op(lambda e: e.tensor_reduce(out=cols[:, 2:3], in_=rln[:], axis=AX.X,
                                           op=ALU.add), ve)
            S.wait(ve)
            S.op(lambda e: e.activation(out=cols[:, 4:5], in_=cols[:, 2:3],
                                        func=AF.Identity, scale=1.0 / DM), se)
            V.wait(se)
            V.op(lambda e: e.tensor_tensor(
                out=xsq[:], in0=rln[:],
                in1=cols[:, 4:5].broadcast_to([128, DM]), op=ALU.subtract))
            V.wait(sr)
            V.op(lambda e: e.tensor_tensor(out=xln[:], in0=xsq[:], in1=xsq[:],
                                           op=ALU.mult))
            V.op(lambda e: e.tensor_reduce(out=cols[:, 5:6], in_=xln[:], axis=AX.X,
                                           op=ALU.add), ve)
            S.wait(ve)
            S.op(lambda e: e.activation(out=cols[:, 8:9], in_=cols[:, 5:6],
                                        func=AF.Sqrt, scale=1.0 / DM,
                                        bias=cns[:, 0:1]), se)
            V.wait(se)
            V.op(lambda e: e.reciprocal(out=cols[:, 9:10], in_=cols[:, 8:9]), ve)
            S.wait(ve)
            S.op(lambda e: e.activation(out=cols[:, 10:11], in_=cols[:, 9:10],
                                        func=AF.Identity), se)
            V.wait(se)
            V.op(lambda e: e.tensor_tensor(
                out=xln[:], in0=xsq[:],
                in1=cols[:, 10:11].broadcast_to([128, DM]), op=ALU.mult), ve)
            SY.wait(ve)
            SY.op(lambda e, c=c: e.dma_start(
                out=r1scr_d[c * 128:(c + 1) * 128, :], in_=xln[:]), sr, 16)
            for k in range(8):
                bank = pb[2 + (k % 2)]
                PE.wait(ve)
                PE.op(lambda e, k=k, bank=bank: e.transpose(
                    bank[:, 0:128], xln[:, k * 128:(k + 1) * 128], I32[:]), pe)
                V.wait(pe)
                V.op(lambda e, bank=bank: e.tensor_copy(out=xt32[:],
                                                        in_=bank[:, 0:128]))
                V.op(lambda e, k=k, c=c, bank=bank: e.tensor_copy(
                    out=xT[:, k * L + c * 128:k * L + (c + 1) * 128],
                    in_=bank[:, 0:128]), ve)
                PE.wait(ve)
                PE.op(lambda e, k=k: e.matmul(
                    pb[4][0:8, 0:128], wr_s[:, k * 8:(k + 1) * 8], xt32[:],
                    start=(k == 0), stop=(k == 7)), pe if k == 7 else None)
            S.wait(pe)
            S.op(lambda e: e.activation(out=logT[:], in_=pb[4][0:8, 0:128],
                                        func=AF.Identity, bias=br_s[:]), se)
            PE.wait(se)
            PE.wait(ve)
            PE.op(lambda e: e.transpose(pb[5][:, 0:8], logT[:], I32[0:8, 0:8]), pe)
            V.wait(pe)
            V.op(lambda e: e.tensor_reduce(out=cols[:, 14:15], in_=pb[5][:, 0:8],
                                           axis=AX.X, op=ALU.max), ve)
            S.wait(ve)
            S.op(lambda e: e.activation(out=cols[:, 13:14], in_=cols[:, 14:15],
                                        func=AF.Identity), se)
            V.wait(se)
            V.op(lambda e: e.tensor_tensor(
                out=xt32[:, 8:16], in0=pb[5][:, 0:8],
                in1=cols[:, 13:14].broadcast_to([128, 8]), op=ALU.is_ge))
            V.op(lambda e: e.tensor_tensor(out=cols[:, 15:16], in0=cols[:, 13:14],
                                           in1=cns[:, 4:5], op=ALU.mult), ve)
            S.wait(ve)
            S.op(lambda e: e.activation(out=xt32[:, 0:8], in_=pb[5][:, 0:8],
                                        func=AF.Exp, bias=cols[:, 15:16],
                                        accum_out=cols[:, 11:12]), se)
            V.wait(se)
            V.op(lambda e: e.reciprocal(out=cols[:, 12:13], in_=cols[:, 11:12]), ve)
            S.wait(ve)
            S.op(lambda e: e.activation(out=cols[:, 10:11], in_=cols[:, 12:13],
                                        func=AF.Identity), se)
            V.wait(se)
            V.op(lambda e, c=c: e.tensor_tensor(
                out=gsel[:, c * 8:(c + 1) * 8], in0=xt32[:, 8:16],
                in1=cols[:, 10:11].broadcast_to([128, 8]), op=ALU.mult), ve)

        emit_wcasts(20)
        proj_stage(0)
        nssd = NCH if upto is None else min(NCH, upto)
        for c in range(nssd):
            if c + 1 < NCH:
                proj_stage(c + 1)
            ssd_stage(c)
            emit_wcasts(33)

        emit_wcasts(len(wcast_jobs))
        GQ.wait(gq)
        PE.wait(gq)
        if upto is None:
            for tg in range(4):
                V.op(lambda e: e.memset(acc[:], 0.0), ve)
                for e_i in range(E):
                    par_e = e_i % 2
                    SY.wait(se)
                    SY.op(lambda e, e_i=e_i, par_e=par_e: e.dma_start(
                        out=b1c_s[par_e][:], in_=b1_d[e_i]), sy, 16)
                    for ft in range(32):
                        par = ft % 4
                        GQ.wait(pe, pe.count if ft == 0 else pe.count - 3)
                        GQ.op(lambda e, e_i=e_i, ft=ft, par=par: e.dma_start(
                            out=w1bf4[par][:], in_=w1bf_d[e_i, ft]), gq, 16)
                        bank = pb[ft % 4]
                        PE.wait(gq)
                        PE.wait(se)
                        for k in range(8):
                            PE.op(lambda e, k=k, par=par, bank=bank, tg=tg: e.matmul(
                                bank[:, 0:512], w1bf4[par][:, k * 128:(k + 1) * 128],
                                xT[:, k * L + tg * 512:k * L + (tg + 1) * 512],
                                start=(k == 0), stop=(k == 7)),
                                pe if k == 7 else None)
                        S.wait(pe)
                        S.wait(sy)
                        S.op(lambda e, ft=ft, bank=bank, par_e=par_e: e.activation(
                            out=h1T[:, ft * 512:(ft + 1) * 512], in_=bank[:, 0:512],
                            func=AF.Relu, bias=b1c_s[par_e][:, ft:ft + 1]), se)
                    for fs in range(33):
                        par = fs % 4
                        GQ.wait(pe, pe.count if fs == 0 else pe.count - 3)
                        GQ.op(lambda e, e_i=e_i, fs=fs, par=par: e.dma_start(
                            out=w2bf4[par][:], in_=w2bf_d[e_i, fs]), gq, 16)
                        PE.wait(gq)
                        if fs == 0:
                            PE.wait(se)
                        for tt in range(4):
                            for nh in range(2):
                                PE.op(lambda e, fs=fs, tt=tt, nh=nh, par=par: e.matmul(
                                    pb[tt * 2 + nh][:, 0:512],
                                    h1T[:, fs * 512 + tt * 128:
                                        fs * 512 + (tt + 1) * 128],
                                    w2bf4[par][:, nh * 512:(nh + 1) * 512],
                                    start=(fs == 0), stop=(fs == 32)),
                                    pe if (tt == 3 and nh == 1) else None)
                    V.wait(pe)
                    for tt in range(4):
                        ci = tg * 4 + tt
                        for nh in range(2):
                            V.op(lambda e, tt=tt, nh=nh, ci=ci, e_i=e_i:
                                 e.scalar_tensor_tensor(
                                     out=acc[:, tt * DM + nh * 512:
                                            tt * DM + (nh + 1) * 512],
                                     in0=pb[tt * 2 + nh][:, 0:512],
                                     scalar=gsel[:, ci * 8 + e_i:ci * 8 + e_i + 1],
                                     in1=acc[:, tt * DM + nh * 512:
                                            tt * DM + (nh + 1) * 512],
                                     op0=ALU.mult, op1=ALU.add), ve)
                if e_i == E - 1:
                    for tt in range(4):
                        ci = tg * 4 + tt
                        par = tt % 2
                        SY.wait(ve)
                        SY.op(lambda e, ci=ci, par=par: e.dma_start(
                            out=usb[par][:], in_=r1scr_d[ci * 128:(ci + 1) * 128, :]),
                            sy, 16)
                        V.wait(sy)
                        V.op(lambda e, tt=tt, par=par: e.tensor_tensor(
                            out=rln[:], in0=usb[par][:],
                            in1=acc[:, tt * DM:(tt + 1) * DM], op=ALU.add))
                        V.op(lambda e: e.tensor_reduce(
                            out=cols[:, 2:3], in_=rln[:], axis=AX.X, op=ALU.add), ve)
                        S.wait(ve)
                        S.op(lambda e: e.activation(
                            out=cols[:, 4:5], in_=cols[:, 2:3], func=AF.Identity,
                            scale=1.0 / DM), se)
                        V.wait(se)
                        V.op(lambda e: e.tensor_tensor(
                            out=xsq[:], in0=rln[:],
                            in1=cols[:, 4:5].broadcast_to([128, DM]),
                            op=ALU.subtract))
                        V.op(lambda e: e.tensor_tensor(
                            out=w1st[0][:], in0=xsq[:, 0:512], in1=xsq[:, 0:512],
                            op=ALU.mult))
                        V.op(lambda e: e.tensor_tensor(
                            out=w1st[1][:], in0=xsq[:, 512:1024],
                            in1=xsq[:, 512:1024], op=ALU.mult))
                        V.op(lambda e: e.tensor_reduce(
                            out=cols[:, 5:6], in_=w1st[0][:], axis=AX.X, op=ALU.add))
                        V.op(lambda e: e.tensor_reduce(
                            out=cols[:, 6:7], in_=w1st[1][:], axis=AX.X, op=ALU.add),
                            ve)
                        S.wait(ve)
                        S.op(lambda e: e.activation(
                            out=cols[:, 7:8], in_=cols[:, 5:6], func=AF.Identity,
                            bias=cols[:, 6:7]))
                        S.op(lambda e: e.activation(
                            out=cols[:, 10:11], in_=cols[:, 7:8], func=AF.Sqrt,
                            scale=1.0 / DM, bias=cns[:, 0:1]), se)
                        V.wait(se)
                        V.op(lambda e: e.reciprocal(out=cols[:, 11:12],
                                                    in_=cols[:, 10:11]), ve)
                        S.wait(ve)
                        S.op(lambda e: e.activation(
                            out=cols[:, 12:13], in_=cols[:, 11:12],
                            func=AF.Identity), se)
                        V.wait(se)
                        V.wait(so)
                        V.op(lambda e, par=par: e.tensor_tensor(
                            out=xo32[par][:], in0=xsq[:],
                            in1=cols[:, 12:13].broadcast_to([128, DM]),
                            op=ALU.mult), ve)
                        SY.wait(ve)
                        SY.op(lambda e, ci=ci, par=par: e.dma_start(
                            out=out_d[ci * 128:(ci + 1) * 128, :], in_=xo32[par][:]),
                            so, 16)
            SY.wait(so)

        if taps:
            SY.wait(ve); SY.wait(se); SY.wait(pe); SY.wait(gq, gq.count)
            for tname, sname, shape in taps:
                t = SBUF[sname]
                td = dt("tap_" + tname, list(shape), t.dtype,
                        kind="ExternalOutput").ap()
                SY.op(lambda e, t=t, td=td, shape=shape: e.dma_start(
                    out=td, in_=t[0:shape[0], 0:shape[1]]), so, 16)
            SY.wait(so)

        with nc.Block() as block:
            @block.sync
            def _(eng):
                SY.replay(eng)

            @block.gpsimd
            def _(eng):
                GQ.replay(eng)

            @block.tensor
            def _(eng):
                PE.replay(eng)

            @block.vector
            def _(eng):
                V.replay(eng)

            @block.scalar
            def _(eng):
                S.replay(eng)

    return nc


def _host_inputs(inputs):
    u = np.asarray(inputs["L0_attn_u"], np.float32)
    ipw = np.asarray(inputs["L0_attn_in_proj_w"], np.float32)
    perm = np.concatenate([
        np.arange(0, 2048), np.arange(2048, 4096), np.arange(4096, 4224),
        np.arange(4224, 4256), np.arange(5280, 5312), np.arange(4256, 5280)])
    winT = np.ascontiguousarray(ipw[perm].T)
    woutT = np.ascontiguousarray(
        np.asarray(inputs["L0_attn_out_proj_w"], np.float32).T)
    W1 = np.asarray(inputs["L0_ffn_W1"], np.float32)
    w1r = np.ascontiguousarray(
        W1.reshape(E, 8, 128, 32, 128).transpose(0, 3, 1, 2, 4))
    W2 = np.asarray(inputs["L0_ffn_W2"], np.float32)
    b2 = np.asarray(inputs["L0_ffn_b2"], np.float32)
    w2p = np.zeros((E, 33, 128, DM), np.float32)
    w2p[:, :32] = W2.reshape(E, 32, 128, DM)
    w2p[:, 32, 0, :] = b2
    b1 = np.asarray(inputs["L0_ffn_b1"], np.float32)
    b1c = np.ascontiguousarray(b1.reshape(E, 32, 128).transpose(0, 2, 1))
    Bb = np.asarray(inputs["L0_attn_B_bias"], np.float32)
    Cb = np.asarray(inputs["L0_attn_C_bias"], np.float32)

    def vint(bias):
        outs = np.zeros((2, 128, 1024), np.float32)
        m2g, kg = np.divmod(np.arange(128), 32)
        for half in range(2):
            for j in range(8):
                outs[half, :, 128 * j:128 * (j + 1)] = \
                    bias[4 * j + m2g, half * 32 + kg][:, None]
        return outs

    nwrep = np.ascontiguousarray(np.broadcast_to(
        np.asarray(inputs["L0_attn_norm_w"], np.float32)[None, :], (128, DI)))
    ii, jjx = np.meshgrid(np.arange(128), np.arange(128), indexing="ij")
    mneg = np.where(ii < jjx, -1e30, 0.0).astype(np.float32)
    kk = np.arange(128) % 32
    shared = dict(
        winT=winT, woutT=woutT,
        wr=np.ascontiguousarray(np.asarray(inputs["L0_ffn_Wr"], np.float32)),
        br=np.asarray(inputs["L0_ffn_br"], np.float32).reshape(E, 1),
        w1=w1r, b1c=b1c, w2=w2p,
        dtb=np.asarray(inputs["L0_attn_dt_bias"], np.float32).reshape(H, 1),
        alog=np.asarray(inputs["L0_attn_A_log"], np.float32).reshape(H, 1),
        dsk4=np.ascontiguousarray(np.broadcast_to(
            np.asarray(inputs["L0_attn_D"], np.float32).reshape(-1)[
                (4 * np.arange(8)[None, :] + np.arange(4)[:, None])][None, :, :],
            (128, 4, 8))),
        bnw=np.asarray(inputs["L0_attn_B_norm_w"], np.float32).reshape(N, 1),
        cnw=np.asarray(inputs["L0_attn_C_norm_w"], np.float32).reshape(N, 1),
        bbv=vint(Bb), cbv=vint(Cb), nwrep=nwrep,
        ident=np.eye(128, dtype=np.float32), mneg=mneg,
        repl1=(np.arange(N)[:, None] == kk[None, :]).astype(np.float32),
        repl2=(np.arange(N)[:, None] == (32 + kk)[None, :]).astype(np.float32),
        repl4=((np.arange(H)[:, None] % 4) ==
               (np.arange(128)[None, :] // 32)).astype(np.float32),
        selj8=((np.arange(H)[:, None] // 4) ==
               np.arange(8)[None, :]).astype(np.float32),
    )
    in_maps = []
    for core in range(8):
        d = dict(shared)
        d["u"] = np.ascontiguousarray(u[core])
        in_maps.append(d)
    return in_maps


def kernel(**inputs):
    in_maps = _host_inputs(inputs)
    nc = build_program()
    res = run_bass_kernel_spmd(nc, in_maps, core_ids=list(range(8)))
    out = np.stack([np.asarray(r["out"], np.float32) for r in res.results], axis=0)
    return out


# version 03
# speedup vs baseline: 1.6441x; 1.0176x over previous
import math
import numpy as np
from contextlib import ExitStack

import concourse.bass as bass
import concourse.mybir as mybir
from concourse.bass_utils import run_bass_kernel_spmd

F32 = mybir.dt.float32
BF16 = mybir.dt.bfloat16
AF = mybir.ActivationFunctionType
ALU = mybir.AluOpType
AX = mybir.AxisListType

L = 2048
DM = 1024
H = 32
P = 64
N = 64
DI = H * P
NCH = 16
Q = 128
DPROJ = 5312
E = 8
DFF = 4096
TWO_PI = 2.0 * math.pi
MAGIC = 1.5 * (2.0 ** 23)
PI = math.pi
PIC = math.pi * (1.0 - 1e-6)

ZC0, XC0, BCC0, DLC0, THC0 = 0, 2048, 4096, 4224, 4288


class Sem:
    def __init__(self, handle):
        self.h = handle
        self.count = 0


class Emitter:
    def __init__(self, name):
        self.name = name
        self.items = []
        self._waited = {}

    def wait(self, sem, val=None):
        v = sem.count if val is None else val
        if self._waited.get(id(sem), -1) >= v or v <= 0:
            return
        self._waited[id(sem)] = v
        self.items.append(lambda e, s=sem, v=v: e.wait_ge(s.h, v))

    def op(self, fn, inc=None, amt=1):
        if inc is not None:
            inc.count += amt
            def run(e, fn=fn, inc=inc, amt=amt):
                fn(e).then_inc(inc.h, amt)
        else:
            def run(e, fn=fn):
                fn(e)
        self.items.append(run)

    def replay(self, eng):
        for it in self.items:
            it(eng)


def build_program(upto=None, taps=()):
    nc = bass.Bass("TRN2", target_bir_lowering=False)
    dt = nc.dram_tensor

    u_d = dt("u", [L, DM], F32, kind="ExternalInput").ap()
    winT_d = dt("winT", [DM, DPROJ], F32, kind="ExternalInput").ap()
    woutT_d = dt("woutT", [DI, DM], F32, kind="ExternalInput").ap()
    wr_d = dt("wr", [DM, E], F32, kind="ExternalInput").ap()
    br_d = dt("br", [E, 1], F32, kind="ExternalInput").ap()
    w1_d = dt("w1", [E, 32, 8, 128, 128], F32, kind="ExternalInput").ap()
    b1_d = dt("b1c", [E, 128, 32], F32, kind="ExternalInput").ap()
    w2_d = dt("w2", [E, 33, 128, DM], F32, kind="ExternalInput").ap()
    dtb_d = dt("dtb", [H, 1], F32, kind="ExternalInput").ap()
    alog_d = dt("alog", [H, 1], F32, kind="ExternalInput").ap()
    dsk_d = dt("dsk4", [128, 4, 8], F32, kind="ExternalInput").ap()
    bnw_d = dt("bnw", [N, 1], F32, kind="ExternalInput").ap()
    cnw_d = dt("cnw", [N, 1], F32, kind="ExternalInput").ap()
    bbv_d = dt("bbv", [2, 128, 1024], F32, kind="ExternalInput").ap()
    cbv_d = dt("cbv", [2, 128, 1024], F32, kind="ExternalInput").ap()
    nwrep_d = dt("nwrep", [128, DI], F32, kind="ExternalInput").ap()
    ident_d = dt("ident", [128, 128], F32, kind="ExternalInput").ap()
    mneg_d = dt("mneg", [128, 128], F32, kind="ExternalInput").ap()
    repl1_d = dt("repl1", [N, 128], F32, kind="ExternalInput").ap()
    repl2_d = dt("repl2", [N, 128], F32, kind="ExternalInput").ap()
    repl4_d = dt("repl4", [H, 128], F32, kind="ExternalInput").ap()
    selj8_d = dt("selj8", [H, 8], F32, kind="ExternalInput").ap()
    out_d = dt("out", [L, DM], F32, kind="ExternalOutput").ap()

    winTbf_d = dt("winTbf", [DM, DPROJ], BF16, kind="Internal").ap()
    woutTbf_d = dt("woutTbf", [DI, DM], BF16, kind="Internal").ap()
    r1scr_d = dt("r1scr", [L, DM], F32, kind="Internal").ap()
    w1bf_d = dt("w1bfd", [E, 32, 128, 1024], BF16, kind="Internal").ap()
    w2bf_d = dt("w2bfd", [E, 33, 128, DM], BF16, kind="Internal").ap()


    st = ExitStack()
    SBUF = {}

    def sb(n, sh, d=F32):
        t = st.enter_context(nc.sbuf_tensor(n, sh, d))
        SBUF[n] = t
        return t

    ps = lambda n, sh: st.enter_context(nc.psum_tensor(n, sh, F32))
    mksem = lambda n: Sem(st.enter_context(nc.semaphore(n)))

    with st:
        xT = sb("xT", [128, 8 * L], BF16)
        acc = sb("acc", [128, 4 * DM], BF16)
        h1T = sb("h1T", [128, 33 * 512], BF16)
        zsb = h1T[:, 0:4096]
        Xsb = h1T[:, 4096:8192]
        tcV = h1T[:, 8192:12288].bitcast(F32)
        bpp = h1T[:, 12288:14336]
        sil = h1T[:, 14336:16384]
        bcT = sb("bcT", [128, 2 * 128], F32)
        dtT = sb("dtT", [32, 2 * 128], F32)
        lamT = sb("lamT", [32, 2 * 128], F32)
        g_t = sb("g_t", [32, 2 * 128], F32)
        q_t = sb("q_t", [32, 2 * 128], F32)
        wgb = [sb(f"wgb{i}", [128, 8 * 512], BF16) for i in range(2)]
        wob = [sb(f"wob{i}", [128, DM], BF16) for i in range(2)]
        usb = [sb(f"usb{i}", [128, DM], F32) for i in range(2)]

        ghat = sb("ghat", [32, 128], F32)
        s_t = sb("s_t", [32, 128], F32)
        S_T = sb("S_T", [32, 128], F32)
        lng = sb("lng", [32, 128], F32)
        wT = sb("wT", [32, 128], F32)
        expd = sb("expd", [32, 128], F32)
        pack = sb("pack", [128, 128], F32)
        packN = sb("packN", [128, 128], F32)
        E_nat = sb("E_nat", [128, 32], F32)
        dec_c = sb("dec_c", [32, 1], F32)
        dec4i = sb("dec4i", [32, 8], F32)
        dec4 = sb("dec4", [128, 8], F32)
        wvsel = sb("wvsel", [32, 8 * 128], BF16)
        state = sb("state", [128, 1024], F32)
        state_bf = sb("state_bf", [128, 1024], BF16)
        sqT = sb("sqT", [128, 128], F32)
        invB = sb("invB", [1, 128], F32)
        invC = sb("invC", [1, 128], F32)
        brawn = sb("brawn", [64, 128], BF16)
        brawn8 = sb("brawn8", [64, 8 * 128], BF16)
        crawn = sb("crawn", [64, 128], BF16)
        crawn8 = sb("crawn8", [64, 8 * 128], BF16)
        vB1 = sb("vB1", [128, 1024], BF16)
        vB2 = sb("vB2", [128, 1024], BF16)
        vC1 = sb("vC1", [128, 1024], BF16)
        vC2 = sb("vC2", [128, 1024], BF16)
        rta = sb("rta", [128, 1024], BF16)
        rtb = sb("rtb", [128, 1024], BF16)
        vBW1 = rta
        vBW2 = rtb
        rmod = sb("rmod", [128, 1024], F32)
        cosV = sb("cosV", [128, 1024], BF16)
        sinV = sb("sinV", [128, 1024], BF16)
        pre_t = [sb(f"pre{i}", [128, 128], F32) for i in range(2)]
        D2 = [sb(f"D2_{i}", [128, 128], F32) for i in range(2)]
        G_bf = [sb(f"G_{i}", [128, 128], BF16) for i in range(2)]
        ysc = sb("ysc", [128, 512], F32)
        yg = sb("yg", [128, DI], BF16)
        I32 = sb("I32", [128, 128], F32)
        Ibf = sb("Ibf", [128, 128], BF16)
        Mneg = sb("Mneg", [128, 128], F32)
        rp1 = sb("rp1", [64, 128], F32)
        rp2 = sb("rp2", [64, 128], F32)
        repl1w = sb("repl1w", [64, 128], BF16)
        repl2w = sb("repl2w", [64, 128], BF16)
        repl1wc = sb("repl1wc", [64, 128], BF16)
        repl2wc = sb("repl2wc", [64, 128], BF16)
        repl4s = sb("repl4s", [32, 128], F32)
        repl4bf = sb("repl4bf", [32, 128], BF16)
        selj8s = sb("selj8s", [32, 8], F32)
        bbv_s = sb("bbv_s", [128, 2 * 1024], BF16)
        cbv_s = sb("cbv_s", [128, 2 * 1024], BF16)
        nwrep_s = sb("nwrep_s", [128, DI], BF16)
        onec = sb("onec", [128, 128], F32)
        a_col = sb("a_col", [32, 1], F32)
        acol_t = sb("acol_t", [32, 1], F32)
        dtb_s = sb("dtb_s", [32, 1], F32)
        dsk_s = sb("dsk_s", [128, 32], F32)
        bnw_s = sb("bnw_s", [64, 1], F32)
        cnw_s = sb("cnw_s", [64, 1], F32)
        wr_s = sb("wr_s", [128, 8 * 8], F32)
        br_s = sb("br_s", [8, 1], F32)
        zcol = sb("zcol", [32, 1], F32)
        rln = sb("rln", [128, DM], F32)
        xln = sb("xln", [128, DM], F32)
        carg = rln
        atmp = xln
        xsq = rmod
        cst32 = [rln, xln]
        cstbf_t = None
        xsq = sb("xsq", [128, DM], F32)
        xt32 = sb("xt32", [128, 128], F32)
        logT = sb("logT", [8, 128], F32)
        cols = sb("cols", [128, 16], F32)
        cns = sb("cns", [128, 6], F32)
        gsel = sb("gsel", [128, 8 * 16], F32)
        w1st = [sb(f"w1st{i}", [128, 512], F32) for i in range(4)]
        w1bf = [sb(f"w1bf{i}", [128, 1024], BF16) for i in range(2)]
        w1bf4 = [sb(f"w1bf4_{i}", [128, 1024], BF16) for i in range(4)]
        w2bf4 = [w1bf4[0], w1bf4[1], w1bf4[2], w1bf4[3]]
        b1c_s = [sb(f"b1c{i}", [128, 32], F32) for i in range(2)]
        xo32 = [xln, xln]

        pb = [ps(f"pb{i}", [128, 512]) for i in range(8)]

        for _n, _a in (("zsb", zsb), ("Xsb", Xsb), ("tcV", tcV),
                       ("bpp", bpp), ("sil", sil)):
            SBUF[_n] = _a
        sy = mksem("sy")
        sr = mksem("sr")
        gq = mksem("gq")
        pe = mksem("pe")
        ve = mksem("ve")
        se = mksem("se")
        so = mksem("so")

        SY = Emitter("sync"); GQ = Emitter("gpsimd"); PE = Emitter("tensor")
        V = Emitter("vector"); S = Emitter("scalar")

        for a, b in [(I32, ident_d), (Mneg, mneg_d), (repl4s, repl4_d),
                     (selj8s, selj8_d), (a_col, alog_d), (dtb_s, dtb_d),
                     (dsk_s, dsk_d.rearrange("p a b -> p (a b)")), (bnw_s, bnw_d), (cnw_s, cnw_d),
                     (br_s, br_d), (rp1, repl1_d),
                     (rp2, repl2_d)]:
            SY.op(lambda e, a=a, b=b: e.dma_start(out=a[:], in_=b), sy, 16)
        SY.op(lambda e: e.dma_start(out=wr_s[:].rearrange("p (k c) -> p k c", k=8),
                                    in_=wr_d.rearrange("(k p) c -> p k c", p=128)), sy, 16)

        V.wait(sy)
        V.op(lambda e: e.memset(onec[:], 1.0))
        V.op(lambda e: e.tensor_copy(out=Ibf[:], in_=I32[:]))
        V.op(lambda e: e.tensor_copy(out=repl4bf[:], in_=repl4s[:]))
        V.op(lambda e: e.tensor_scalar(out=repl1w[:], in0=rp1[:], scalar1=bnw_s[:],
                                       scalar2=None, op0=ALU.mult))
        V.op(lambda e: e.tensor_scalar(out=repl2w[:], in0=rp2[:], scalar1=bnw_s[:],
                                       scalar2=None, op0=ALU.mult))
        V.op(lambda e: e.tensor_scalar(out=repl1wc[:], in0=rp1[:], scalar1=cnw_s[:],
                                       scalar2=None, op0=ALU.mult))
        V.op(lambda e: e.tensor_scalar(out=repl2wc[:], in0=rp2[:], scalar1=cnw_s[:],
                                       scalar2=None, op0=ALU.mult))
        V.op(lambda e: e.memset(zcol[:], 0.0))
        V.op(lambda e: e.memset(cns[:, 0:1], 1e-5))
        V.op(lambda e: e.memset(cns[:, 1:2], 1.0))
        V.op(lambda e: e.memset(cns[:, 2:3], 1.0 / DI))
        V.op(lambda e: e.memset(cns[:, 3:4], 1.0 / DM))
        V.op(lambda e: e.memset(cns[:, 4:5], -1.0))
        V.op(lambda e: e.memset(state[:], 0.0))
        V.op(lambda e: e.memset(state_bf[:], 0.0))
        V.op(lambda e: e.memset(h1T[:, 32 * 512:33 * 512], 0.0))
        V.op(lambda e: e.memset(h1T[0:1, 32 * 512:32 * 512 + 512], 1.0), ve)
        S.wait(sy)
        S.op(lambda e: e.activation(out=acol_t[:], in_=a_col[:], func=AF.Exp), se)
        V.wait(se)
        V.op(lambda e: e.tensor_scalar(out=a_col[:], in0=acol_t[:], scalar1=-1.0,
                                       scalar2=None, op0=ALU.mult), ve)

        cast_jobs = []
        wpieces = [(i * 1024, 1024) for i in range(5)] + [(5120, 192)]
        for k in range(8):
            for s0, w in wpieces:
                cast_jobs.append((winT_d[k * 128:(k + 1) * 128, s0:s0 + w],
                                  winTbf_d[k * 128:(k + 1) * 128, s0:s0 + w],
                                  None, w))
        for k in range(16):
            cast_jobs.append((woutT_d[k * 128:(k + 1) * 128, :],
                              woutTbf_d[k * 128:(k + 1) * 128, :], None, DM))
        for half in range(2):
            cast_jobs.append((bbv_d[half], None,
                              bbv_s[:, half * 1024:(half + 1) * 1024], 1024))
            cast_jobs.append((cbv_d[half], None,
                              cbv_s[:, half * 1024:(half + 1) * 1024], 1024))
            cast_jobs.append((nwrep_d[:, half * 1024:(half + 1) * 1024], None,
                              nwrep_s[:, half * 1024:(half + 1) * 1024], 1024))

        def emit_cast_job(srcap, dstap, sdst, w, par):
            GQ.wait(ve)
            GQ.op(lambda e, s=srcap, w=w, par=par: e.dma_start(
                out=cst32[par][:, 0:w], in_=s), gq, 16)
            V.wait(gq)
            if sdst is None:
                V.op(lambda e, w=w, par=par: e.tensor_copy(
                    out=yg[:, par * 1024:par * 1024 + w],
                    in_=cst32[par][:, 0:w]), ve)
                GQ.wait(ve)
                GQ.op(lambda e, d=dstap, w=w, par=par: e.dma_start(
                    out=d, in_=yg[:, par * 1024:par * 1024 + w]), gq, 16)
            else:
                V.op(lambda e, sdst=sdst, w=w, par=par: e.tensor_copy(
                    out=sdst, in_=cst32[par][:, 0:w]), ve)

        for idx, job in enumerate(cast_jobs):
            emit_cast_job(*job, idx % 2)
        scratch_done = gq.count

        wcast_jobs = []
        for e_i in range(E):
            for ft in range(32):
                wcast_jobs.append((
                    ("w1", w1_d[e_i, ft].transpose([1, 0, 2])),
                    w1bf_d[e_i, ft], 1024))
            for fs in range(33):
                wcast_jobs.append((("w2", w2_d[e_i, fs]), w2bf_d[e_i, fs], 1024))
        wcast_pos = [0]

        def emit_wcasts(count):
            for _ in range(count):
                if wcast_pos[0] >= len(wcast_jobs):
                    return
                (kind, srcap), dstap, w = wcast_jobs[wcast_pos[0]]
                par = wcast_pos[0] % 2
                wcast_pos[0] += 1
                if kind == "w1":
                    h0 = srcap[:, 0:4, :]
                    h1 = srcap[:, 4:8, :]
                else:
                    h0 = srcap[:, 0:512]
                    h1 = srcap[:, 512:1024]
                GQ.wait(ve)
                GQ.op(lambda e, s=h0, par=par: e.dma_start(
                    out=w1st[2 * par][:], in_=s), gq, 16)
                GQ.op(lambda e, s=h1, par=par: e.dma_start(
                    out=w1st[2 * par + 1][:], in_=s), gq, 16)
                V.wait(gq)
                V.op(lambda e, par=par: e.tensor_copy(
                    out=w1bf[par][:, 0:512], in_=w1st[2 * par][:]))
                V.op(lambda e, par=par: e.tensor_copy(
                    out=w1bf[par][:, 512:1024], in_=w1st[2 * par + 1][:]), ve)
                GQ.wait(ve)
                GQ.op(lambda e, d=dstap, par=par: e.dma_start(
                    out=d, in_=w1bf[par][:]), gq, 16)

        pe_tile_end = {}
        for t in range(16):
            par = t % 2
            if t >= 2:
                SY.wait(pe, pe_tile_end[t - 2])
            SY.op(lambda e, t=t, par=par: e.dma_start(
                out=usb[par][:], in_=u_d[t * 128:(t + 1) * 128, :]), sy, 16)
            PE.wait(sy)
            for k in range(8):
                bank = pb[k % 2]
                PE.wait(ve, ve.count - 1)
                PE.op(lambda e, k=k, par=par, bank=bank: e.transpose(
                    bank[:, 0:128], usb[par][:, k * 128:(k + 1) * 128], I32[:]), pe)
                V.wait(pe)
                V.op(lambda e, k=k, t=t, bank=bank: e.tensor_copy(
                    out=xT[:, k * L + t * 128: k * L + (t + 1) * 128],
                    in_=bank[:, 0:128]), ve)
            pe_tile_end[t] = pe.count

        PE.wait(gq, scratch_done)
        GQ.wait(gq, scratch_done)

        nat_groups = [(ZC0 + 512 * i, 512) for i in range(4)] + \
                     [(XC0 + 512 * i, 512) for i in range(4)]
        tt_groups = [(BCC0, 128), (DLC0, 64)] + [(THC0 + 128 * j, 128) for j in range(8)]

        def proj_stage(c):
            par = c % 2
            xcols = slice(c * 128, (c + 1) * 128)
            for gi, (c0, w) in enumerate(nat_groups + tt_groups):
                is_nat = gi < len(nat_groups)
                bank = pb[gi % 2]
                buf = wgb[gi % 2]
                QE, qs = (SY, sy) if gi % 2 == 0 else (GQ, gq)
                QE.wait(pe)
                QE.op(lambda e, c0=c0, w=w, buf=buf: e.dma_start(
                    out=buf[:, 0:8 * w].rearrange("p (k c) -> p k c", k=8),
                    in_=winTbf_d[:, c0:c0 + w].rearrange("(k p) c -> p k c", p=128)),
                    qs, 16)
                PE.wait(qs)
                PE.wait(ve)
                for k in range(8):
                    if is_nat:
                        PE.op(lambda e, k=k, w=w, buf=buf, bank=bank, xcols=xcols:
                              e.matmul(bank[:, 0:w],
                                       xT[:, k * L:(k + 1) * L][:, xcols],
                                       buf[:, k * w:(k + 1) * w],
                                       start=(k == 0), stop=(k == 7)),
                              pe if k == 7 else None)
                    else:
                        PE.op(lambda e, k=k, w=w, buf=buf, bank=bank, xcols=xcols:
                              e.matmul(bank[0:w, 0:128],
                                       buf[:, k * w:(k + 1) * w],
                                       xT[:, k * L:(k + 1) * L][:, xcols],
                                       start=(k == 0), stop=(k == 7)),
                              pe if k == 7 else None)
                V.wait(pe)
                if is_nat:
                    if c0 < XC0:
                        off = c0 - ZC0
                        V.op(lambda e, off=off, w=w, bank=bank, par=par: e.tensor_copy(
                            out=zsb[:, par * DI + off: par * DI + off + w],
                            in_=bank[:, 0:w]), ve)
                    else:
                        off = c0 - XC0
                        V.op(lambda e, off=off, w=w, bank=bank, par=par: e.tensor_copy(
                            out=Xsb[:, par * DI + off: par * DI + off + w],
                            in_=bank[:, 0:w]), ve)
                elif c0 == BCC0:
                    V.op(lambda e, bank=bank, par=par: e.tensor_copy(
                        out=bcT[:, par * 128:(par + 1) * 128], in_=bank[:, 0:128]), ve)
                elif c0 == DLC0:
                    V.op(lambda e, bank=bank, par=par: e.tensor_copy(
                        out=dtT[:, par * 128:(par + 1) * 128], in_=bank[0:32, 0:128]))
                    V.op(lambda e, bank=bank, par=par: e.tensor_copy(
                        out=lamT[:, par * 128:(par + 1) * 128], in_=bank[32:64, 0:128]),
                        ve)
                else:
                    j = (c0 - THC0) // 128
                    if c == 0:
                        ini = 0.0
                    else:
                        prev = (c - 1) % 2
                        ini = tcV[:, prev * 1024 + j * 128 + 127:
                                  prev * 1024 + j * 128 + 128]
                    V.op(lambda e, bank=bank, par=par, j=j, ini=ini:
                         e.tensor_tensor_scan(
                             out=tcV[:, par * 1024 + j * 128: par * 1024 + (j + 1) * 128],
                             data0=onec[:], data1=bank[:, 0:128], initial=ini,
                             op0=ALU.mult, op1=ALU.add), ve)
            S.wait(ve)
            S.op(lambda e, par=par: e.activation(
                out=s_t[:], in_=dtT[:, par * 128:(par + 1) * 128], func=AF.Exp,
                bias=dtb_s[:]))
            S.op(lambda e, par=par: e.activation(
                out=dtT[:, par * 128:(par + 1) * 128], in_=s_t[:], func=AF.Ln,
                bias=cns[0:32, 1:2]))
            S.op(lambda e, par=par: e.activation(
                out=lamT[:, par * 128:(par + 1) * 128],
                in_=lamT[:, par * 128:(par + 1) * 128], func=AF.Sigmoid), se)
            V.wait(se)
            V.op(lambda e, par=par: e.tensor_tensor(
                out=g_t[:, par * 128:(par + 1) * 128],
                in0=lamT[:, par * 128:(par + 1) * 128],
                in1=dtT[:, par * 128:(par + 1) * 128], op=ALU.mult))
            V.op(lambda e, par=par: e.tensor_tensor(
                out=q_t[:, par * 128:(par + 1) * 128],
                in0=dtT[:, par * 128:(par + 1) * 128],
                in1=g_t[:, par * 128:(par + 1) * 128], op=ALU.subtract), ve)

        def ssd_stage(c):
            par = c % 2
            npar = (c + 1) % 2
            S.wait(ve)
            S.op(lambda e, par=par: e.activation(
                out=sil, in_=zsb[:, par * DI:(par + 1) * DI], func=AF.Silu), se)
            V.op(lambda e, par=par: e.tensor_tensor(
                out=ghat[:, 0:127], in0=g_t[:, par * 128:par * 128 + 127],
                in1=q_t[:, par * 128 + 1:par * 128 + 128], op=ALU.add))
            qn = zcol[:] if c == NCH - 1 else q_t[:, npar * 128:npar * 128 + 1]
            V.op(lambda e, qn=qn, par=par: e.tensor_tensor(
                out=ghat[:, 127:128], in0=g_t[:, par * 128 + 127:par * 128 + 128],
                in1=qn, op=ALU.add))
            V.op(lambda e, par=par: e.tensor_scalar(
                out=s_t[:], in0=dtT[:, par * 128:(par + 1) * 128], scalar1=a_col[:],
                scalar2=None, op0=ALU.mult))
            V.op(lambda e: e.tensor_tensor_scan(
                out=S_T[:], data0=onec[0:32, :], data1=s_t[:], initial=0.0,
                op0=ALU.mult, op1=ALU.add), ve)
            S.wait(ve)
            S.op(lambda e: e.activation(out=lng[:], in_=ghat[:], func=AF.Ln))
            S.op(lambda e: e.activation(out=expd[:], in_=S_T[:], func=AF.Exp,
                                        bias=S_T[:, 127:128], scale=-1.0))
            S.op(lambda e: e.activation(out=dec_c[:], in_=S_T[:, 127:128],
                                        func=AF.Exp), se)
            V.wait(se)
            V.op(lambda e: e.tensor_tensor(out=pack[0:32, :], in0=lng[:], in1=S_T[:],
                                           op=ALU.subtract))
            V.op(lambda e, par=par: e.tensor_tensor(
                out=pack[32:64, :], in0=g_t[:, par * 128:(par + 1) * 128],
                in1=ghat[:], op=ALU.subtract))
            V.op(lambda e: e.tensor_copy(out=pack[64:96, :], in_=S_T[:]))
            V.op(lambda e: e.tensor_tensor(out=wT[:], in0=ghat[:], in1=expd[:],
                                           op=ALU.mult))
            V.op(lambda e: e.tensor_scalar(out=dec4i[:], in0=selj8s[:],
                                           scalar1=dec_c[:], scalar2=None,
                                           op0=ALU.mult), ve)
            PE.wait(ve)
            PE.op(lambda e: e.transpose(pb[2][:, 0:128], pack[:], I32[:]), pe)
            PE.op(lambda e: e.matmul(pb[3][:, 0:8], repl4s[:], dec4i[:],
                                     start=True, stop=True), pe)
            V.wait(pe)
            V.op(lambda e: e.tensor_copy(out=packN[:], in_=pb[2][:, 0:128]))
            V.op(lambda e: e.tensor_copy(out=dec4[:], in_=pb[3][:, 0:8]), ve)
            S.wait(ve)
            S.op(lambda e: e.activation(out=E_nat[:], in_=packN[:, 64:96],
                                        func=AF.Exp), se)

            tcs = tcV[:, par * 1024:(par + 1) * 1024]
            V.op(lambda e, tcs=tcs: e.tensor_scalar(
                out=atmp[:], in0=tcs, scalar1=1.0 / TWO_PI, scalar2=MAGIC,
                op0=ALU.mult, op1=ALU.add))
            V.op(lambda e: e.tensor_scalar(
                out=atmp[:], in0=atmp[:], scalar1=MAGIC, scalar2=None,
                op0=ALU.subtract))
            V.op(lambda e, tcs=tcs: e.scalar_tensor_tensor(
                out=rmod[:], in0=atmp[:], scalar=-TWO_PI, in1=tcs,
                op0=ALU.mult, op1=ALU.add))
            V.op(lambda e: e.tensor_scalar(
                out=rmod[:], in0=rmod[:], scalar1=-PIC, scalar2=PIC,
                op0=ALU.max, op1=ALU.min))
            V.op(lambda e: e.tensor_scalar(
                out=atmp[:], in0=rmod[:], scalar1=PI / 2, scalar2=None, op0=ALU.is_gt))
            V.op(lambda e: e.tensor_scalar(
                out=carg[:], in0=rmod[:], scalar1=PI / 2, scalar2=None, op0=ALU.add))
            V.op(lambda e: e.scalar_tensor_tensor(
                out=carg[:], in0=atmp[:], scalar=-TWO_PI, in1=carg[:],
                op0=ALU.mult, op1=ALU.add))
            V.op(lambda e: e.tensor_scalar(
                out=carg[:], in0=carg[:], scalar1=-PIC, scalar2=PIC,
                op0=ALU.max, op1=ALU.min), ve)
            S.wait(ve)
            S.op(lambda e: e.activation(out=rmod[:], in_=rmod[:], func=AF.Sin))
            S.op(lambda e: e.activation(out=carg[:], in_=carg[:], func=AF.Sin), se)
            V.wait(se)
            V.op(lambda e: e.tensor_copy(out=sinV[:], in_=rmod[:]))
            V.op(lambda e: e.tensor_copy(out=cosV[:], in_=carg[:]), ve)

            bct = bcT[:, par * 128:(par + 1) * 128]
            V.op(lambda e, bct=bct: e.tensor_tensor(out=sqT[:], in0=bct, in1=bct,
                                                    op=ALU.mult), ve)
            PE.wait(ve)
            PE.op(lambda e: e.matmul(pb[2][0:1, 0:128], onec[0:64, 0:1], sqT[0:64, :],
                                     start=True, stop=True), pe)
            PE.op(lambda e: e.matmul(pb[3][0:1, 0:128], onec[64:128, 0:1],
                                     sqT[64:128, :], start=True, stop=True,
                                     tile_position=(64, 0)), pe)
            S.wait(pe)
            S.op(lambda e: e.activation(out=invB[:], in_=pb[2][0:1, 0:128],
                                        func=AF.Sqrt, scale=1.0 / 64.0, bias=cns[0:1, 0:1]))
            S.op(lambda e: e.activation(out=invC[:], in_=pb[3][0:1, 0:128],
                                        func=AF.Sqrt, scale=1.0 / 64.0, bias=cns[0:1, 0:1]), se)
            V.wait(se)
            V.op(lambda e: e.reciprocal(out=invB[:], in_=invB[:]))
            V.op(lambda e: e.reciprocal(out=invC[:], in_=invC[:]), ve)
            PE.wait(ve)
            PE.op(lambda e: e.matmul(pb[2][0:64, 0:128], onec[0:1, 0:64], invB[:],
                                     start=True, stop=True), pe)
            PE.op(lambda e: e.matmul(pb[3][0:64, 0:128], onec[0:1, 0:64], invC[:],
                                     start=True, stop=True), pe)
            V.wait(pe)
            V.op(lambda e, bct=bct: e.tensor_tensor(out=brawn[:], in0=bct[0:64, :],
                                                    in1=pb[2][0:64, 0:128],
                                                    op=ALU.mult))
            V.op(lambda e, bct=bct: e.tensor_tensor(out=crawn[:], in0=bct[64:128, :],
                                                    in1=pb[3][0:64, 0:128],
                                                    op=ALU.mult))
            V.op(lambda e: e.tensor_copy(
                out=brawn8[:].rearrange("p (r c) -> p r c", r=8),
                in_=brawn[:].unsqueeze(1).broadcast_to([64, 8, 128])))
            V.op(lambda e: e.tensor_copy(
                out=crawn8[:].rearrange("p (r c) -> p r c", r=8),
                in_=crawn[:].unsqueeze(1).broadcast_to([64, 8, 128])), ve)
            PE.wait(ve)
            for half in range(2):
                rw = repl1w if half == 0 else repl2w
                rwc = repl1wc if half == 0 else repl2wc
                dst_b = vB1 if half == 0 else vB2
                dst_c = vC1 if half == 0 else vC2
                for piece in range(2):
                    PE.wait(ve)
                    PE.op(lambda e, rw=rw, piece=piece: e.matmul(
                        pb[2][:, 0:512], rw[:],
                        brawn8[:, piece * 512:(piece + 1) * 512],
                        start=True, stop=True), pe)
                    PE.op(lambda e, rwc=rwc, piece=piece: e.matmul(
                        pb[3][:, 0:512], rwc[:],
                        crawn8[:, piece * 512:(piece + 1) * 512],
                        start=True, stop=True), pe)
                    V.wait(pe)
                    V.op(lambda e, dst_b=dst_b, piece=piece, half=half: e.tensor_tensor(
                        out=dst_b[:, piece * 512:(piece + 1) * 512],
                        in0=bbv_s[:, half * 1024 + piece * 512:
                                  half * 1024 + (piece + 1) * 512],
                        in1=pb[2][:, 0:512], op=ALU.add))
                    V.op(lambda e, dst_c=dst_c, piece=piece, half=half: e.tensor_tensor(
                        out=dst_c[:, piece * 512:(piece + 1) * 512],
                        in0=cbv_s[:, half * 1024 + piece * 512:
                                  half * 1024 + (piece + 1) * 512],
                        in1=pb[3][:, 0:512], op=ALU.add), ve)
            V.wait(se)
            V.wait(pe)
            for (v1, v2) in ((vB1, vB2), (vC1, vC2)):
                V.op(lambda e, v1=v1: e.tensor_tensor(out=rta[:], in0=v1[:],
                                                      in1=cosV[:], op=ALU.mult))
                V.op(lambda e, v2=v2: e.tensor_tensor(out=rtb[:], in0=v2[:],
                                                      in1=sinV[:], op=ALU.mult))
                V.op(lambda e, v2=v2: e.tensor_tensor(out=v2[:], in0=v2[:],
                                                      in1=cosV[:], op=ALU.mult))
                V.op(lambda e, v1=v1: e.tensor_tensor(out=v1[:], in0=v1[:],
                                                      in1=sinV[:], op=ALU.mult))
                V.op(lambda e, v1=v1, v2=v2: e.tensor_tensor(out=v2[:], in0=v2[:],
                                                             in1=v1[:], op=ALU.add))
                V.op(lambda e, v1=v1: e.tensor_tensor(out=v1[:], in0=rta[:],
                                                      in1=rtb[:], op=ALU.subtract), ve)
            V.op(lambda e: e.tensor_tensor(
                out=wvsel[:].rearrange("p (r c) -> p r c", r=8),
                in0=wT[:].unsqueeze(1).broadcast_to([32, 8, 128]),
                in1=selj8s[:].unsqueeze(2).broadcast_to([32, 8, 128]),
                op=ALU.mult), ve)
            PE.wait(ve)
            for piece in range(2):
                PE.op(lambda e, piece=piece: e.matmul(
                    pb[2 + piece][:, 0:512], repl4bf[:],
                    wvsel[:, piece * 512:(piece + 1) * 512],
                    start=True, stop=True), pe)
            V.wait(pe)
            for piece in range(2):
                V.op(lambda e, piece=piece: e.tensor_tensor(
                    out=vBW1[:, piece * 512:(piece + 1) * 512],
                    in0=vB1[:, piece * 512:(piece + 1) * 512],
                    in1=pb[2 + piece][:, 0:512], op=ALU.mult))
                V.op(lambda e, piece=piece: e.tensor_tensor(
                    out=vBW2[:, piece * 512:(piece + 1) * 512],
                    in0=vB2[:, piece * 512:(piece + 1) * 512],
                    in1=pb[2 + piece][:, 0:512], op=ALU.mult), ve)
            for j in range(8):
                for half in range(2):
                    src = vBW1 if half == 0 else vBW2
                    bank = pb[2 + ((j * 2 + half) % 2)]
                    PE.wait(ve)
                    PE.op(lambda e, src=src, j=j, bank=bank: e.matmul(
                        bank[:, 0:128], src[:, j * 128:(j + 1) * 128], Ibf[:],
                        start=True, stop=True), pe)
                    V.wait(pe)
                    V.op(lambda e, j=j, half=half, bank=bank: e.tensor_copy(
                        out=bpp.rearrange("p (jj mm n) -> p jj mm n", jj=8, mm=4)
                            [:, j, :, half * 32:half * 32 + 32],
                        in_=bank[:, 0:128].rearrange("p (m k) -> p m k", m=4)), ve)

            cap_tr = [0, 0]; cap_d2 = [0, 0]; cap_g = [0, 0]
            cap_cb = [0, 0]; cap_yx = [0, 0]; cap_ysc = [0, 0]
            cap_d3v = [0, 0]; cap_ys = [0, 0]
            cap_prev_state = ve.count
            cap_ygw = ve.count
            for m in range(4):
                pbYx = pb[4 + (m % 2)]
                pbYs = pb[6 + (m % 2)]
                pm = slice(32 * m, 32 * m + 32)
                for jj in range(8):
                    h = 4 * jj + m
                    j = jj
                    dpar = jj % 2
                    cbank = pb[jj % 2]
                    V.wait(pe, cap_tr[dpar])
                    V.op(lambda e, h=h, dpar=dpar: e.tensor_tensor(
                        out=pre_t[dpar][:], in0=Mneg[:],
                        in1=packN[:, 64 + h:65 + h].broadcast_to([128, 128]),
                        op=ALU.add), ve)
                    cap_prev = ve.count
                    PE.wait(ve, cap_prev)
                    PE.wait(se, cap_d2[dpar])
                    PE.op(lambda e, dpar=dpar: e.transpose(
                        pb[2 + dpar][:, 0:128], pre_t[dpar][:], I32[:]), pe)
                    cap_tr[dpar] = pe.count
                    PE.wait(ve, cap_g[jj % 2])
                    PE.op(lambda e, pm=pm, j=j, cbank=cbank, m=m: e.matmul(
                        cbank[:, 0:128], vB1[pm, j * 128:(j + 1) * 128],
                        vC1[pm, j * 128:(j + 1) * 128], start=True, stop=False,
                        tile_position=(32 * m, 0)))
                    PE.op(lambda e, pm=pm, j=j, cbank=cbank, m=m: e.matmul(
                        cbank[:, 0:128], vB2[pm, j * 128:(j + 1) * 128],
                        vC2[pm, j * 128:(j + 1) * 128], start=False, stop=True,
                        tile_position=(32 * m, 0)), pe)
                    cap_cb[dpar] = pe.count
                    S.wait(pe, cap_tr[dpar])
                    S.wait(ve, cap_d3v[dpar])
                    S.op(lambda e, dpar=dpar, h=h: e.activation(
                        out=D2[dpar][:], in_=pb[2 + dpar][:, 0:128], func=AF.Exp,
                        bias=packN[:, h:h + 1]), se)
                    cap_d2[dpar] = se.count
                    V.wait(se, cap_d2[dpar])
                    V.op(lambda e, dpar=dpar, h=h: e.scalar_tensor_tensor(
                        out=D2[dpar][:], in0=I32[:], scalar=packN[:, 32 + h:33 + h],
                        in1=D2[dpar][:], op0=ALU.mult, op1=ALU.add))
                    V.wait(pe, cap_cb[dpar])
                    V.wait(pe, cap_yx[dpar])
                    V.op(lambda e, dpar=dpar, cbank=cbank: e.tensor_tensor(
                        out=G_bf[dpar][:], in0=D2[dpar][:], in1=cbank[:, 0:128],
                        op=ALU.mult), ve)
                    cap_g[dpar] = ve.count
                    cap_d3v[dpar] = ve.count
                    PE.wait(ve, max(cap_g[dpar], cap_prev_state))
                    if jj == 0:
                        PE.wait(se, cap_ysc[1])
                    PE.op(lambda e, dpar=dpar, h=h, jj=jj, pbYx=pbYx, par=par:
                          e.matmul(pbYx[:, jj * 64:(jj + 1) * 64], G_bf[dpar][:],
                                   Xsb[:, par * DI + h * 64: par * DI + (h + 1) * 64],
                                   start=True, stop=True, skip_group_check=True),
                          pe)
                    cap_yx[dpar] = pe.count
                    PE.op(lambda e, pm=pm, j=j, jj=jj, pbYs=pbYs, m=m: e.matmul(
                        pbYs[:, jj * 64:(jj + 1) * 64],
                        vC1[pm, j * 128:(j + 1) * 128],
                        state_bf[pm, j * 128:j * 128 + 64], start=True,
                        stop=False, tile_position=(32 * m, 0),
                        skip_group_check=True))
                    PE.op(lambda e, pm=pm, j=j, jj=jj, pbYs=pbYs, m=m: e.matmul(
                        pbYs[:, jj * 64:(jj + 1) * 64],
                        vC2[pm, j * 128:(j + 1) * 128],
                        state_bf[pm, j * 128 + 64:(j + 1) * 128], start=False,
                        stop=True, tile_position=(32 * m, 0),
                        skip_group_check=True), pe)
                    cap_ys[dpar] = pe.count
                    S.wait(pe, cap_ys[dpar])
                    if jj == 0:
                        S.wait(ve, cap_ygw)
                    S.op(lambda e, jj=jj, h=h, pbYs=pbYs: e.activation(
                        out=ysc[:, jj * 64:(jj + 1) * 64],
                        in_=pbYs[:, jj * 64:(jj + 1) * 64],
                        func=AF.Identity, scale=E_nat[:, h:h + 1]), se)
                    cap_ysc[0] = cap_ysc[1]
                    cap_ysc[1] = se.count
                V.wait(se)
                V.op(lambda e, pbYx=pbYx: e.tensor_tensor(
                    out=ysc[:], in0=ysc[:], in1=pbYx[:, 0:512], op=ALU.add))
                V.op(lambda e, m=m, par=par: e.tensor_tensor(
                    out=rmod[:, 0:512].rearrange("p (j q) -> p j q", j=8),
                    in0=Xsb[:, par * DI:(par + 1) * DI].rearrange(
                        "p (j hm q) -> p j hm q", j=8, hm=4)[:, :, m, :],
                    in1=dsk_s[:].rearrange("p (a b) -> p a b", a=4)[:, m, :]
                        .unsqueeze(2).broadcast_to([128, 8, 64]), op=ALU.mult))
                V.op(lambda e: e.tensor_tensor(
                    out=ysc[:], in0=ysc[:], in1=rmod[:, 0:512], op=ALU.add))
                V.op(lambda e, m=m: e.tensor_copy(
                    out=yg[:].rearrange("p (j hm q) -> p j hm q", j=8, hm=4)[:, :, m, :],
                    in_=ysc[:].rearrange("p (j c) -> p j c", j=8)), ve)
                cap_ygw = ve.count

            V.op(lambda e: e.tensor_tensor(out=yg[:], in0=yg[:], in1=sil,
                                           op=ALU.mult), ve)
            PE.wait(ve)
            for m in range(4):
                pm = slice(32 * m, 32 * m + 32)
                for j in range(8):
                    h = 4 * j + m
                    bank = pb[6] if j < 4 else pb[7]
                    jo = (j % 4) * 128
                    PE.op(lambda e, h=h, pm=pm, bank=bank, jo=jo, m=m, par=par:
                          e.matmul(bank[pm, jo:jo + 64],
                                   bpp[:, h * 64:h * 64 + 32],
                                   Xsb[:, par * DI + h * 64: par * DI + (h + 1) * 64],
                                   start=True, stop=True,
                                   tile_position=(0, 32 * m),
                                   skip_group_check=True))
                    PE.op(lambda e, h=h, pm=pm, bank=bank, jo=jo, m=m, par=par:
                          e.matmul(bank[pm, jo + 64:jo + 128],
                                   bpp[:, h * 64 + 32:(h + 1) * 64],
                                   Xsb[:, par * DI + h * 64: par * DI + (h + 1) * 64],
                                   start=True, stop=True,
                                   tile_position=(0, 32 * m),
                                   skip_group_check=True),
                          pe if (j in (3, 7)) else None)
            V.wait(pe)
            V.op(lambda e: e.tensor_tensor(
                out=state[:].rearrange("p (j c) -> p j c", j=8),
                in0=state[:].rearrange("p (j c) -> p j c", j=8),
                in1=dec4[:].unsqueeze(2).broadcast_to([128, 8, 128]), op=ALU.mult))
            V.op(lambda e: e.tensor_tensor(out=state[:, 0:512], in0=state[:, 0:512],
                                           in1=pb[6][:, 0:512], op=ALU.add))
            V.op(lambda e: e.tensor_tensor(out=state[:, 512:1024],
                                           in0=state[:, 512:1024],
                                           in1=pb[7][:, 0:512], op=ALU.add))
            V.op(lambda e: e.tensor_copy(out=state_bf[:], in_=state[:]), ve)

            V.op(lambda e: e.tensor_tensor(out=rmod[:], in0=yg[:, 0:1024],
                                           in1=yg[:, 0:1024], op=ALU.mult))
            V.op(lambda e: e.tensor_tensor(out=rln[:], in0=yg[:, 1024:2048],
                                           in1=yg[:, 1024:2048], op=ALU.mult))
            V.op(lambda e: e.tensor_reduce(out=cols[:, 0:1], in_=rmod[:], axis=AX.X,
                                           op=ALU.add))
            V.op(lambda e: e.tensor_reduce(out=cols[:, 1:2], in_=rln[:], axis=AX.X,
                                           op=ALU.add), ve)
            S.wait(ve)
            S.op(lambda e: e.activation(out=cols[:, 12:13], in_=cols[:, 0:1],
                                        func=AF.Identity, bias=cols[:, 1:2]))
            S.op(lambda e: e.activation(out=cols[:, 1:2], in_=cols[:, 12:13],
                                        func=AF.Sqrt, scale=1.0 / DI,
                                        bias=cns[:, 0:1]), se)
            V.wait(se)
            V.op(lambda e: e.reciprocal(out=cols[:, 13:14], in_=cols[:, 1:2]), ve)
            S.wait(ve)
            S.op(lambda e: e.activation(out=cols[:, 2:3], in_=cols[:, 13:14],
                                        func=AF.Identity), se)
            V.wait(se)
            V.op(lambda e: e.tensor_tensor(
                out=yg[:], in0=yg[:],
                in1=cols[:, 2:3].broadcast_to([128, DI]), op=ALU.mult))
            V.op(lambda e: e.tensor_tensor(out=yg[:], in0=yg[:], in1=nwrep_s[:],
                                           op=ALU.mult), ve)
            for k in range(16):
                bank = pb[2 + (k % 2)]
                ytd = vB1 if k < 8 else vB2
                ko = k % 8
                PE.wait(ve)
                PE.wait(se)
                PE.op(lambda e, k=k, bank=bank: e.matmul(
                    bank[:, 0:128], yg[:, k * 128:(k + 1) * 128], Ibf[:],
                    start=True, stop=True), pe)
                V.wait(pe)
                V.op(lambda e, ko=ko, bank=bank, ytd=ytd: e.tensor_copy(
                    out=ytd[:, ko * 128:(ko + 1) * 128],
                    in_=bank[:, 0:128]), ve)
            for k in range(16):
                ytd = vB1 if k < 8 else vB2
                ko = k % 8
                GQ.wait(pe)
                GQ.op(lambda e, k=k: e.dma_start(
                    out=wob[k % 2][:], in_=woutTbf_d[k * 128:(k + 1) * 128, :]),
                    gq, 16)
                PE.wait(gq)
                PE.wait(ve)
                for nh in range(2):
                    PE.op(lambda e, ko=ko, nh=nh, k=k, ytd=ytd: e.matmul(
                        pb[nh][:, 0:512], ytd[:, ko * 128:(ko + 1) * 128],
                        wob[k % 2][:, nh * 512:(nh + 1) * 512],
                        start=(k == 0), stop=(k == 15)), pe)
            SY.wait(ve)
            SY.op(lambda e, c=c, par=par: e.dma_start(
                out=usb[par][:], in_=u_d[c * 128:(c + 1) * 128, :]), sy, 16)
            V.wait(sy)
            V.wait(pe)
            V.op(lambda e, par=par: e.tensor_tensor(
                out=rln[:, 0:512], in0=usb[par][:, 0:512], in1=pb[0][:, 0:512],
                op=ALU.add))
            V.op(lambda e, par=par: e.tensor_tensor(
                out=rln[:, 512:1024], in0=usb[par][:, 512:1024], in1=pb[1][:, 0:512],
                op=ALU.add))
            V.op(lambda e: e.tensor_reduce(out=cols[:, 2:3], in_=rln[:], axis=AX.X,
                                           op=ALU.add), ve)
            S.wait(ve)
            S.op(lambda e: e.activation(out=cols[:, 4:5], in_=cols[:, 2:3],
                                        func=AF.Identity, scale=1.0 / DM), se)
            V.wait(se)
            V.op(lambda e: e.tensor_tensor(
                out=xsq[:], in0=rln[:],
                in1=cols[:, 4:5].broadcast_to([128, DM]), op=ALU.subtract))
            V.wait(sr)
            V.op(lambda e: e.tensor_tensor(out=xln[:], in0=xsq[:], in1=xsq[:],
                                           op=ALU.mult))
            V.op(lambda e: e.tensor_reduce(out=cols[:, 5:6], in_=xln[:], axis=AX.X,
                                           op=ALU.add), ve)
            S.wait(ve)
            S.op(lambda e: e.activation(out=cols[:, 8:9], in_=cols[:, 5:6],
                                        func=AF.Sqrt, scale=1.0 / DM,
                                        bias=cns[:, 0:1]), se)
            V.wait(se)
            V.op(lambda e: e.reciprocal(out=cols[:, 9:10], in_=cols[:, 8:9]), ve)
            S.wait(ve)
            S.op(lambda e: e.activation(out=cols[:, 10:11], in_=cols[:, 9:10],
                                        func=AF.Identity), se)
            V.wait(se)
            V.op(lambda e: e.tensor_tensor(
                out=xln[:], in0=xsq[:],
                in1=cols[:, 10:11].broadcast_to([128, DM]), op=ALU.mult), ve)
            SY.wait(ve)
            SY.op(lambda e, c=c: e.dma_start(
                out=r1scr_d[c * 128:(c + 1) * 128, :], in_=xln[:]), sr, 16)
            for k in range(8):
                bank = pb[2 + (k % 2)]
                PE.wait(ve)
                PE.op(lambda e, k=k, bank=bank: e.transpose(
                    bank[:, 0:128], xln[:, k * 128:(k + 1) * 128], I32[:]), pe)
                V.wait(pe)
                V.op(lambda e, bank=bank: e.tensor_copy(out=xt32[:],
                                                        in_=bank[:, 0:128]))
                V.op(lambda e, k=k, c=c, bank=bank: e.tensor_copy(
                    out=xT[:, k * L + c * 128:k * L + (c + 1) * 128],
                    in_=bank[:, 0:128]), ve)
                PE.wait(ve)
                PE.op(lambda e, k=k: e.matmul(
                    pb[4][0:8, 0:128], wr_s[:, k * 8:(k + 1) * 8], xt32[:],
                    start=(k == 0), stop=(k == 7)), pe if k == 7 else None)
            S.wait(pe)
            S.op(lambda e: e.activation(out=logT[:], in_=pb[4][0:8, 0:128],
                                        func=AF.Identity, bias=br_s[:]), se)
            PE.wait(se)
            PE.wait(ve)
            PE.op(lambda e: e.transpose(pb[5][:, 0:8], logT[:], I32[0:8, 0:8]), pe)
            V.wait(pe)
            V.op(lambda e: e.tensor_reduce(out=cols[:, 14:15], in_=pb[5][:, 0:8],
                                           axis=AX.X, op=ALU.max), ve)
            S.wait(ve)
            S.op(lambda e: e.activation(out=cols[:, 13:14], in_=cols[:, 14:15],
                                        func=AF.Identity), se)
            V.wait(se)
            V.op(lambda e: e.tensor_tensor(
                out=xt32[:, 8:16], in0=pb[5][:, 0:8],
                in1=cols[:, 13:14].broadcast_to([128, 8]), op=ALU.is_ge))
            V.op(lambda e: e.tensor_tensor(out=cols[:, 15:16], in0=cols[:, 13:14],
                                           in1=cns[:, 4:5], op=ALU.mult), ve)
            S.wait(ve)
            S.op(lambda e: e.activation(out=xt32[:, 0:8], in_=pb[5][:, 0:8],
                                        func=AF.Exp, bias=cols[:, 15:16],
                                        accum_out=cols[:, 11:12]), se)
            V.wait(se)
            V.op(lambda e: e.reciprocal(out=cols[:, 12:13], in_=cols[:, 11:12]), ve)
            S.wait(ve)
            S.op(lambda e: e.activation(out=cols[:, 10:11], in_=cols[:, 12:13],
                                        func=AF.Identity), se)
            V.wait(se)
            V.op(lambda e, c=c: e.tensor_tensor(
                out=gsel[:, c * 8:(c + 1) * 8], in0=xt32[:, 8:16],
                in1=cols[:, 10:11].broadcast_to([128, 8]), op=ALU.mult), ve)

        emit_wcasts(20)
        proj_stage(0)
        nssd = NCH if upto is None else min(NCH, upto)
        for c in range(nssd):
            if c + 1 < NCH:
                proj_stage(c + 1)
            ssd_stage(c)
            emit_wcasts(33)

        emit_wcasts(len(wcast_jobs))
        GQ.wait(gq)
        PE.wait(gq)
        if upto is None:
            for tg in range(4):
                V.op(lambda e: e.memset(acc[:], 0.0), ve)
                for e_i in range(E):
                    par_e = e_i % 2
                    SY.wait(se)
                    SY.op(lambda e, e_i=e_i, par_e=par_e: e.dma_start(
                        out=b1c_s[par_e][:], in_=b1_d[e_i]), sy, 16)
                    for ft in range(32):
                        par = ft % 4
                        GQ.wait(pe, pe.count if ft == 0 else pe.count - 3)
                        GQ.op(lambda e, e_i=e_i, ft=ft, par=par: e.dma_start(
                            out=w1bf4[par][:], in_=w1bf_d[e_i, ft]), gq, 16)
                        bank = pb[ft % 4]
                        PE.wait(gq)
                        PE.wait(se)
                        for k in range(8):
                            PE.op(lambda e, k=k, par=par, bank=bank, tg=tg: e.matmul(
                                bank[:, 0:512], w1bf4[par][:, k * 128:(k + 1) * 128],
                                xT[:, k * L + tg * 512:k * L + (tg + 1) * 512],
                                start=(k == 0), stop=(k == 7)),
                                pe if k == 7 else None)
                        S.wait(pe)
                        S.wait(sy)
                        S.op(lambda e, ft=ft, bank=bank, par_e=par_e: e.activation(
                            out=h1T[:, ft * 512:(ft + 1) * 512], in_=bank[:, 0:512],
                            func=AF.Relu, bias=b1c_s[par_e][:, ft:ft + 1]), se)
                    for fs in range(33):
                        par = fs % 4
                        GQ.wait(pe, pe.count if fs == 0 else pe.count - 3)
                        GQ.op(lambda e, e_i=e_i, fs=fs, par=par: e.dma_start(
                            out=w2bf4[par][:], in_=w2bf_d[e_i, fs]), gq, 16)
                        PE.wait(gq)
                        if fs == 0:
                            PE.wait(se)
                        for tt in range(4):
                            for nh in range(2):
                                PE.op(lambda e, fs=fs, tt=tt, nh=nh, par=par: e.matmul(
                                    pb[tt * 2 + nh][:, 0:512],
                                    h1T[:, fs * 512 + tt * 128:
                                        fs * 512 + (tt + 1) * 128],
                                    w2bf4[par][:, nh * 512:(nh + 1) * 512],
                                    start=(fs == 0), stop=(fs == 32)),
                                    pe if (tt == 3 and nh == 1) else None)
                    V.wait(pe)
                    for tt in range(4):
                        ci = tg * 4 + tt
                        for nh in range(2):
                            V.op(lambda e, tt=tt, nh=nh, ci=ci, e_i=e_i:
                                 e.scalar_tensor_tensor(
                                     out=acc[:, tt * DM + nh * 512:
                                            tt * DM + (nh + 1) * 512],
                                     in0=pb[tt * 2 + nh][:, 0:512],
                                     scalar=gsel[:, ci * 8 + e_i:ci * 8 + e_i + 1],
                                     in1=acc[:, tt * DM + nh * 512:
                                            tt * DM + (nh + 1) * 512],
                                     op0=ALU.mult, op1=ALU.add), ve)
                if e_i == E - 1:
                    for tt in range(4):
                        ci = tg * 4 + tt
                        par = tt % 2
                        SY.wait(ve)
                        SY.op(lambda e, ci=ci, par=par: e.dma_start(
                            out=usb[par][:], in_=r1scr_d[ci * 128:(ci + 1) * 128, :]),
                            sy, 16)
                        V.wait(sy)
                        V.op(lambda e, tt=tt, par=par: e.tensor_tensor(
                            out=rln[:], in0=usb[par][:],
                            in1=acc[:, tt * DM:(tt + 1) * DM], op=ALU.add))
                        V.op(lambda e: e.tensor_reduce(
                            out=cols[:, 2:3], in_=rln[:], axis=AX.X, op=ALU.add), ve)
                        S.wait(ve)
                        S.op(lambda e: e.activation(
                            out=cols[:, 4:5], in_=cols[:, 2:3], func=AF.Identity,
                            scale=1.0 / DM), se)
                        V.wait(se)
                        V.op(lambda e: e.tensor_tensor(
                            out=xsq[:], in0=rln[:],
                            in1=cols[:, 4:5].broadcast_to([128, DM]),
                            op=ALU.subtract))
                        V.op(lambda e: e.tensor_tensor(
                            out=w1st[0][:], in0=xsq[:, 0:512], in1=xsq[:, 0:512],
                            op=ALU.mult))
                        V.op(lambda e: e.tensor_tensor(
                            out=w1st[1][:], in0=xsq[:, 512:1024],
                            in1=xsq[:, 512:1024], op=ALU.mult))
                        V.op(lambda e: e.tensor_reduce(
                            out=cols[:, 5:6], in_=w1st[0][:], axis=AX.X, op=ALU.add))
                        V.op(lambda e: e.tensor_reduce(
                            out=cols[:, 6:7], in_=w1st[1][:], axis=AX.X, op=ALU.add),
                            ve)
                        S.wait(ve)
                        S.op(lambda e: e.activation(
                            out=cols[:, 7:8], in_=cols[:, 5:6], func=AF.Identity,
                            bias=cols[:, 6:7]))
                        S.op(lambda e: e.activation(
                            out=cols[:, 10:11], in_=cols[:, 7:8], func=AF.Sqrt,
                            scale=1.0 / DM, bias=cns[:, 0:1]), se)
                        V.wait(se)
                        V.op(lambda e: e.reciprocal(out=cols[:, 11:12],
                                                    in_=cols[:, 10:11]), ve)
                        S.wait(ve)
                        S.op(lambda e: e.activation(
                            out=cols[:, 12:13], in_=cols[:, 11:12],
                            func=AF.Identity), se)
                        V.wait(se)
                        V.wait(so)
                        V.op(lambda e, par=par: e.tensor_tensor(
                            out=xo32[par][:], in0=xsq[:],
                            in1=cols[:, 12:13].broadcast_to([128, DM]),
                            op=ALU.mult), ve)
                        SY.wait(ve)
                        SY.op(lambda e, ci=ci, par=par: e.dma_start(
                            out=out_d[ci * 128:(ci + 1) * 128, :], in_=xo32[par][:]),
                            so, 16)
            SY.wait(so)

        if taps:
            SY.wait(ve); SY.wait(se); SY.wait(pe); SY.wait(gq, gq.count)
            for tname, sname, shape in taps:
                t = SBUF[sname]
                td = dt("tap_" + tname, list(shape), t.dtype,
                        kind="ExternalOutput").ap()
                SY.op(lambda e, t=t, td=td, shape=shape: e.dma_start(
                    out=td, in_=t[0:shape[0], 0:shape[1]]), so, 16)
            SY.wait(so)

        with nc.Block() as block:
            @block.sync
            def _(eng):
                SY.replay(eng)

            @block.gpsimd
            def _(eng):
                GQ.replay(eng)

            @block.tensor
            def _(eng):
                PE.replay(eng)

            @block.vector
            def _(eng):
                V.replay(eng)

            @block.scalar
            def _(eng):
                S.replay(eng)

    return nc


def _host_inputs(inputs):
    u = np.asarray(inputs["L0_attn_u"], np.float32)
    ipw = np.asarray(inputs["L0_attn_in_proj_w"], np.float32)
    perm = np.concatenate([
        np.arange(0, 2048), np.arange(2048, 4096), np.arange(4096, 4224),
        np.arange(4224, 4256), np.arange(5280, 5312), np.arange(4256, 5280)])
    winT = np.ascontiguousarray(ipw[perm].T)
    woutT = np.ascontiguousarray(
        np.asarray(inputs["L0_attn_out_proj_w"], np.float32).T)
    W1 = np.asarray(inputs["L0_ffn_W1"], np.float32)
    w1r = np.ascontiguousarray(
        W1.reshape(E, 8, 128, 32, 128).transpose(0, 3, 1, 2, 4))
    W2 = np.asarray(inputs["L0_ffn_W2"], np.float32)
    b2 = np.asarray(inputs["L0_ffn_b2"], np.float32)
    w2p = np.zeros((E, 33, 128, DM), np.float32)
    w2p[:, :32] = W2.reshape(E, 32, 128, DM)
    w2p[:, 32, 0, :] = b2
    b1 = np.asarray(inputs["L0_ffn_b1"], np.float32)
    b1c = np.ascontiguousarray(b1.reshape(E, 32, 128).transpose(0, 2, 1))
    Bb = np.asarray(inputs["L0_attn_B_bias"], np.float32)
    Cb = np.asarray(inputs["L0_attn_C_bias"], np.float32)

    def vint(bias):
        outs = np.zeros((2, 128, 1024), np.float32)
        m2g, kg = np.divmod(np.arange(128), 32)
        for half in range(2):
            for j in range(8):
                outs[half, :, 128 * j:128 * (j + 1)] = \
                    bias[4 * j + m2g, half * 32 + kg][:, None]
        return outs

    nwrep = np.ascontiguousarray(np.broadcast_to(
        np.asarray(inputs["L0_attn_norm_w"], np.float32)[None, :], (128, DI)))
    ii, jjx = np.meshgrid(np.arange(128), np.arange(128), indexing="ij")
    mneg = np.where(ii < jjx, -1e30, 0.0).astype(np.float32)
    kk = np.arange(128) % 32
    shared = dict(
        winT=winT, woutT=woutT,
        wr=np.ascontiguousarray(np.asarray(inputs["L0_ffn_Wr"], np.float32)),
        br=np.asarray(inputs["L0_ffn_br"], np.float32).reshape(E, 1),
        w1=w1r, b1c=b1c, w2=w2p,
        dtb=np.asarray(inputs["L0_attn_dt_bias"], np.float32).reshape(H, 1),
        alog=np.asarray(inputs["L0_attn_A_log"], np.float32).reshape(H, 1),
        dsk4=np.ascontiguousarray(np.broadcast_to(
            np.asarray(inputs["L0_attn_D"], np.float32).reshape(-1)[
                (4 * np.arange(8)[None, :] + np.arange(4)[:, None])][None, :, :],
            (128, 4, 8))),
        bnw=np.asarray(inputs["L0_attn_B_norm_w"], np.float32).reshape(N, 1),
        cnw=np.asarray(inputs["L0_attn_C_norm_w"], np.float32).reshape(N, 1),
        bbv=vint(Bb), cbv=vint(Cb), nwrep=nwrep,
        ident=np.eye(128, dtype=np.float32), mneg=mneg,
        repl1=(np.arange(N)[:, None] == kk[None, :]).astype(np.float32),
        repl2=(np.arange(N)[:, None] == (32 + kk)[None, :]).astype(np.float32),
        repl4=((np.arange(H)[:, None] % 4) ==
               (np.arange(128)[None, :] // 32)).astype(np.float32),
        selj8=((np.arange(H)[:, None] // 4) ==
               np.arange(8)[None, :]).astype(np.float32),
    )
    in_maps = []
    for core in range(8):
        d = dict(shared)
        d["u"] = np.ascontiguousarray(u[core])
        in_maps.append(d)
    return in_maps


def kernel(**inputs):
    in_maps = _host_inputs(inputs)
    nc = build_program()
    res = run_bass_kernel_spmd(nc, in_maps, core_ids=list(range(8)))
    out = np.stack([np.asarray(r["out"], np.float32) for r in res.results], axis=0)
    return out


# version 04
# speedup vs baseline: 2.0094x; 1.2222x over previous
import math
import numpy as np
from contextlib import ExitStack

import concourse.bass as bass
import concourse.mybir as mybir
from concourse.bass_utils import run_bass_kernel_spmd

F32 = mybir.dt.float32
BF16 = mybir.dt.bfloat16
AF = mybir.ActivationFunctionType
ALU = mybir.AluOpType
AX = mybir.AxisListType

L = 2048
DM = 1024
H = 32
P = 64
N = 64
DI = H * P
NCH = 16
Q = 128
DPROJ = 5312
E = 8
DFF = 4096
TWO_PI = 2.0 * math.pi
MAGIC = 1.5 * (2.0 ** 23)
PI = math.pi
PIC = math.pi * (1.0 - 1e-6)

ZC0, XC0, BCC0, DLC0, THC0 = 0, 2048, 4096, 4224, 4288


class Sem:
    def __init__(self, handle):
        self.h = handle
        self.count = 0


class Emitter:
    def __init__(self, name):
        self.name = name
        self.items = []
        self._waited = {}

    def wait(self, sem, val=None):
        v = sem.count if val is None else val
        if self._waited.get(id(sem), -1) >= v or v <= 0:
            return
        self._waited[id(sem)] = v
        self.items.append(lambda e, s=sem, v=v: e.wait_ge(s.h, v))

    def op(self, fn, inc=None, amt=1):
        if inc is not None:
            inc.count += amt
            def run(e, fn=fn, inc=inc, amt=amt):
                fn(e).then_inc(inc.h, amt)
        else:
            def run(e, fn=fn):
                fn(e)
        self.items.append(run)

    def replay(self, eng):
        for it in self.items:
            it(eng)


def build_program(upto=None, taps=()):
    nc = bass.Bass("TRN2", target_bir_lowering=False)
    dt = nc.dram_tensor

    u_d = dt("u", [L, DM], F32, kind="ExternalInput").ap()
    winT_d = dt("winT", [DM, DPROJ], F32, kind="ExternalInput").ap()
    woutT_d = dt("woutT", [DI, DM], F32, kind="ExternalInput").ap()
    wr_d = dt("wr", [DM, E], F32, kind="ExternalInput").ap()
    br_d = dt("br", [E, 1], F32, kind="ExternalInput").ap()
    w1_d = dt("w1", [E, 32, 8, 128, 128], F32, kind="ExternalInput").ap()
    b1_d = dt("b1c", [E, 128, 32], F32, kind="ExternalInput").ap()
    w2_d = dt("w2", [E, 33, 128, DM], F32, kind="ExternalInput").ap()
    dtb_d = dt("dtb", [H, 1], F32, kind="ExternalInput").ap()
    alog_d = dt("alog", [H, 1], F32, kind="ExternalInput").ap()
    dsk_d = dt("dsk4", [128, 4, 8], F32, kind="ExternalInput").ap()
    bnw_d = dt("bnw", [N, 1], F32, kind="ExternalInput").ap()
    cnw_d = dt("cnw", [N, 1], F32, kind="ExternalInput").ap()
    bbv_d = dt("bbv", [2, 128, 1024], F32, kind="ExternalInput").ap()
    cbv_d = dt("cbv", [2, 128, 1024], F32, kind="ExternalInput").ap()
    nwrep_d = dt("nwrep", [128, DI], F32, kind="ExternalInput").ap()
    ident_d = dt("ident", [128, 128], F32, kind="ExternalInput").ap()
    mneg_d = dt("mneg", [128, 128], F32, kind="ExternalInput").ap()
    repl1_d = dt("repl1", [N, 128], F32, kind="ExternalInput").ap()
    repl2_d = dt("repl2", [N, 128], F32, kind="ExternalInput").ap()
    repl4_d = dt("repl4", [H, 128], F32, kind="ExternalInput").ap()
    selj8_d = dt("selj8", [H, 8], F32, kind="ExternalInput").ap()
    out_d = dt("out", [L, DM], F32, kind="ExternalOutput").ap()

    winTbf_d = dt("winTbf", [DM, DPROJ], BF16, kind="Internal").ap()
    woutTbf_d = dt("woutTbf", [DI, DM], BF16, kind="Internal").ap()
    r1scr_d = dt("r1scr", [L, DM], F32, kind="Internal").ap()
    w1bf_d = dt("w1bfd", [E, 32, 128, 1024], BF16, kind="Internal").ap()
    w2bf_d = dt("w2bfd", [E, 33, 128, DM], BF16, kind="Internal").ap()


    st = ExitStack()
    SBUF = {}

    def sb(n, sh, d=F32):
        t = st.enter_context(nc.sbuf_tensor(n, sh, d))
        SBUF[n] = t
        return t

    ps = lambda n, sh: st.enter_context(nc.psum_tensor(n, sh, F32))
    mksem = lambda n: Sem(st.enter_context(nc.semaphore(n)))

    with st:
        xT = sb("xT", [128, 8 * L], BF16)
        acc = sb("acc", [128, 4 * DM], BF16)
        h1T = sb("h1T", [128, 33 * 512], BF16)
        zsb = h1T[:, 0:4096]
        Xsb = h1T[:, 4096:8192]
        tcV = h1T[:, 8192:12288].bitcast(F32)
        bpp = h1T[:, 12288:14336]
        sil = h1T[:, 14336:16384]
        bcT = sb("bcT", [128, 2 * 128], F32)
        dtT = sb("dtT", [32, 2 * 128], F32)
        lamT = sb("lamT", [32, 2 * 128], F32)
        g_t = sb("g_t", [32, 2 * 128], F32)
        q_t = sb("q_t", [32, 2 * 128], F32)
        wgb = [sb(f"wgb{i}", [128, 8 * 256], BF16) for i in range(4)]
        wob = [sb(f"wob{i}", [128, DM], BF16) for i in range(2)]
        usb = [sb(f"usb{i}", [128, DM], F32) for i in range(2)]

        ghat = sb("ghat", [32, 128], F32)
        s_t = sb("s_t", [32, 128], F32)
        S_T = sb("S_T", [32, 128], F32)
        lng = sb("lng", [32, 128], F32)
        wT = sb("wT", [32, 128], F32)
        expd = sb("expd", [32, 128], F32)
        pack = sb("pack", [128, 128], F32)
        packN = sb("packN", [128, 128], F32)
        E_nat = sb("E_nat", [128, 32], F32)
        dec_c = sb("dec_c", [32, 1], F32)
        dec4i = sb("dec4i", [32, 8], F32)
        dec4 = sb("dec4", [128, 8], F32)
        wvsel = sb("wvsel", [32, 8 * 128], BF16)
        state = sb("state", [128, 1024], F32)
        state_bf = sb("state_bf", [128, 1024], BF16)
        sqT = sb("sqT", [128, 128], F32)
        invB = sb("invB", [1, 128], F32)
        invC = sb("invC", [1, 128], F32)
        brawn = sb("brawn", [64, 128], BF16)
        brawn8 = sb("brawn8", [64, 8 * 128], BF16)
        crawn = sb("crawn", [64, 128], BF16)
        crawn8 = sb("crawn8", [64, 8 * 128], BF16)
        vB1 = sb("vB1", [128, 1024], BF16)
        vB2 = sb("vB2", [128, 1024], BF16)
        vC1 = sb("vC1", [128, 1024], BF16)
        vC2 = sb("vC2", [128, 1024], BF16)
        rta = sb("rta", [128, 1024], BF16)
        rtb = sb("rtb", [128, 1024], BF16)
        vBW1 = rta
        vBW2 = rtb
        rmod = sb("rmod", [128, 1024], F32)
        cosV = sb("cosV", [128, 1024], BF16)
        sinV = sb("sinV", [128, 1024], BF16)
        pre_t = [sb(f"pre{i}", [128, 128], F32) for i in range(2)]
        D2 = [sb(f"D2_{i}", [128, 128], F32) for i in range(2)]
        G_bf = [sb(f"G_{i}", [128, 128], BF16) for i in range(2)]
        ysc = sb("ysc", [128, 512], F32)
        yg = sb("yg", [128, DI], BF16)
        I32 = sb("I32", [128, 128], F32)
        Ibf = sb("Ibf", [128, 128], BF16)
        Mneg = sb("Mneg", [128, 128], F32)
        rp1 = sb("rp1", [64, 128], F32)
        rp2 = sb("rp2", [64, 128], F32)
        repl1w = sb("repl1w", [64, 128], BF16)
        repl2w = sb("repl2w", [64, 128], BF16)
        repl1wc = sb("repl1wc", [64, 128], BF16)
        repl2wc = sb("repl2wc", [64, 128], BF16)
        repl4s = sb("repl4s", [32, 128], F32)
        repl4bf = sb("repl4bf", [32, 128], BF16)
        selj8s = sb("selj8s", [32, 8], F32)
        bbv_s = sb("bbv_s", [128, 2 * 1024], BF16)
        cbv_s = sb("cbv_s", [128, 2 * 1024], BF16)
        nwrep_s = sb("nwrep_s", [128, DI], BF16)
        onec = sb("onec", [128, 128], F32)
        a_col = sb("a_col", [32, 1], F32)
        acol_t = sb("acol_t", [32, 1], F32)
        dtb_s = sb("dtb_s", [32, 1], F32)
        dsk_s = sb("dsk_s", [128, 32], F32)
        bnw_s = sb("bnw_s", [64, 1], F32)
        cnw_s = sb("cnw_s", [64, 1], F32)
        wr_s = sb("wr_s", [128, 8 * 8], F32)
        br_s = sb("br_s", [8, 1], F32)
        zcol = sb("zcol", [32, 1], F32)
        rln = sb("rln", [128, DM], F32)
        xln = sb("xln", [128, DM], F32)
        carg = rln
        atmp = xln
        xsq = rmod
        cst32 = [rln, xln]
        cstbf_t = None
        xsq = sb("xsq", [128, DM], F32)
        xt32 = sb("xt32", [128, 128], F32)
        logT = sb("logT", [8, 128], F32)
        cols = sb("cols", [128, 16], F32)
        cns = sb("cns", [128, 6], F32)
        gsel = sb("gsel", [128, 8 * 16], F32)
        w1st = [sb(f"w1st{i}", [128, 512], F32) for i in range(4)]
        w1bf = [sb(f"w1bf{i}", [128, 1024], BF16) for i in range(2)]
        w1bf4 = [sb(f"w1bf4_{i}", [128, 1024], BF16) for i in range(4)]
        w2bf4 = [w1bf4[0], w1bf4[1], w1bf4[2], w1bf4[3]]
        b1c_s = [sb(f"b1c{i}", [128, 32], F32) for i in range(2)]
        xo32 = [xln, xln]

        pb = [ps(f"pb{i}", [128, 512]) for i in range(8)]

        for _n, _a in (("zsb", zsb), ("Xsb", Xsb), ("tcV", tcV),
                       ("bpp", bpp), ("sil", sil)):
            SBUF[_n] = _a
        sy = mksem("sy")
        sr = mksem("sr")
        gq = mksem("gq")
        pe = mksem("pe")
        ve = mksem("ve")
        se = mksem("se")
        so = mksem("so")

        SY = Emitter("sync"); GQ = Emitter("gpsimd"); PE = Emitter("tensor")
        V = Emitter("vector"); S = Emitter("scalar")

        for a, b in [(I32, ident_d), (Mneg, mneg_d), (repl4s, repl4_d),
                     (selj8s, selj8_d), (a_col, alog_d), (dtb_s, dtb_d),
                     (dsk_s, dsk_d.rearrange("p a b -> p (a b)")), (bnw_s, bnw_d), (cnw_s, cnw_d),
                     (br_s, br_d), (rp1, repl1_d),
                     (rp2, repl2_d)]:
            SY.op(lambda e, a=a, b=b: e.dma_start(out=a[:], in_=b), sy, 16)
        SY.op(lambda e: e.dma_start(out=wr_s[:].rearrange("p (k c) -> p k c", k=8),
                                    in_=wr_d.rearrange("(k p) c -> p k c", p=128)), sy, 16)

        V.wait(sy)
        V.op(lambda e: e.memset(onec[:], 1.0))
        V.op(lambda e: e.tensor_copy(out=Ibf[:], in_=I32[:]))
        V.op(lambda e: e.tensor_copy(out=repl4bf[:], in_=repl4s[:]))
        V.op(lambda e: e.tensor_scalar(out=repl1w[:], in0=rp1[:], scalar1=bnw_s[:],
                                       scalar2=None, op0=ALU.mult))
        V.op(lambda e: e.tensor_scalar(out=repl2w[:], in0=rp2[:], scalar1=bnw_s[:],
                                       scalar2=None, op0=ALU.mult))
        V.op(lambda e: e.tensor_scalar(out=repl1wc[:], in0=rp1[:], scalar1=cnw_s[:],
                                       scalar2=None, op0=ALU.mult))
        V.op(lambda e: e.tensor_scalar(out=repl2wc[:], in0=rp2[:], scalar1=cnw_s[:],
                                       scalar2=None, op0=ALU.mult))
        V.op(lambda e: e.memset(zcol[:], 0.0))
        V.op(lambda e: e.memset(cns[:, 0:1], 1e-5))
        V.op(lambda e: e.memset(cns[:, 1:2], 1.0))
        V.op(lambda e: e.memset(cns[:, 2:3], 1.0 / DI))
        V.op(lambda e: e.memset(cns[:, 3:4], 1.0 / DM))
        V.op(lambda e: e.memset(cns[:, 4:5], -1.0))
        V.op(lambda e: e.memset(state[:], 0.0))
        V.op(lambda e: e.memset(state_bf[:], 0.0))
        V.op(lambda e: e.memset(h1T[:, 32 * 512:33 * 512], 0.0))
        V.op(lambda e: e.memset(h1T[0:1, 32 * 512:32 * 512 + 512], 1.0), ve)
        S.wait(sy)
        S.op(lambda e: e.activation(out=acol_t[:], in_=a_col[:], func=AF.Exp), se)
        V.wait(se)
        V.op(lambda e: e.tensor_scalar(out=a_col[:], in0=acol_t[:], scalar1=-1.0,
                                       scalar2=None, op0=ALU.mult), ve)

        cast_jobs = []
        wpieces = [(i * 1024, 1024) for i in range(5)] + [(5120, 192)]
        for k in range(8):
            for s0, w in wpieces:
                cast_jobs.append((winT_d[k * 128:(k + 1) * 128, s0:s0 + w],
                                  winTbf_d[k * 128:(k + 1) * 128, s0:s0 + w],
                                  None, w))
        for k in range(16):
            cast_jobs.append((woutT_d[k * 128:(k + 1) * 128, :],
                              woutTbf_d[k * 128:(k + 1) * 128, :], None, DM))
        for half in range(2):
            cast_jobs.append((bbv_d[half], None,
                              bbv_s[:, half * 1024:(half + 1) * 1024], 1024))
            cast_jobs.append((cbv_d[half], None,
                              cbv_s[:, half * 1024:(half + 1) * 1024], 1024))
            cast_jobs.append((nwrep_d[:, half * 1024:(half + 1) * 1024], None,
                              nwrep_s[:, half * 1024:(half + 1) * 1024], 1024))

        def emit_cast_job(srcap, dstap, sdst, w, par):
            GQ.wait(ve)
            GQ.op(lambda e, s=srcap, w=w, par=par: e.dma_start(
                out=cst32[par][:, 0:w], in_=s), gq, 16)
            V.wait(gq)
            if sdst is None:
                V.op(lambda e, w=w, par=par: e.tensor_copy(
                    out=yg[:, par * 1024:par * 1024 + w],
                    in_=cst32[par][:, 0:w]), ve)
                GQ.wait(ve)
                GQ.op(lambda e, d=dstap, w=w, par=par: e.dma_start(
                    out=d, in_=yg[:, par * 1024:par * 1024 + w]), gq, 16)
            else:
                V.op(lambda e, sdst=sdst, w=w, par=par: e.tensor_copy(
                    out=sdst, in_=cst32[par][:, 0:w]), ve)

        for idx, job in enumerate(cast_jobs):
            emit_cast_job(*job, idx % 2)
        scratch_done = gq.count

        wcast_jobs = []
        for e_i in range(E):
            for ft in range(32):
                wcast_jobs.append((
                    ("w1", w1_d[e_i, ft].transpose([1, 0, 2])),
                    w1bf_d[e_i, ft], 1024))
            for fs in range(33):
                wcast_jobs.append((("w2", w2_d[e_i, fs]), w2bf_d[e_i, fs], 1024))
        wcast_pos = [0]

        wc_slots32 = [
            (lambda: acc[:, 0:1024].bitcast(F32),
             lambda: acc[:, 1024:2048].bitcast(F32)),
            (lambda: acc[:, 2048:3072].bitcast(F32),
             lambda: acc[:, 3072:4096].bitcast(F32)),
            (lambda: w1st[0][:], lambda: w1st[1][:]),
            (lambda: w1st[2][:], lambda: w1st[3][:]),
        ]
        wc_state = {"loaded": -1, "gq_cap": [0, 0, 0, 0],
                    "ve_slot": [0, 0, 0, 0], "st_cap": [0, 0]}

        def _wc_src(i):
            (kind, srcap), dstap, w = wcast_jobs[i]
            if kind == "w1":
                return srcap.rearrange("hp k fi -> hp (k fi)") if False else srcap,                     dstap, kind
            return srcap, dstap, kind

        def emit_wcasts(count):
            for _ in range(count):
                i = wcast_pos[0]
                if i >= len(wcast_jobs):
                    return
                wcast_pos[0] += 1
                while wc_state["loaded"] < min(i + 2, len(wcast_jobs) - 1):
                    li = wc_state["loaded"] + 1
                    wc_state["loaded"] = li
                    sl = li % 4
                    (kind, srcap), dstap, w = wcast_jobs[li]
                    GQ.wait(ve, wc_state["ve_slot"][li % 4])
                    if kind == "w1":
                        h0s, h1s = srcap[:, 0:4, :], srcap[:, 4:8, :]
                    else:
                        h0s, h1s = srcap[:, 0:512], srcap[:, 512:1024]
                    GQ.op(lambda e, s=h0s, sl=sl: e.dma_start(
                        out=wc_slots32[sl][0](), in_=s), gq, 16)
                    GQ.op(lambda e, s=h1s, sl=sl: e.dma_start(
                        out=wc_slots32[sl][1](), in_=s), gq, 16)
                    wc_state["gq_cap"][sl] = gq.count
                sl = i % 4
                par = i % 2
                (kind, srcap), dstap, w = wcast_jobs[i]
                V.wait(gq, max(wc_state["gq_cap"][sl], wc_state["st_cap"][par]))
                V.op(lambda e, sl=sl, par=par: e.tensor_copy(
                    out=w1bf[par][:, 0:512], in_=wc_slots32[sl][0]()))
                V.op(lambda e, sl=sl, par=par: e.tensor_copy(
                    out=w1bf[par][:, 512:1024], in_=wc_slots32[sl][1]()), ve)
                wc_state["ve_slot"][sl] = ve.count
                GQ.wait(ve, ve.count)
                GQ.op(lambda e, d=dstap, par=par: e.dma_start(
                    out=d, in_=w1bf[par][:]), gq, 16)
                wc_state["st_cap"][par] = gq.count

        pe_tile_end = {}
        for t in range(16):
            par = t % 2
            if t >= 2:
                SY.wait(pe, pe_tile_end[t - 2])
            SY.op(lambda e, t=t, par=par: e.dma_start(
                out=usb[par][:], in_=u_d[t * 128:(t + 1) * 128, :]), sy, 16)
            PE.wait(sy)
            for k in range(8):
                bank = pb[k % 2]
                PE.wait(ve, ve.count - 1)
                PE.op(lambda e, k=k, par=par, bank=bank: e.transpose(
                    bank[:, 0:128], usb[par][:, k * 128:(k + 1) * 128], I32[:]), pe)
                V.wait(pe)
                V.op(lambda e, k=k, t=t, bank=bank: e.tensor_copy(
                    out=xT[:, k * L + t * 128: k * L + (t + 1) * 128],
                    in_=bank[:, 0:128]), ve)
            pe_tile_end[t] = pe.count

        PE.wait(gq, scratch_done)
        GQ.wait(gq, scratch_done)

        nat_groups = [(ZC0 + 256 * i, 256) for i in range(8)] + \
                     [(XC0 + 256 * i, 256) for i in range(8)]
        tt_groups = [(BCC0, 128), (DLC0, 64)] + [(THC0 + 128 * j, 128) for j in range(8)]

        proj_caps = [0, 0, 0, 0]

        def proj_stage(c):
            par = c % 2
            xcols = slice(c * 128, (c + 1) * 128)
            for gi, (c0, w) in enumerate(nat_groups + tt_groups):
                is_nat = gi < len(nat_groups)
                bank = pb[gi % 2]
                buf = wgb[gi % 4]
                QE, qs = (SY, sy) if gi % 2 == 0 else (GQ, gq)
                QE.wait(pe, proj_caps[gi % 4])
                QE.op(lambda e, c0=c0, w=w, buf=buf: e.dma_start(
                    out=buf[:, 0:8 * w].rearrange("p (k c) -> p k c", k=8),
                    in_=winTbf_d[:, c0:c0 + w].rearrange("(k p) c -> p k c", p=128)),
                    qs, 16)
                PE.wait(qs)
                PE.wait(ve)
                for k in range(8):
                    if is_nat:
                        PE.op(lambda e, k=k, w=w, buf=buf, bank=bank, xcols=xcols:
                              e.matmul(bank[:, 0:w],
                                       xT[:, k * L:(k + 1) * L][:, xcols],
                                       buf[:, k * w:(k + 1) * w],
                                       start=(k == 0), stop=(k == 7)),
                              pe if k == 7 else None)
                    else:
                        PE.op(lambda e, k=k, w=w, buf=buf, bank=bank, xcols=xcols:
                              e.matmul(bank[0:w, 0:128],
                                       buf[:, k * w:(k + 1) * w],
                                       xT[:, k * L:(k + 1) * L][:, xcols],
                                       start=(k == 0), stop=(k == 7)),
                              pe if k == 7 else None)
                proj_caps[gi % 4] = pe.count
                V.wait(pe)
                if is_nat:
                    if c0 < XC0:
                        off = c0 - ZC0
                        V.op(lambda e, off=off, w=w, bank=bank, par=par: e.tensor_copy(
                            out=zsb[:, par * DI + off: par * DI + off + w],
                            in_=bank[:, 0:w]), ve)
                    else:
                        off = c0 - XC0
                        V.op(lambda e, off=off, w=w, bank=bank, par=par: e.tensor_copy(
                            out=Xsb[:, par * DI + off: par * DI + off + w],
                            in_=bank[:, 0:w]), ve)
                elif c0 == BCC0:
                    V.op(lambda e, bank=bank, par=par: e.tensor_copy(
                        out=bcT[:, par * 128:(par + 1) * 128], in_=bank[:, 0:128]), ve)
                elif c0 == DLC0:
                    V.op(lambda e, bank=bank, par=par: e.tensor_copy(
                        out=dtT[:, par * 128:(par + 1) * 128], in_=bank[0:32, 0:128]))
                    V.op(lambda e, bank=bank, par=par: e.tensor_copy(
                        out=lamT[:, par * 128:(par + 1) * 128], in_=bank[32:64, 0:128]),
                        ve)
                else:
                    j = (c0 - THC0) // 128
                    if c == 0:
                        ini = 0.0
                    else:
                        prev = (c - 1) % 2
                        ini = tcV[:, prev * 1024 + j * 128 + 127:
                                  prev * 1024 + j * 128 + 128]
                    V.op(lambda e, bank=bank, par=par, j=j, ini=ini:
                         e.tensor_tensor_scan(
                             out=tcV[:, par * 1024 + j * 128: par * 1024 + (j + 1) * 128],
                             data0=onec[:], data1=bank[:, 0:128], initial=ini,
                             op0=ALU.mult, op1=ALU.add), ve)
            S.wait(ve)
            S.op(lambda e, par=par: e.activation(
                out=s_t[:], in_=dtT[:, par * 128:(par + 1) * 128], func=AF.Exp,
                bias=dtb_s[:]))
            S.op(lambda e, par=par: e.activation(
                out=dtT[:, par * 128:(par + 1) * 128], in_=s_t[:], func=AF.Ln,
                bias=cns[0:32, 1:2]))
            S.op(lambda e, par=par: e.activation(
                out=lamT[:, par * 128:(par + 1) * 128],
                in_=lamT[:, par * 128:(par + 1) * 128], func=AF.Sigmoid), se)
            V.wait(se)
            V.op(lambda e, par=par: e.tensor_tensor(
                out=g_t[:, par * 128:(par + 1) * 128],
                in0=lamT[:, par * 128:(par + 1) * 128],
                in1=dtT[:, par * 128:(par + 1) * 128], op=ALU.mult))
            V.op(lambda e, par=par: e.tensor_tensor(
                out=q_t[:, par * 128:(par + 1) * 128],
                in0=dtT[:, par * 128:(par + 1) * 128],
                in1=g_t[:, par * 128:(par + 1) * 128], op=ALU.subtract), ve)

        def ssd_stage(c):
            par = c % 2
            npar = (c + 1) % 2
            S.wait(ve)
            S.op(lambda e, par=par: e.activation(
                out=sil, in_=zsb[:, par * DI:(par + 1) * DI], func=AF.Silu), se)
            V.op(lambda e, par=par: e.tensor_tensor(
                out=ghat[:, 0:127], in0=g_t[:, par * 128:par * 128 + 127],
                in1=q_t[:, par * 128 + 1:par * 128 + 128], op=ALU.add))
            qn = zcol[:] if c == NCH - 1 else q_t[:, npar * 128:npar * 128 + 1]
            V.op(lambda e, qn=qn, par=par: e.tensor_tensor(
                out=ghat[:, 127:128], in0=g_t[:, par * 128 + 127:par * 128 + 128],
                in1=qn, op=ALU.add))
            V.op(lambda e, par=par: e.tensor_scalar(
                out=s_t[:], in0=dtT[:, par * 128:(par + 1) * 128], scalar1=a_col[:],
                scalar2=None, op0=ALU.mult))
            V.op(lambda e: e.tensor_tensor_scan(
                out=S_T[:], data0=onec[0:32, :], data1=s_t[:], initial=0.0,
                op0=ALU.mult, op1=ALU.add), ve)
            S.wait(ve)
            S.op(lambda e: e.activation(out=lng[:], in_=ghat[:], func=AF.Ln))
            S.op(lambda e: e.activation(out=expd[:], in_=S_T[:], func=AF.Exp,
                                        bias=S_T[:, 127:128], scale=-1.0))
            S.op(lambda e: e.activation(out=dec_c[:], in_=S_T[:, 127:128],
                                        func=AF.Exp), se)
            V.wait(se)
            V.op(lambda e: e.tensor_tensor(out=pack[0:32, :], in0=lng[:], in1=S_T[:],
                                           op=ALU.subtract))
            V.op(lambda e, par=par: e.tensor_tensor(
                out=pack[32:64, :], in0=g_t[:, par * 128:(par + 1) * 128],
                in1=ghat[:], op=ALU.subtract))
            V.op(lambda e: e.tensor_copy(out=pack[64:96, :], in_=S_T[:]))
            V.op(lambda e: e.tensor_tensor(out=wT[:], in0=ghat[:], in1=expd[:],
                                           op=ALU.mult))
            V.op(lambda e: e.tensor_scalar(out=dec4i[:], in0=selj8s[:],
                                           scalar1=dec_c[:], scalar2=None,
                                           op0=ALU.mult), ve)
            PE.wait(ve)
            PE.op(lambda e: e.transpose(pb[2][:, 0:128], pack[:], I32[:]), pe)
            PE.op(lambda e: e.matmul(pb[3][:, 0:8], repl4s[:], dec4i[:],
                                     start=True, stop=True), pe)
            V.wait(pe)
            V.op(lambda e: e.tensor_copy(out=packN[:], in_=pb[2][:, 0:128]))
            V.op(lambda e: e.tensor_copy(out=dec4[:], in_=pb[3][:, 0:8]), ve)
            S.wait(ve)
            S.op(lambda e: e.activation(out=E_nat[:], in_=packN[:, 64:96],
                                        func=AF.Exp), se)

            tcs = tcV[:, par * 1024:(par + 1) * 1024]
            V.op(lambda e, tcs=tcs: e.tensor_scalar(
                out=atmp[:], in0=tcs, scalar1=1.0 / TWO_PI, scalar2=MAGIC,
                op0=ALU.mult, op1=ALU.add))
            V.op(lambda e: e.tensor_scalar(
                out=atmp[:], in0=atmp[:], scalar1=MAGIC, scalar2=None,
                op0=ALU.subtract))
            V.op(lambda e, tcs=tcs: e.scalar_tensor_tensor(
                out=rmod[:], in0=atmp[:], scalar=-TWO_PI, in1=tcs,
                op0=ALU.mult, op1=ALU.add))
            V.op(lambda e: e.tensor_scalar(
                out=rmod[:], in0=rmod[:], scalar1=-PIC, scalar2=PIC,
                op0=ALU.max, op1=ALU.min))
            V.op(lambda e: e.tensor_scalar(
                out=atmp[:], in0=rmod[:], scalar1=PI / 2, scalar2=None, op0=ALU.is_gt))
            V.op(lambda e: e.tensor_scalar(
                out=carg[:], in0=rmod[:], scalar1=PI / 2, scalar2=None, op0=ALU.add))
            V.op(lambda e: e.scalar_tensor_tensor(
                out=carg[:], in0=atmp[:], scalar=-TWO_PI, in1=carg[:],
                op0=ALU.mult, op1=ALU.add))
            V.op(lambda e: e.tensor_scalar(
                out=carg[:], in0=carg[:], scalar1=-PIC, scalar2=PIC,
                op0=ALU.max, op1=ALU.min), ve)
            S.wait(ve)
            S.op(lambda e: e.activation(out=rmod[:], in_=rmod[:], func=AF.Sin))
            S.op(lambda e: e.activation(out=carg[:], in_=carg[:], func=AF.Sin), se)
            V.wait(se)
            V.op(lambda e: e.tensor_copy(out=sinV[:], in_=rmod[:]))
            V.op(lambda e: e.tensor_copy(out=cosV[:], in_=carg[:]), ve)

            bct = bcT[:, par * 128:(par + 1) * 128]
            V.op(lambda e, bct=bct: e.tensor_tensor(out=sqT[:], in0=bct, in1=bct,
                                                    op=ALU.mult), ve)
            PE.wait(ve)
            PE.op(lambda e: e.matmul(pb[2][0:1, 0:128], onec[0:64, 0:1], sqT[0:64, :],
                                     start=True, stop=True), pe)
            PE.op(lambda e: e.matmul(pb[3][0:1, 0:128], onec[64:128, 0:1],
                                     sqT[64:128, :], start=True, stop=True,
                                     tile_position=(64, 0)), pe)
            S.wait(pe)
            S.op(lambda e: e.activation(out=invB[:], in_=pb[2][0:1, 0:128],
                                        func=AF.Sqrt, scale=1.0 / 64.0, bias=cns[0:1, 0:1]))
            S.op(lambda e: e.activation(out=invC[:], in_=pb[3][0:1, 0:128],
                                        func=AF.Sqrt, scale=1.0 / 64.0, bias=cns[0:1, 0:1]), se)
            V.wait(se)
            V.op(lambda e: e.reciprocal(out=invB[:], in_=invB[:]))
            V.op(lambda e: e.reciprocal(out=invC[:], in_=invC[:]), ve)
            PE.wait(ve)
            PE.op(lambda e: e.matmul(pb[2][0:64, 0:128], onec[0:1, 0:64], invB[:],
                                     start=True, stop=True), pe)
            PE.op(lambda e: e.matmul(pb[3][0:64, 0:128], onec[0:1, 0:64], invC[:],
                                     start=True, stop=True), pe)
            V.wait(pe)
            V.op(lambda e, bct=bct: e.tensor_tensor(out=brawn[:], in0=bct[0:64, :],
                                                    in1=pb[2][0:64, 0:128],
                                                    op=ALU.mult))
            V.op(lambda e, bct=bct: e.tensor_tensor(out=crawn[:], in0=bct[64:128, :],
                                                    in1=pb[3][0:64, 0:128],
                                                    op=ALU.mult))
            V.op(lambda e: e.tensor_copy(
                out=brawn8[:].rearrange("p (r c) -> p r c", r=8),
                in_=brawn[:].unsqueeze(1).broadcast_to([64, 8, 128])))
            V.op(lambda e: e.tensor_copy(
                out=crawn8[:].rearrange("p (r c) -> p r c", r=8),
                in_=crawn[:].unsqueeze(1).broadcast_to([64, 8, 128])), ve)
            PE.wait(ve)
            for half in range(2):
                rw = repl1w if half == 0 else repl2w
                rwc = repl1wc if half == 0 else repl2wc
                dst_b = vB1 if half == 0 else vB2
                dst_c = vC1 if half == 0 else vC2
                for piece in range(2):
                    PE.wait(ve)
                    PE.op(lambda e, rw=rw, piece=piece: e.matmul(
                        pb[2][:, 0:512], rw[:],
                        brawn8[:, piece * 512:(piece + 1) * 512],
                        start=True, stop=True), pe)
                    PE.op(lambda e, rwc=rwc, piece=piece: e.matmul(
                        pb[3][:, 0:512], rwc[:],
                        crawn8[:, piece * 512:(piece + 1) * 512],
                        start=True, stop=True), pe)
                    V.wait(pe)
                    V.op(lambda e, dst_b=dst_b, piece=piece, half=half: e.tensor_tensor(
                        out=dst_b[:, piece * 512:(piece + 1) * 512],
                        in0=bbv_s[:, half * 1024 + piece * 512:
                                  half * 1024 + (piece + 1) * 512],
                        in1=pb[2][:, 0:512], op=ALU.add))
                    V.op(lambda e, dst_c=dst_c, piece=piece, half=half: e.tensor_tensor(
                        out=dst_c[:, piece * 512:(piece + 1) * 512],
                        in0=cbv_s[:, half * 1024 + piece * 512:
                                  half * 1024 + (piece + 1) * 512],
                        in1=pb[3][:, 0:512], op=ALU.add), ve)
            V.wait(se)
            V.wait(pe)
            for (v1, v2) in ((vB1, vB2), (vC1, vC2)):
                V.op(lambda e, v1=v1: e.tensor_tensor(out=rta[:], in0=v1[:],
                                                      in1=cosV[:], op=ALU.mult))
                V.op(lambda e, v2=v2: e.tensor_tensor(out=rtb[:], in0=v2[:],
                                                      in1=sinV[:], op=ALU.mult))
                V.op(lambda e, v2=v2: e.tensor_tensor(out=v2[:], in0=v2[:],
                                                      in1=cosV[:], op=ALU.mult))
                V.op(lambda e, v1=v1: e.tensor_tensor(out=v1[:], in0=v1[:],
                                                      in1=sinV[:], op=ALU.mult))
                V.op(lambda e, v1=v1, v2=v2: e.tensor_tensor(out=v2[:], in0=v2[:],
                                                             in1=v1[:], op=ALU.add))
                V.op(lambda e, v1=v1: e.tensor_tensor(out=v1[:], in0=rta[:],
                                                      in1=rtb[:], op=ALU.subtract), ve)
            V.op(lambda e: e.tensor_tensor(
                out=wvsel[:].rearrange("p (r c) -> p r c", r=8),
                in0=wT[:].unsqueeze(1).broadcast_to([32, 8, 128]),
                in1=selj8s[:].unsqueeze(2).broadcast_to([32, 8, 128]),
                op=ALU.mult), ve)
            PE.wait(ve)
            for piece in range(2):
                PE.op(lambda e, piece=piece: e.matmul(
                    pb[2 + piece][:, 0:512], repl4bf[:],
                    wvsel[:, piece * 512:(piece + 1) * 512],
                    start=True, stop=True), pe)
            V.wait(pe)
            for piece in range(2):
                V.op(lambda e, piece=piece: e.tensor_tensor(
                    out=vBW1[:, piece * 512:(piece + 1) * 512],
                    in0=vB1[:, piece * 512:(piece + 1) * 512],
                    in1=pb[2 + piece][:, 0:512], op=ALU.mult))
                V.op(lambda e, piece=piece: e.tensor_tensor(
                    out=vBW2[:, piece * 512:(piece + 1) * 512],
                    in0=vB2[:, piece * 512:(piece + 1) * 512],
                    in1=pb[2 + piece][:, 0:512], op=ALU.mult), ve)
            for j in range(8):
                for half in range(2):
                    src = vBW1 if half == 0 else vBW2
                    bank = pb[2 + ((j * 2 + half) % 2)]
                    PE.wait(ve)
                    PE.op(lambda e, src=src, j=j, bank=bank: e.matmul(
                        bank[:, 0:128], src[:, j * 128:(j + 1) * 128], Ibf[:],
                        start=True, stop=True), pe)
                    V.wait(pe)
                    V.op(lambda e, j=j, half=half, bank=bank: e.tensor_copy(
                        out=bpp.rearrange("p (jj mm n) -> p jj mm n", jj=8, mm=4)
                            [:, j, :, half * 32:half * 32 + 32],
                        in_=bank[:, 0:128].rearrange("p (m k) -> p m k", m=4)), ve)

            cap_tr = [0, 0]; cap_d2 = [0, 0]; cap_g = [0, 0]
            cap_cb = [0, 0]; cap_yx = [0, 0]; cap_ysc = [0, 0]
            cap_d3v = [0, 0]; cap_ys = [0, 0]
            cap_prev_state = ve.count
            cap_ygw = ve.count
            for m in range(4):
                pbYx = pb[4 + (m % 2)]
                pbYs = pb[6 + (m % 2)]
                pm = slice(32 * m, 32 * m + 32)
                for jj in range(8):
                    h = 4 * jj + m
                    j = jj
                    dpar = jj % 2
                    cbank = pb[jj % 2]
                    V.wait(pe, cap_tr[dpar])
                    V.op(lambda e, h=h, dpar=dpar: e.tensor_tensor(
                        out=pre_t[dpar][:], in0=Mneg[:],
                        in1=packN[:, 64 + h:65 + h].broadcast_to([128, 128]),
                        op=ALU.add), ve)
                    cap_prev = ve.count
                    PE.wait(ve, cap_prev)
                    PE.wait(se, cap_d2[dpar])
                    PE.op(lambda e, dpar=dpar: e.transpose(
                        pb[2 + dpar][:, 0:128], pre_t[dpar][:], I32[:]), pe)
                    cap_tr[dpar] = pe.count
                    PE.wait(ve, cap_g[jj % 2])
                    PE.op(lambda e, pm=pm, j=j, cbank=cbank, m=m: e.matmul(
                        cbank[:, 0:128], vB1[pm, j * 128:(j + 1) * 128],
                        vC1[pm, j * 128:(j + 1) * 128], start=True, stop=False,
                        tile_position=(32 * m, 0)))
                    PE.op(lambda e, pm=pm, j=j, cbank=cbank, m=m: e.matmul(
                        cbank[:, 0:128], vB2[pm, j * 128:(j + 1) * 128],
                        vC2[pm, j * 128:(j + 1) * 128], start=False, stop=True,
                        tile_position=(32 * m, 0)), pe)
                    cap_cb[dpar] = pe.count
                    S.wait(pe, cap_tr[dpar])
                    S.wait(ve, cap_d3v[dpar])
                    S.op(lambda e, dpar=dpar, h=h: e.activation(
                        out=D2[dpar][:], in_=pb[2 + dpar][:, 0:128], func=AF.Exp,
                        bias=packN[:, h:h + 1]), se)
                    cap_d2[dpar] = se.count
                    V.wait(se, cap_d2[dpar])
                    V.op(lambda e, dpar=dpar, h=h: e.scalar_tensor_tensor(
                        out=D2[dpar][:], in0=I32[:], scalar=packN[:, 32 + h:33 + h],
                        in1=D2[dpar][:], op0=ALU.mult, op1=ALU.add))
                    V.wait(pe, cap_cb[dpar])
                    V.wait(pe, cap_yx[dpar])
                    V.op(lambda e, dpar=dpar, cbank=cbank: e.tensor_tensor(
                        out=G_bf[dpar][:], in0=D2[dpar][:], in1=cbank[:, 0:128],
                        op=ALU.mult), ve)
                    cap_g[dpar] = ve.count
                    cap_d3v[dpar] = ve.count
                    PE.wait(ve, max(cap_g[dpar], cap_prev_state))
                    if jj == 0:
                        PE.wait(se, cap_ysc[1])
                    PE.op(lambda e, dpar=dpar, h=h, jj=jj, pbYx=pbYx, par=par:
                          e.matmul(pbYx[:, jj * 64:(jj + 1) * 64], G_bf[dpar][:],
                                   Xsb[:, par * DI + h * 64: par * DI + (h + 1) * 64],
                                   start=True, stop=True, skip_group_check=True),
                          pe)
                    cap_yx[dpar] = pe.count
                    PE.op(lambda e, pm=pm, j=j, jj=jj, pbYs=pbYs, m=m: e.matmul(
                        pbYs[:, jj * 64:(jj + 1) * 64],
                        vC1[pm, j * 128:(j + 1) * 128],
                        state_bf[pm, j * 128:j * 128 + 64], start=True,
                        stop=False, tile_position=(32 * m, 0),
                        skip_group_check=True))
                    PE.op(lambda e, pm=pm, j=j, jj=jj, pbYs=pbYs, m=m: e.matmul(
                        pbYs[:, jj * 64:(jj + 1) * 64],
                        vC2[pm, j * 128:(j + 1) * 128],
                        state_bf[pm, j * 128 + 64:(j + 1) * 128], start=False,
                        stop=True, tile_position=(32 * m, 0),
                        skip_group_check=True), pe)
                    cap_ys[dpar] = pe.count
                    S.wait(pe, cap_ys[dpar])
                    if jj == 0:
                        S.wait(ve, cap_ygw)
                    S.op(lambda e, jj=jj, h=h, pbYs=pbYs: e.activation(
                        out=ysc[:, jj * 64:(jj + 1) * 64],
                        in_=pbYs[:, jj * 64:(jj + 1) * 64],
                        func=AF.Identity, scale=E_nat[:, h:h + 1]), se)
                    cap_ysc[0] = cap_ysc[1]
                    cap_ysc[1] = se.count
                V.wait(se)
                V.op(lambda e, pbYx=pbYx: e.tensor_tensor(
                    out=ysc[:], in0=ysc[:], in1=pbYx[:, 0:512], op=ALU.add))
                V.op(lambda e, m=m, par=par: e.tensor_tensor(
                    out=rmod[:, 0:512].rearrange("p (j q) -> p j q", j=8),
                    in0=Xsb[:, par * DI:(par + 1) * DI].rearrange(
                        "p (j hm q) -> p j hm q", j=8, hm=4)[:, :, m, :],
                    in1=dsk_s[:].rearrange("p (a b) -> p a b", a=4)[:, m, :]
                        .unsqueeze(2).broadcast_to([128, 8, 64]), op=ALU.mult))
                V.op(lambda e: e.tensor_tensor(
                    out=ysc[:], in0=ysc[:], in1=rmod[:, 0:512], op=ALU.add))
                V.op(lambda e, m=m: e.tensor_copy(
                    out=yg[:].rearrange("p (j hm q) -> p j hm q", j=8, hm=4)[:, :, m, :],
                    in_=ysc[:].rearrange("p (j c) -> p j c", j=8)), ve)
                cap_ygw = ve.count

            V.op(lambda e: e.tensor_tensor(out=yg[:], in0=yg[:], in1=sil,
                                           op=ALU.mult), ve)
            PE.wait(ve)
            for m in range(4):
                pm = slice(32 * m, 32 * m + 32)
                for j in range(8):
                    h = 4 * j + m
                    bank = pb[6] if j < 4 else pb[7]
                    jo = (j % 4) * 128
                    PE.op(lambda e, h=h, pm=pm, bank=bank, jo=jo, m=m, par=par:
                          e.matmul(bank[pm, jo:jo + 64],
                                   bpp[:, h * 64:h * 64 + 32],
                                   Xsb[:, par * DI + h * 64: par * DI + (h + 1) * 64],
                                   start=True, stop=True,
                                   tile_position=(0, 32 * m),
                                   skip_group_check=True))
                    PE.op(lambda e, h=h, pm=pm, bank=bank, jo=jo, m=m, par=par:
                          e.matmul(bank[pm, jo + 64:jo + 128],
                                   bpp[:, h * 64 + 32:(h + 1) * 64],
                                   Xsb[:, par * DI + h * 64: par * DI + (h + 1) * 64],
                                   start=True, stop=True,
                                   tile_position=(0, 32 * m),
                                   skip_group_check=True),
                          pe if (j in (3, 7)) else None)
            V.wait(pe)
            V.op(lambda e: e.tensor_tensor(
                out=state[:].rearrange("p (j c) -> p j c", j=8),
                in0=state[:].rearrange("p (j c) -> p j c", j=8),
                in1=dec4[:].unsqueeze(2).broadcast_to([128, 8, 128]), op=ALU.mult))
            V.op(lambda e: e.tensor_tensor(out=state[:, 0:512], in0=state[:, 0:512],
                                           in1=pb[6][:, 0:512], op=ALU.add))
            V.op(lambda e: e.tensor_tensor(out=state[:, 512:1024],
                                           in0=state[:, 512:1024],
                                           in1=pb[7][:, 0:512], op=ALU.add))
            V.op(lambda e: e.tensor_copy(out=state_bf[:], in_=state[:]), ve)

            V.op(lambda e: e.tensor_tensor(out=rmod[:], in0=yg[:, 0:1024],
                                           in1=yg[:, 0:1024], op=ALU.mult))
            V.op(lambda e: e.tensor_tensor(out=rln[:], in0=yg[:, 1024:2048],
                                           in1=yg[:, 1024:2048], op=ALU.mult))
            V.op(lambda e: e.tensor_reduce(out=cols[:, 0:1], in_=rmod[:], axis=AX.X,
                                           op=ALU.add))
            V.op(lambda e: e.tensor_reduce(out=cols[:, 1:2], in_=rln[:], axis=AX.X,
                                           op=ALU.add), ve)
            S.wait(ve)
            S.op(lambda e: e.activation(out=cols[:, 12:13], in_=cols[:, 0:1],
                                        func=AF.Identity, bias=cols[:, 1:2]))
            S.op(lambda e: e.activation(out=cols[:, 1:2], in_=cols[:, 12:13],
                                        func=AF.Sqrt, scale=1.0 / DI,
                                        bias=cns[:, 0:1]), se)
            V.wait(se)
            V.op(lambda e: e.reciprocal(out=cols[:, 13:14], in_=cols[:, 1:2]), ve)
            S.wait(ve)
            S.op(lambda e: e.activation(out=cols[:, 2:3], in_=cols[:, 13:14],
                                        func=AF.Identity), se)
            V.wait(se)
            V.op(lambda e: e.tensor_tensor(
                out=yg[:], in0=yg[:],
                in1=cols[:, 2:3].broadcast_to([128, DI]), op=ALU.mult))
            V.op(lambda e: e.tensor_tensor(out=yg[:], in0=yg[:], in1=nwrep_s[:],
                                           op=ALU.mult), ve)
            for k in range(16):
                bank = pb[2 + (k % 2)]
                ytd = vB1 if k < 8 else vB2
                ko = k % 8
                PE.wait(ve)
                PE.wait(se)
                PE.op(lambda e, k=k, bank=bank: e.matmul(
                    bank[:, 0:128], yg[:, k * 128:(k + 1) * 128], Ibf[:],
                    start=True, stop=True), pe)
                V.wait(pe)
                V.op(lambda e, ko=ko, bank=bank, ytd=ytd: e.tensor_copy(
                    out=ytd[:, ko * 128:(ko + 1) * 128],
                    in_=bank[:, 0:128]), ve)
            for k in range(16):
                ytd = vB1 if k < 8 else vB2
                ko = k % 8
                GQ.wait(pe)
                GQ.op(lambda e, k=k: e.dma_start(
                    out=wob[k % 2][:], in_=woutTbf_d[k * 128:(k + 1) * 128, :]),
                    gq, 16)
                PE.wait(gq)
                PE.wait(ve)
                for nh in range(2):
                    PE.op(lambda e, ko=ko, nh=nh, k=k, ytd=ytd: e.matmul(
                        pb[nh][:, 0:512], ytd[:, ko * 128:(ko + 1) * 128],
                        wob[k % 2][:, nh * 512:(nh + 1) * 512],
                        start=(k == 0), stop=(k == 15)), pe)
            SY.wait(ve)
            SY.op(lambda e, c=c, par=par: e.dma_start(
                out=usb[par][:], in_=u_d[c * 128:(c + 1) * 128, :]), sy, 16)
            V.wait(sy)
            V.wait(pe)
            V.op(lambda e, par=par: e.tensor_tensor(
                out=rln[:, 0:512], in0=usb[par][:, 0:512], in1=pb[0][:, 0:512],
                op=ALU.add))
            V.op(lambda e, par=par: e.tensor_tensor(
                out=rln[:, 512:1024], in0=usb[par][:, 512:1024], in1=pb[1][:, 0:512],
                op=ALU.add))
            V.op(lambda e: e.tensor_reduce(out=cols[:, 2:3], in_=rln[:], axis=AX.X,
                                           op=ALU.add), ve)
            S.wait(ve)
            S.op(lambda e: e.activation(out=cols[:, 4:5], in_=cols[:, 2:3],
                                        func=AF.Identity, scale=1.0 / DM), se)
            V.wait(se)
            V.op(lambda e: e.tensor_tensor(
                out=xsq[:], in0=rln[:],
                in1=cols[:, 4:5].broadcast_to([128, DM]), op=ALU.subtract))
            V.wait(sr)
            V.op(lambda e: e.tensor_tensor(out=xln[:], in0=xsq[:], in1=xsq[:],
                                           op=ALU.mult))
            V.op(lambda e: e.tensor_reduce(out=cols[:, 5:6], in_=xln[:], axis=AX.X,
                                           op=ALU.add), ve)
            S.wait(ve)
            S.op(lambda e: e.activation(out=cols[:, 8:9], in_=cols[:, 5:6],
                                        func=AF.Sqrt, scale=1.0 / DM,
                                        bias=cns[:, 0:1]), se)
            V.wait(se)
            V.op(lambda e: e.reciprocal(out=cols[:, 9:10], in_=cols[:, 8:9]), ve)
            S.wait(ve)
            S.op(lambda e: e.activation(out=cols[:, 10:11], in_=cols[:, 9:10],
                                        func=AF.Identity), se)
            V.wait(se)
            V.op(lambda e: e.tensor_tensor(
                out=xln[:], in0=xsq[:],
                in1=cols[:, 10:11].broadcast_to([128, DM]), op=ALU.mult), ve)
            SY.wait(ve)
            SY.op(lambda e, c=c: e.dma_start(
                out=r1scr_d[c * 128:(c + 1) * 128, :], in_=xln[:]), sr, 16)
            for k in range(8):
                bank = pb[2 + (k % 2)]
                PE.wait(ve)
                PE.op(lambda e, k=k, bank=bank: e.transpose(
                    bank[:, 0:128], xln[:, k * 128:(k + 1) * 128], I32[:]), pe)
                V.wait(pe)
                V.op(lambda e, bank=bank: e.tensor_copy(out=xt32[:],
                                                        in_=bank[:, 0:128]))
                V.op(lambda e, k=k, c=c, bank=bank: e.tensor_copy(
                    out=xT[:, k * L + c * 128:k * L + (c + 1) * 128],
                    in_=bank[:, 0:128]), ve)
                PE.wait(ve)
                PE.op(lambda e, k=k: e.matmul(
                    pb[4][0:8, 0:128], wr_s[:, k * 8:(k + 1) * 8], xt32[:],
                    start=(k == 0), stop=(k == 7)), pe if k == 7 else None)
            S.wait(pe)
            S.op(lambda e: e.activation(out=logT[:], in_=pb[4][0:8, 0:128],
                                        func=AF.Identity, bias=br_s[:]), se)
            PE.wait(se)
            PE.wait(ve)
            PE.op(lambda e: e.transpose(pb[5][:, 0:8], logT[:], I32[0:8, 0:8]), pe)
            V.wait(pe)
            V.op(lambda e: e.tensor_reduce(out=cols[:, 14:15], in_=pb[5][:, 0:8],
                                           axis=AX.X, op=ALU.max), ve)
            S.wait(ve)
            S.op(lambda e: e.activation(out=cols[:, 13:14], in_=cols[:, 14:15],
                                        func=AF.Identity), se)
            V.wait(se)
            V.op(lambda e: e.tensor_tensor(
                out=xt32[:, 8:16], in0=pb[5][:, 0:8],
                in1=cols[:, 13:14].broadcast_to([128, 8]), op=ALU.is_ge))
            V.op(lambda e: e.tensor_tensor(out=cols[:, 15:16], in0=cols[:, 13:14],
                                           in1=cns[:, 4:5], op=ALU.mult), ve)
            S.wait(ve)
            S.op(lambda e: e.activation(out=xt32[:, 0:8], in_=pb[5][:, 0:8],
                                        func=AF.Exp, bias=cols[:, 15:16],
                                        accum_out=cols[:, 11:12]), se)
            V.wait(se)
            V.op(lambda e: e.reciprocal(out=cols[:, 12:13], in_=cols[:, 11:12]), ve)
            S.wait(ve)
            S.op(lambda e: e.activation(out=cols[:, 10:11], in_=cols[:, 12:13],
                                        func=AF.Identity), se)
            V.wait(se)
            V.op(lambda e, c=c: e.tensor_tensor(
                out=gsel[:, c * 8:(c + 1) * 8], in0=xt32[:, 8:16],
                in1=cols[:, 10:11].broadcast_to([128, 8]), op=ALU.mult), ve)

        emit_wcasts(20)
        proj_stage(0)
        nssd = NCH if upto is None else min(NCH, upto)
        for c in range(nssd):
            if c + 1 < NCH:
                proj_stage(c + 1)
            ssd_stage(c)
            emit_wcasts(33)

        emit_wcasts(len(wcast_jobs))
        GQ.wait(gq)
        PE.wait(gq)
        if upto is None:
            for tg in range(4):
                V.op(lambda e: e.memset(acc[:], 0.0), ve)
                for e_i in range(E):
                    par_e = e_i % 2
                    SY.wait(se)
                    SY.op(lambda e, e_i=e_i, par_e=par_e: e.dma_start(
                        out=b1c_s[par_e][:], in_=b1_d[e_i]), sy, 16)
                    for ft in range(32):
                        par = ft % 4
                        GQ.wait(pe, pe.count if ft == 0 else pe.count - 3)
                        GQ.op(lambda e, e_i=e_i, ft=ft, par=par: e.dma_start(
                            out=w1bf4[par][:], in_=w1bf_d[e_i, ft]), gq, 16)
                        bank = pb[ft % 4]
                        PE.wait(gq)
                        PE.wait(se)
                        for k in range(8):
                            PE.op(lambda e, k=k, par=par, bank=bank, tg=tg: e.matmul(
                                bank[:, 0:512], w1bf4[par][:, k * 128:(k + 1) * 128],
                                xT[:, k * L + tg * 512:k * L + (tg + 1) * 512],
                                start=(k == 0), stop=(k == 7)),
                                pe if k == 7 else None)
                        S.wait(pe)
                        S.wait(sy)
                        S.op(lambda e, ft=ft, bank=bank, par_e=par_e: e.activation(
                            out=h1T[:, ft * 512:(ft + 1) * 512], in_=bank[:, 0:512],
                            func=AF.Relu, bias=b1c_s[par_e][:, ft:ft + 1]), se)
                    for fs in range(33):
                        par = fs % 4
                        GQ.wait(pe, pe.count if fs == 0 else pe.count - 3)
                        GQ.op(lambda e, e_i=e_i, fs=fs, par=par: e.dma_start(
                            out=w2bf4[par][:], in_=w2bf_d[e_i, fs]), gq, 16)
                        PE.wait(gq)
                        if fs == 0:
                            PE.wait(se)
                        for tt in range(4):
                            for nh in range(2):
                                PE.op(lambda e, fs=fs, tt=tt, nh=nh, par=par: e.matmul(
                                    pb[tt * 2 + nh][:, 0:512],
                                    h1T[:, fs * 512 + tt * 128:
                                        fs * 512 + (tt + 1) * 128],
                                    w2bf4[par][:, nh * 512:(nh + 1) * 512],
                                    start=(fs == 0), stop=(fs == 32)),
                                    pe if (tt == 3 and nh == 1) else None)
                    V.wait(pe)
                    for tt in range(4):
                        ci = tg * 4 + tt
                        for nh in range(2):
                            V.op(lambda e, tt=tt, nh=nh, ci=ci, e_i=e_i:
                                 e.scalar_tensor_tensor(
                                     out=acc[:, tt * DM + nh * 512:
                                            tt * DM + (nh + 1) * 512],
                                     in0=pb[tt * 2 + nh][:, 0:512],
                                     scalar=gsel[:, ci * 8 + e_i:ci * 8 + e_i + 1],
                                     in1=acc[:, tt * DM + nh * 512:
                                            tt * DM + (nh + 1) * 512],
                                     op0=ALU.mult, op1=ALU.add), ve)
                if e_i == E - 1:
                    for tt in range(4):
                        ci = tg * 4 + tt
                        par = tt % 2
                        SY.wait(ve)
                        SY.op(lambda e, ci=ci, par=par: e.dma_start(
                            out=usb[par][:], in_=r1scr_d[ci * 128:(ci + 1) * 128, :]),
                            sy, 16)
                        V.wait(sy)
                        V.op(lambda e, tt=tt, par=par: e.tensor_tensor(
                            out=rln[:], in0=usb[par][:],
                            in1=acc[:, tt * DM:(tt + 1) * DM], op=ALU.add))
                        V.op(lambda e: e.tensor_reduce(
                            out=cols[:, 2:3], in_=rln[:], axis=AX.X, op=ALU.add), ve)
                        S.wait(ve)
                        S.op(lambda e: e.activation(
                            out=cols[:, 4:5], in_=cols[:, 2:3], func=AF.Identity,
                            scale=1.0 / DM), se)
                        V.wait(se)
                        V.op(lambda e: e.tensor_tensor(
                            out=xsq[:], in0=rln[:],
                            in1=cols[:, 4:5].broadcast_to([128, DM]),
                            op=ALU.subtract))
                        V.op(lambda e: e.tensor_tensor(
                            out=w1st[0][:], in0=xsq[:, 0:512], in1=xsq[:, 0:512],
                            op=ALU.mult))
                        V.op(lambda e: e.tensor_tensor(
                            out=w1st[1][:], in0=xsq[:, 512:1024],
                            in1=xsq[:, 512:1024], op=ALU.mult))
                        V.op(lambda e: e.tensor_reduce(
                            out=cols[:, 5:6], in_=w1st[0][:], axis=AX.X, op=ALU.add))
                        V.op(lambda e: e.tensor_reduce(
                            out=cols[:, 6:7], in_=w1st[1][:], axis=AX.X, op=ALU.add),
                            ve)
                        S.wait(ve)
                        S.op(lambda e: e.activation(
                            out=cols[:, 7:8], in_=cols[:, 5:6], func=AF.Identity,
                            bias=cols[:, 6:7]))
                        S.op(lambda e: e.activation(
                            out=cols[:, 10:11], in_=cols[:, 7:8], func=AF.Sqrt,
                            scale=1.0 / DM, bias=cns[:, 0:1]), se)
                        V.wait(se)
                        V.op(lambda e: e.reciprocal(out=cols[:, 11:12],
                                                    in_=cols[:, 10:11]), ve)
                        S.wait(ve)
                        S.op(lambda e: e.activation(
                            out=cols[:, 12:13], in_=cols[:, 11:12],
                            func=AF.Identity), se)
                        V.wait(se)
                        V.wait(so)
                        V.op(lambda e, par=par: e.tensor_tensor(
                            out=xo32[par][:], in0=xsq[:],
                            in1=cols[:, 12:13].broadcast_to([128, DM]),
                            op=ALU.mult), ve)
                        SY.wait(ve)
                        SY.op(lambda e, ci=ci, par=par: e.dma_start(
                            out=out_d[ci * 128:(ci + 1) * 128, :], in_=xo32[par][:]),
                            so, 16)
            SY.wait(so)

        if taps:
            SY.wait(ve); SY.wait(se); SY.wait(pe); SY.wait(gq, gq.count)
            for tname, sname, shape in taps:
                t = SBUF[sname]
                td = dt("tap_" + tname, list(shape), t.dtype,
                        kind="ExternalOutput").ap()
                SY.op(lambda e, t=t, td=td, shape=shape: e.dma_start(
                    out=td, in_=t[0:shape[0], 0:shape[1]]), so, 16)
            SY.wait(so)

        with nc.Block() as block:
            @block.sync
            def _(eng):
                SY.replay(eng)

            @block.gpsimd
            def _(eng):
                GQ.replay(eng)

            @block.tensor
            def _(eng):
                PE.replay(eng)

            @block.vector
            def _(eng):
                V.replay(eng)

            @block.scalar
            def _(eng):
                S.replay(eng)

    return nc


def _host_inputs(inputs):
    u = np.asarray(inputs["L0_attn_u"], np.float32)
    ipw = np.asarray(inputs["L0_attn_in_proj_w"], np.float32)
    perm = np.concatenate([
        np.arange(0, 2048), np.arange(2048, 4096), np.arange(4096, 4224),
        np.arange(4224, 4256), np.arange(5280, 5312), np.arange(4256, 5280)])
    winT = np.ascontiguousarray(ipw[perm].T)
    woutT = np.ascontiguousarray(
        np.asarray(inputs["L0_attn_out_proj_w"], np.float32).T)
    W1 = np.asarray(inputs["L0_ffn_W1"], np.float32)
    w1r = np.ascontiguousarray(
        W1.reshape(E, 8, 128, 32, 128).transpose(0, 3, 1, 2, 4))
    W2 = np.asarray(inputs["L0_ffn_W2"], np.float32)
    b2 = np.asarray(inputs["L0_ffn_b2"], np.float32)
    w2p = np.zeros((E, 33, 128, DM), np.float32)
    w2p[:, :32] = W2.reshape(E, 32, 128, DM)
    w2p[:, 32, 0, :] = b2
    b1 = np.asarray(inputs["L0_ffn_b1"], np.float32)
    b1c = np.ascontiguousarray(b1.reshape(E, 32, 128).transpose(0, 2, 1))
    Bb = np.asarray(inputs["L0_attn_B_bias"], np.float32)
    Cb = np.asarray(inputs["L0_attn_C_bias"], np.float32)

    def vint(bias):
        outs = np.zeros((2, 128, 1024), np.float32)
        m2g, kg = np.divmod(np.arange(128), 32)
        for half in range(2):
            for j in range(8):
                outs[half, :, 128 * j:128 * (j + 1)] = \
                    bias[4 * j + m2g, half * 32 + kg][:, None]
        return outs

    nwrep = np.ascontiguousarray(np.broadcast_to(
        np.asarray(inputs["L0_attn_norm_w"], np.float32)[None, :], (128, DI)))
    ii, jjx = np.meshgrid(np.arange(128), np.arange(128), indexing="ij")
    mneg = np.where(ii < jjx, -1e30, 0.0).astype(np.float32)
    kk = np.arange(128) % 32
    shared = dict(
        winT=winT, woutT=woutT,
        wr=np.ascontiguousarray(np.asarray(inputs["L0_ffn_Wr"], np.float32)),
        br=np.asarray(inputs["L0_ffn_br"], np.float32).reshape(E, 1),
        w1=w1r, b1c=b1c, w2=w2p,
        dtb=np.asarray(inputs["L0_attn_dt_bias"], np.float32).reshape(H, 1),
        alog=np.asarray(inputs["L0_attn_A_log"], np.float32).reshape(H, 1),
        dsk4=np.ascontiguousarray(np.broadcast_to(
            np.asarray(inputs["L0_attn_D"], np.float32).reshape(-1)[
                (4 * np.arange(8)[None, :] + np.arange(4)[:, None])][None, :, :],
            (128, 4, 8))),
        bnw=np.asarray(inputs["L0_attn_B_norm_w"], np.float32).reshape(N, 1),
        cnw=np.asarray(inputs["L0_attn_C_norm_w"], np.float32).reshape(N, 1),
        bbv=vint(Bb), cbv=vint(Cb), nwrep=nwrep,
        ident=np.eye(128, dtype=np.float32), mneg=mneg,
        repl1=(np.arange(N)[:, None] == kk[None, :]).astype(np.float32),
        repl2=(np.arange(N)[:, None] == (32 + kk)[None, :]).astype(np.float32),
        repl4=((np.arange(H)[:, None] % 4) ==
               (np.arange(128)[None, :] // 32)).astype(np.float32),
        selj8=((np.arange(H)[:, None] // 4) ==
               np.arange(8)[None, :]).astype(np.float32),
    )
    in_maps = []
    for core in range(8):
        d = dict(shared)
        d["u"] = np.ascontiguousarray(u[core])
        in_maps.append(d)
    return in_maps


def kernel(**inputs):
    in_maps = _host_inputs(inputs)
    nc = build_program()
    res = run_bass_kernel_spmd(nc, in_maps, core_ids=list(range(8)))
    out = np.stack([np.asarray(r["out"], np.float32) for r in res.results], axis=0)
    return out


# version 05
# speedup vs baseline: 2.0241x; 1.0073x over previous
import math
import numpy as np
from contextlib import ExitStack

import concourse.bass as bass
import concourse.mybir as mybir
from concourse.bass_utils import run_bass_kernel_spmd

F32 = mybir.dt.float32
BF16 = mybir.dt.bfloat16
AF = mybir.ActivationFunctionType
ALU = mybir.AluOpType
AX = mybir.AxisListType

L = 2048
DM = 1024
H = 32
P = 64
N = 64
DI = H * P
NCH = 16
Q = 128
DPROJ = 5312
E = 8
DFF = 4096
TWO_PI = 2.0 * math.pi
MAGIC = 1.5 * (2.0 ** 23)
PI = math.pi
PIC = math.pi * (1.0 - 1e-6)

ZC0, XC0, BCC0, DLC0, THC0 = 0, 2048, 4096, 4224, 4288


class Sem:
    def __init__(self, handle):
        self.h = handle
        self.count = 0


class Emitter:
    def __init__(self, name):
        self.name = name
        self.items = []
        self._waited = {}

    def wait(self, sem, val=None):
        v = sem.count if val is None else val
        if self._waited.get(id(sem), -1) >= v or v <= 0:
            return
        self._waited[id(sem)] = v
        self.items.append(lambda e, s=sem, v=v: e.wait_ge(s.h, v))

    def op(self, fn, inc=None, amt=1):
        if inc is not None:
            inc.count += amt
            def run(e, fn=fn, inc=inc, amt=amt):
                fn(e).then_inc(inc.h, amt)
        else:
            def run(e, fn=fn):
                fn(e)
        self.items.append(run)

    def replay(self, eng):
        for it in self.items:
            it(eng)


def build_program(upto=None, taps=()):
    nc = bass.Bass("TRN2", target_bir_lowering=False)
    dt = nc.dram_tensor

    u_d = dt("u", [L, DM], F32, kind="ExternalInput").ap()
    winT_d = dt("winT", [DM, DPROJ], F32, kind="ExternalInput").ap()
    woutT_d = dt("woutT", [DI, DM], F32, kind="ExternalInput").ap()
    wr_d = dt("wr", [DM, E], F32, kind="ExternalInput").ap()
    br_d = dt("br", [E, 1], F32, kind="ExternalInput").ap()
    w1_d = dt("w1", [E, 32, 8, 128, 128], F32, kind="ExternalInput").ap()
    b1_d = dt("b1c", [E, 128, 32], F32, kind="ExternalInput").ap()
    w2_d = dt("w2", [E, 33, 128, DM], F32, kind="ExternalInput").ap()
    dtb_d = dt("dtb", [H, 1], F32, kind="ExternalInput").ap()
    alog_d = dt("alog", [H, 1], F32, kind="ExternalInput").ap()
    dsk_d = dt("dsk4", [128, 4, 8], F32, kind="ExternalInput").ap()
    bnw_d = dt("bnw", [N, 1], F32, kind="ExternalInput").ap()
    cnw_d = dt("cnw", [N, 1], F32, kind="ExternalInput").ap()
    bbv_d = dt("bbv", [2, 128, 1024], F32, kind="ExternalInput").ap()
    cbv_d = dt("cbv", [2, 128, 1024], F32, kind="ExternalInput").ap()
    nwrep_d = dt("nwrep", [128, DI], F32, kind="ExternalInput").ap()
    ident_d = dt("ident", [128, 128], F32, kind="ExternalInput").ap()
    mneg_d = dt("mneg", [128, 128], F32, kind="ExternalInput").ap()
    repl1_d = dt("repl1", [N, 128], F32, kind="ExternalInput").ap()
    repl2_d = dt("repl2", [N, 128], F32, kind="ExternalInput").ap()
    repl4_d = dt("repl4", [H, 128], F32, kind="ExternalInput").ap()
    selj8_d = dt("selj8", [H, 8], F32, kind="ExternalInput").ap()
    out_d = dt("out", [L, DM], F32, kind="ExternalOutput").ap()

    winTbf_d = dt("winTbf", [DM, DPROJ], BF16, kind="Internal").ap()
    woutTbf_d = dt("woutTbf", [DI, DM], BF16, kind="Internal").ap()
    r1scr_d = dt("r1scr", [L, DM], F32, kind="Internal").ap()
    w1bf_d = dt("w1bfd", [E, 32, 128, 1024], BF16, kind="Internal").ap()
    w2bf_d = dt("w2bfd", [E, 33, 128, DM], BF16, kind="Internal").ap()


    st = ExitStack()
    SBUF = {}

    def sb(n, sh, d=F32):
        t = st.enter_context(nc.sbuf_tensor(n, sh, d))
        SBUF[n] = t
        return t

    ps = lambda n, sh: st.enter_context(nc.psum_tensor(n, sh, F32))
    mksem = lambda n: Sem(st.enter_context(nc.semaphore(n)))

    with st:
        xT = sb("xT", [128, 8 * L], BF16)
        acc = sb("acc", [128, 4 * DM], BF16)
        h1T = sb("h1T", [128, 33 * 512], BF16)
        zsb = h1T[:, 0:4096]
        Xsb = h1T[:, 4096:8192]
        tcV = h1T[:, 8192:12288].bitcast(F32)
        bpp = h1T[:, 12288:14336]
        sil = h1T[:, 14336:16384]
        bcT = sb("bcT", [128, 2 * 128], F32)
        dtT = sb("dtT", [32, 2 * 128], F32)
        lamT = sb("lamT", [32, 2 * 128], F32)
        g_t = sb("g_t", [32, 2 * 128], F32)
        q_t = sb("q_t", [32, 2 * 128], F32)
        wgb = [sb(f"wgb{i}", [128, 8 * 256], BF16) for i in range(4)]
        wob = [sb(f"wob{i}", [128, DM], BF16) for i in range(2)]
        usb = [sb(f"usb{i}", [128, DM], F32) for i in range(2)]

        ghat = sb("ghat", [32, 128], F32)
        s_t = sb("s_t", [32, 128], F32)
        S_T = sb("S_T", [32, 128], F32)
        lng = sb("lng", [32, 128], F32)
        wT = sb("wT", [32, 128], F32)
        expd = sb("expd", [32, 128], F32)
        pack = sb("pack", [128, 128], F32)
        packN = sb("packN", [128, 128], F32)
        E_nat = sb("E_nat", [128, 32], F32)
        dec_c = sb("dec_c", [32, 1], F32)
        dec4i = sb("dec4i", [32, 8], F32)
        dec4 = sb("dec4", [128, 8], F32)
        wvsel = sb("wvsel", [32, 8 * 128], BF16)
        state = sb("state", [128, 1024], F32)
        state_bf = sb("state_bf", [128, 1024], BF16)
        sqT = sb("sqT", [128, 128], F32)
        invB = sb("invB", [1, 128], F32)
        invC = sb("invC", [1, 128], F32)
        brawn = sb("brawn", [64, 128], BF16)
        brawn8 = sb("brawn8", [64, 8 * 128], BF16)
        crawn = sb("crawn", [64, 128], BF16)
        crawn8 = sb("crawn8", [64, 8 * 128], BF16)
        vB1 = sb("vB1", [128, 1024], BF16)
        vB2 = sb("vB2", [128, 1024], BF16)
        vC1 = sb("vC1", [128, 1024], BF16)
        vC2 = sb("vC2", [128, 1024], BF16)
        rta = sb("rta", [128, 1024], BF16)
        rtb = sb("rtb", [128, 1024], BF16)
        vBW1 = rta
        vBW2 = rtb
        rmod = sb("rmod", [128, 1024], F32)
        cosV = sb("cosV", [128, 1024], BF16)
        sinV = sb("sinV", [128, 1024], BF16)
        pre_t = [sb(f"pre{i}", [128, 128], F32) for i in range(2)]
        D2 = [sb(f"D2_{i}", [128, 128], F32) for i in range(2)]
        G_bf = [sb(f"G_{i}", [128, 128], BF16) for i in range(2)]
        ysc = sb("ysc", [128, 512], F32)
        yg = sb("yg", [128, DI], BF16)
        I32 = sb("I32", [128, 128], F32)
        Ibf = sb("Ibf", [128, 128], BF16)
        Mneg = sb("Mneg", [128, 128], F32)
        rp1 = sb("rp1", [64, 128], F32)
        rp2 = sb("rp2", [64, 128], F32)
        repl1w = sb("repl1w", [64, 128], BF16)
        repl2w = sb("repl2w", [64, 128], BF16)
        repl1wc = sb("repl1wc", [64, 128], BF16)
        repl2wc = sb("repl2wc", [64, 128], BF16)
        repl4s = sb("repl4s", [32, 128], F32)
        repl4bf = sb("repl4bf", [32, 128], BF16)
        selj8s = sb("selj8s", [32, 8], F32)
        bbv_s = sb("bbv_s", [128, 2 * 1024], BF16)
        cbv_s = sb("cbv_s", [128, 2 * 1024], BF16)
        nwrep_s = sb("nwrep_s", [128, DI], BF16)
        onec = sb("onec", [128, 128], F32)
        a_col = sb("a_col", [32, 1], F32)
        acol_t = sb("acol_t", [32, 1], F32)
        dtb_s = sb("dtb_s", [32, 1], F32)
        dsk_s = sb("dsk_s", [128, 32], F32)
        bnw_s = sb("bnw_s", [64, 1], F32)
        cnw_s = sb("cnw_s", [64, 1], F32)
        wr_s = sb("wr_s", [128, 8 * 8], F32)
        br_s = sb("br_s", [8, 1], F32)
        zcol = sb("zcol", [32, 1], F32)
        rln = sb("rln", [128, DM], F32)
        xln = sb("xln", [128, DM], F32)
        carg = rln
        atmp = xln
        xsq = rmod
        cst32 = [rln, xln]
        cstbf_t = None
        xsq = sb("xsq", [128, DM], F32)
        xt32 = sb("xt32", [128, 128], F32)
        logT = sb("logT", [8, 128], F32)
        cols = sb("cols", [128, 16], F32)
        cns = sb("cns", [128, 6], F32)
        gsel = sb("gsel", [128, 8 * 16], F32)
        w1st = [sb(f"w1st{i}", [128, 512], F32) for i in range(4)]
        w1bf = [sb(f"w1bf{i}", [128, 1024], BF16) for i in range(2)]
        w1bf4 = [sb(f"w1bf4_{i}", [128, 1024], BF16) for i in range(4)]
        w2bf4 = [w1bf4[0], w1bf4[1], w1bf4[2], w1bf4[3]]
        b1c_s = [sb(f"b1c{i}", [128, 32], F32) for i in range(2)]
        xo32 = [xln, xln]

        pb = [ps(f"pb{i}", [128, 512]) for i in range(8)]

        for _n, _a in (("zsb", zsb), ("Xsb", Xsb), ("tcV", tcV),
                       ("bpp", bpp), ("sil", sil)):
            SBUF[_n] = _a
        sy = mksem("sy")
        sr = mksem("sr")
        gq = mksem("gq")
        pe = mksem("pe")
        ve = mksem("ve")
        se = mksem("se")
        so = mksem("so")

        SY = Emitter("sync"); GQ = Emitter("gpsimd"); PE = Emitter("tensor")
        V = Emitter("vector"); S = Emitter("scalar")

        for a, b in [(I32, ident_d), (Mneg, mneg_d), (repl4s, repl4_d),
                     (selj8s, selj8_d), (a_col, alog_d), (dtb_s, dtb_d),
                     (dsk_s, dsk_d.rearrange("p a b -> p (a b)")), (bnw_s, bnw_d), (cnw_s, cnw_d),
                     (br_s, br_d), (rp1, repl1_d),
                     (rp2, repl2_d)]:
            SY.op(lambda e, a=a, b=b: e.dma_start(out=a[:], in_=b), sy, 16)
        SY.op(lambda e: e.dma_start(out=wr_s[:].rearrange("p (k c) -> p k c", k=8),
                                    in_=wr_d.rearrange("(k p) c -> p k c", p=128)), sy, 16)

        V.wait(sy)
        V.op(lambda e: e.memset(onec[:], 1.0))
        V.op(lambda e: e.tensor_copy(out=Ibf[:], in_=I32[:]))
        V.op(lambda e: e.tensor_copy(out=repl4bf[:], in_=repl4s[:]))
        V.op(lambda e: e.tensor_scalar(out=repl1w[:], in0=rp1[:], scalar1=bnw_s[:],
                                       scalar2=None, op0=ALU.mult))
        V.op(lambda e: e.tensor_scalar(out=repl2w[:], in0=rp2[:], scalar1=bnw_s[:],
                                       scalar2=None, op0=ALU.mult))
        V.op(lambda e: e.tensor_scalar(out=repl1wc[:], in0=rp1[:], scalar1=cnw_s[:],
                                       scalar2=None, op0=ALU.mult))
        V.op(lambda e: e.tensor_scalar(out=repl2wc[:], in0=rp2[:], scalar1=cnw_s[:],
                                       scalar2=None, op0=ALU.mult))
        V.op(lambda e: e.memset(zcol[:], 0.0))
        V.op(lambda e: e.memset(cns[:, 0:1], 1e-5))
        V.op(lambda e: e.memset(cns[:, 1:2], 1.0))
        V.op(lambda e: e.memset(cns[:, 2:3], 1.0 / DI))
        V.op(lambda e: e.memset(cns[:, 3:4], 1.0 / DM))
        V.op(lambda e: e.memset(cns[:, 4:5], -1.0))
        V.op(lambda e: e.memset(state[:], 0.0))
        V.op(lambda e: e.memset(state_bf[:], 0.0))
        V.op(lambda e: e.memset(h1T[:, 32 * 512:33 * 512], 0.0))
        V.op(lambda e: e.memset(h1T[0:1, 32 * 512:32 * 512 + 512], 1.0), ve)
        S.wait(sy)
        S.op(lambda e: e.activation(out=acol_t[:], in_=a_col[:], func=AF.Exp), se)
        V.wait(se)
        V.op(lambda e: e.tensor_scalar(out=a_col[:], in0=acol_t[:], scalar1=-1.0,
                                       scalar2=None, op0=ALU.mult), ve)

        cast_jobs = []
        wpieces = [(i * 1024, 1024) for i in range(5)] + [(5120, 192)]
        for k in range(8):
            for s0, w in wpieces:
                cast_jobs.append((winT_d[k * 128:(k + 1) * 128, s0:s0 + w],
                                  winTbf_d[k * 128:(k + 1) * 128, s0:s0 + w],
                                  None, w))
        for k in range(16):
            cast_jobs.append((woutT_d[k * 128:(k + 1) * 128, :],
                              woutTbf_d[k * 128:(k + 1) * 128, :], None, DM))
        for half in range(2):
            cast_jobs.append((bbv_d[half], None,
                              bbv_s[:, half * 1024:(half + 1) * 1024], 1024))
            cast_jobs.append((cbv_d[half], None,
                              cbv_s[:, half * 1024:(half + 1) * 1024], 1024))
            cast_jobs.append((nwrep_d[:, half * 1024:(half + 1) * 1024], None,
                              nwrep_s[:, half * 1024:(half + 1) * 1024], 1024))

        def emit_cast_job(srcap, dstap, sdst, w, par):
            GQ.wait(ve)
            GQ.op(lambda e, s=srcap, w=w, par=par: e.dma_start(
                out=cst32[par][:, 0:w], in_=s), gq, 16)
            V.wait(gq)
            if sdst is None:
                V.op(lambda e, w=w, par=par: e.tensor_copy(
                    out=yg[:, par * 1024:par * 1024 + w],
                    in_=cst32[par][:, 0:w]), ve)
                GQ.wait(ve)
                GQ.op(lambda e, d=dstap, w=w, par=par: e.dma_start(
                    out=d, in_=yg[:, par * 1024:par * 1024 + w]), gq, 16)
            else:
                V.op(lambda e, sdst=sdst, w=w, par=par: e.tensor_copy(
                    out=sdst, in_=cst32[par][:, 0:w]), ve)

        for idx, job in enumerate(cast_jobs):
            emit_cast_job(*job, idx % 2)
        scratch_done = gq.count

        wcast_jobs = []
        for e_i in range(E):
            for ft in range(32):
                wcast_jobs.append((
                    ("w1", w1_d[e_i, ft].transpose([1, 0, 2])),
                    w1bf_d[e_i, ft], 1024))
            for fs in range(33):
                wcast_jobs.append((("w2", w2_d[e_i, fs]), w2bf_d[e_i, fs], 1024))
        wcast_pos = [0]

        wc_slots32 = [
            (lambda: acc[:, 0:1024].bitcast(F32),
             lambda: acc[:, 1024:2048].bitcast(F32)),
            (lambda: acc[:, 2048:3072].bitcast(F32),
             lambda: acc[:, 3072:4096].bitcast(F32)),
            (lambda: w1st[0][:], lambda: w1st[1][:]),
            (lambda: w1st[2][:], lambda: w1st[3][:]),
        ]
        wc_state = {"loaded": -1, "gq_cap": [0, 0, 0, 0],
                    "ve_slot": [0, 0, 0, 0], "st_cap": [0, 0]}

        def _wc_src(i):
            (kind, srcap), dstap, w = wcast_jobs[i]
            if kind == "w1":
                return srcap.rearrange("hp k fi -> hp (k fi)") if False else srcap,                     dstap, kind
            return srcap, dstap, kind

        def emit_wcasts(count):
            for _ in range(count):
                i = wcast_pos[0]
                if i >= len(wcast_jobs):
                    return
                wcast_pos[0] += 1
                while wc_state["loaded"] < min(i + 2, len(wcast_jobs) - 1):
                    li = wc_state["loaded"] + 1
                    wc_state["loaded"] = li
                    sl = li % 4
                    (kind, srcap), dstap, w = wcast_jobs[li]
                    GQ.wait(ve, wc_state["ve_slot"][li % 4])
                    if kind == "w1":
                        h0s, h1s = srcap[:, 0:4, :], srcap[:, 4:8, :]
                    else:
                        h0s, h1s = srcap[:, 0:512], srcap[:, 512:1024]
                    GQ.op(lambda e, s=h0s, sl=sl: e.dma_start(
                        out=wc_slots32[sl][0](), in_=s), gq, 16)
                    GQ.op(lambda e, s=h1s, sl=sl: e.dma_start(
                        out=wc_slots32[sl][1](), in_=s), gq, 16)
                    wc_state["gq_cap"][sl] = gq.count
                sl = i % 4
                par = i % 2
                (kind, srcap), dstap, w = wcast_jobs[i]
                V.wait(gq, max(wc_state["gq_cap"][sl], wc_state["st_cap"][par]))
                V.op(lambda e, sl=sl, par=par: e.tensor_copy(
                    out=w1bf[par][:, 0:512], in_=wc_slots32[sl][0]()))
                V.op(lambda e, sl=sl, par=par: e.tensor_copy(
                    out=w1bf[par][:, 512:1024], in_=wc_slots32[sl][1]()), ve)
                wc_state["ve_slot"][sl] = ve.count
                GQ.wait(ve, ve.count)
                GQ.op(lambda e, d=dstap, par=par: e.dma_start(
                    out=d, in_=w1bf[par][:]), gq, 16)
                wc_state["st_cap"][par] = gq.count

        pe_tile_end = {}
        for t in range(16):
            par = t % 2
            if t >= 2:
                SY.wait(pe, pe_tile_end[t - 2])
            SY.op(lambda e, t=t, par=par: e.dma_start(
                out=usb[par][:], in_=u_d[t * 128:(t + 1) * 128, :]), sy, 16)
            PE.wait(sy)
            for k in range(8):
                bank = pb[k % 2]
                PE.wait(ve, ve.count - 1)
                PE.op(lambda e, k=k, par=par, bank=bank: e.transpose(
                    bank[:, 0:128], usb[par][:, k * 128:(k + 1) * 128], I32[:]), pe)
                V.wait(pe)
                V.op(lambda e, k=k, t=t, bank=bank: e.tensor_copy(
                    out=xT[:, k * L + t * 128: k * L + (t + 1) * 128],
                    in_=bank[:, 0:128]), ve)
            pe_tile_end[t] = pe.count

        PE.wait(gq, scratch_done)
        GQ.wait(gq, scratch_done)

        nat_groups = [(ZC0 + 256 * i, 256) for i in range(8)] + \
                     [(XC0 + 256 * i, 256) for i in range(8)]
        tt_groups = [(BCC0, 128), (DLC0, 64)] + [(THC0 + 128 * j, 128) for j in range(8)]

        proj_caps = [0, 0, 0, 0]

        def proj_stage(c):
            par = c % 2
            xcols = slice(c * 128, (c + 1) * 128)
            for gi, (c0, w) in enumerate(nat_groups + tt_groups):
                is_nat = gi < len(nat_groups)
                bank = pb[gi % 2]
                buf = wgb[gi % 4]
                QE, qs = (SY, sy) if gi % 2 == 0 else (GQ, gq)
                QE.wait(pe, proj_caps[gi % 4])
                QE.op(lambda e, c0=c0, w=w, buf=buf: e.dma_start(
                    out=buf[:, 0:8 * w].rearrange("p (k c) -> p k c", k=8),
                    in_=winTbf_d[:, c0:c0 + w].rearrange("(k p) c -> p k c", p=128)),
                    qs, 16)
                PE.wait(qs)
                PE.wait(ve)
                for k in range(8):
                    if is_nat:
                        PE.op(lambda e, k=k, w=w, buf=buf, bank=bank, xcols=xcols:
                              e.matmul(bank[:, 0:w],
                                       xT[:, k * L:(k + 1) * L][:, xcols],
                                       buf[:, k * w:(k + 1) * w],
                                       start=(k == 0), stop=(k == 7)),
                              pe if k == 7 else None)
                    else:
                        PE.op(lambda e, k=k, w=w, buf=buf, bank=bank, xcols=xcols:
                              e.matmul(bank[0:w, 0:128],
                                       buf[:, k * w:(k + 1) * w],
                                       xT[:, k * L:(k + 1) * L][:, xcols],
                                       start=(k == 0), stop=(k == 7)),
                              pe if k == 7 else None)
                proj_caps[gi % 4] = pe.count
                V.wait(pe)
                if is_nat:
                    if c0 < XC0:
                        off = c0 - ZC0
                        V.op(lambda e, off=off, w=w, bank=bank, par=par: e.tensor_copy(
                            out=zsb[:, par * DI + off: par * DI + off + w],
                            in_=bank[:, 0:w]), ve)
                    else:
                        off = c0 - XC0
                        V.op(lambda e, off=off, w=w, bank=bank, par=par: e.tensor_copy(
                            out=Xsb[:, par * DI + off: par * DI + off + w],
                            in_=bank[:, 0:w]), ve)
                elif c0 == BCC0:
                    V.op(lambda e, bank=bank, par=par: e.tensor_copy(
                        out=bcT[:, par * 128:(par + 1) * 128], in_=bank[:, 0:128]), ve)
                elif c0 == DLC0:
                    V.op(lambda e, bank=bank, par=par: e.tensor_copy(
                        out=dtT[:, par * 128:(par + 1) * 128], in_=bank[0:32, 0:128]))
                    V.op(lambda e, bank=bank, par=par: e.tensor_copy(
                        out=lamT[:, par * 128:(par + 1) * 128], in_=bank[32:64, 0:128]),
                        ve)
                else:
                    j = (c0 - THC0) // 128
                    if c == 0:
                        ini = 0.0
                    else:
                        prev = (c - 1) % 2
                        ini = tcV[:, prev * 1024 + j * 128 + 127:
                                  prev * 1024 + j * 128 + 128]
                    V.op(lambda e, bank=bank, par=par, j=j, ini=ini:
                         e.tensor_tensor_scan(
                             out=tcV[:, par * 1024 + j * 128: par * 1024 + (j + 1) * 128],
                             data0=onec[:], data1=bank[:, 0:128], initial=ini,
                             op0=ALU.mult, op1=ALU.add), ve)
            S.wait(ve)
            S.op(lambda e, par=par: e.activation(
                out=s_t[:], in_=dtT[:, par * 128:(par + 1) * 128], func=AF.Exp,
                bias=dtb_s[:]))
            S.op(lambda e, par=par: e.activation(
                out=dtT[:, par * 128:(par + 1) * 128], in_=s_t[:], func=AF.Ln,
                bias=cns[0:32, 1:2]))
            S.op(lambda e, par=par: e.activation(
                out=lamT[:, par * 128:(par + 1) * 128],
                in_=lamT[:, par * 128:(par + 1) * 128], func=AF.Sigmoid), se)
            V.wait(se)
            V.op(lambda e, par=par: e.tensor_tensor(
                out=g_t[:, par * 128:(par + 1) * 128],
                in0=lamT[:, par * 128:(par + 1) * 128],
                in1=dtT[:, par * 128:(par + 1) * 128], op=ALU.mult))
            V.op(lambda e, par=par: e.tensor_tensor(
                out=q_t[:, par * 128:(par + 1) * 128],
                in0=dtT[:, par * 128:(par + 1) * 128],
                in1=g_t[:, par * 128:(par + 1) * 128], op=ALU.subtract), ve)

        def ssd_stage(c):
            par = c % 2
            npar = (c + 1) % 2
            S.wait(ve)
            S.op(lambda e, par=par: e.activation(
                out=sil, in_=zsb[:, par * DI:(par + 1) * DI], func=AF.Silu), se)
            V.op(lambda e, par=par: e.tensor_tensor(
                out=ghat[:, 0:127], in0=g_t[:, par * 128:par * 128 + 127],
                in1=q_t[:, par * 128 + 1:par * 128 + 128], op=ALU.add))
            qn = zcol[:] if c == NCH - 1 else q_t[:, npar * 128:npar * 128 + 1]
            V.op(lambda e, qn=qn, par=par: e.tensor_tensor(
                out=ghat[:, 127:128], in0=g_t[:, par * 128 + 127:par * 128 + 128],
                in1=qn, op=ALU.add))
            V.op(lambda e, par=par: e.tensor_scalar(
                out=s_t[:], in0=dtT[:, par * 128:(par + 1) * 128], scalar1=a_col[:],
                scalar2=None, op0=ALU.mult))
            V.op(lambda e: e.tensor_tensor_scan(
                out=S_T[:], data0=onec[0:32, :], data1=s_t[:], initial=0.0,
                op0=ALU.mult, op1=ALU.add), ve)
            S.wait(ve)
            S.op(lambda e: e.activation(out=lng[:], in_=ghat[:], func=AF.Ln))
            S.op(lambda e: e.activation(out=expd[:], in_=S_T[:], func=AF.Exp,
                                        bias=S_T[:, 127:128], scale=-1.0))
            S.op(lambda e: e.activation(out=dec_c[:], in_=S_T[:, 127:128],
                                        func=AF.Exp), se)
            V.wait(se)
            V.op(lambda e: e.tensor_tensor(out=pack[0:32, :], in0=lng[:], in1=S_T[:],
                                           op=ALU.subtract))
            V.op(lambda e, par=par: e.tensor_tensor(
                out=pack[32:64, :], in0=g_t[:, par * 128:(par + 1) * 128],
                in1=ghat[:], op=ALU.subtract))
            V.op(lambda e: e.tensor_copy(out=pack[64:96, :], in_=S_T[:]))
            V.op(lambda e: e.tensor_tensor(out=wT[:], in0=ghat[:], in1=expd[:],
                                           op=ALU.mult))
            V.op(lambda e: e.tensor_scalar(out=dec4i[:], in0=selj8s[:],
                                           scalar1=dec_c[:], scalar2=None,
                                           op0=ALU.mult), ve)
            PE.wait(ve)
            PE.op(lambda e: e.transpose(pb[2][:, 0:128], pack[:], I32[:]), pe)
            PE.op(lambda e: e.matmul(pb[3][:, 0:8], repl4s[:], dec4i[:],
                                     start=True, stop=True), pe)
            V.wait(pe)
            V.op(lambda e: e.tensor_copy(out=packN[:], in_=pb[2][:, 0:128]))
            V.op(lambda e: e.tensor_copy(out=dec4[:], in_=pb[3][:, 0:8]), ve)
            S.wait(ve)
            S.op(lambda e: e.activation(out=E_nat[:], in_=packN[:, 64:96],
                                        func=AF.Exp), se)

            tcs = tcV[:, par * 1024:(par + 1) * 1024]
            V.op(lambda e, tcs=tcs: e.tensor_scalar(
                out=atmp[:], in0=tcs, scalar1=1.0 / TWO_PI, scalar2=MAGIC,
                op0=ALU.mult, op1=ALU.add))
            V.op(lambda e: e.tensor_scalar(
                out=atmp[:], in0=atmp[:], scalar1=MAGIC, scalar2=None,
                op0=ALU.subtract))
            V.op(lambda e, tcs=tcs: e.scalar_tensor_tensor(
                out=rmod[:], in0=atmp[:], scalar=-TWO_PI, in1=tcs,
                op0=ALU.mult, op1=ALU.add))
            V.op(lambda e: e.tensor_scalar(
                out=rmod[:], in0=rmod[:], scalar1=-PIC, scalar2=PIC,
                op0=ALU.max, op1=ALU.min))
            V.op(lambda e: e.tensor_scalar(
                out=atmp[:], in0=rmod[:], scalar1=PI / 2, scalar2=None, op0=ALU.is_gt))
            V.op(lambda e: e.tensor_scalar(
                out=carg[:], in0=rmod[:], scalar1=PI / 2, scalar2=None, op0=ALU.add))
            V.op(lambda e: e.scalar_tensor_tensor(
                out=carg[:], in0=atmp[:], scalar=-TWO_PI, in1=carg[:],
                op0=ALU.mult, op1=ALU.add))
            V.op(lambda e: e.tensor_scalar(
                out=carg[:], in0=carg[:], scalar1=-PIC, scalar2=PIC,
                op0=ALU.max, op1=ALU.min), ve)
            S.wait(ve)
            S.op(lambda e: e.activation(out=rmod[:], in_=rmod[:], func=AF.Sin))
            S.op(lambda e: e.activation(out=carg[:], in_=carg[:], func=AF.Sin), se)
            V.wait(se)
            V.op(lambda e: e.tensor_copy(out=sinV[:], in_=rmod[:]))
            V.op(lambda e: e.tensor_copy(out=cosV[:], in_=carg[:]), ve)

            bct = bcT[:, par * 128:(par + 1) * 128]
            V.op(lambda e, bct=bct: e.tensor_tensor(out=sqT[:], in0=bct, in1=bct,
                                                    op=ALU.mult), ve)
            PE.wait(ve)
            PE.op(lambda e: e.matmul(pb[2][0:1, 0:128], onec[0:64, 0:1], sqT[0:64, :],
                                     start=True, stop=True), pe)
            PE.op(lambda e: e.matmul(pb[3][0:1, 0:128], onec[64:128, 0:1],
                                     sqT[64:128, :], start=True, stop=True,
                                     tile_position=(64, 0)), pe)
            S.wait(pe)
            S.op(lambda e: e.activation(out=invB[:], in_=pb[2][0:1, 0:128],
                                        func=AF.Sqrt, scale=1.0 / 64.0, bias=cns[0:1, 0:1]))
            S.op(lambda e: e.activation(out=invC[:], in_=pb[3][0:1, 0:128],
                                        func=AF.Sqrt, scale=1.0 / 64.0, bias=cns[0:1, 0:1]), se)
            V.wait(se)
            V.op(lambda e: e.reciprocal(out=invB[:], in_=invB[:]))
            V.op(lambda e: e.reciprocal(out=invC[:], in_=invC[:]), ve)
            PE.wait(ve)
            PE.op(lambda e: e.matmul(pb[2][0:64, 0:128], onec[0:1, 0:64], invB[:],
                                     start=True, stop=True), pe)
            PE.op(lambda e: e.matmul(pb[3][0:64, 0:128], onec[0:1, 0:64], invC[:],
                                     start=True, stop=True), pe)
            V.wait(pe)
            V.op(lambda e, bct=bct: e.tensor_tensor(out=brawn[:], in0=bct[0:64, :],
                                                    in1=pb[2][0:64, 0:128],
                                                    op=ALU.mult))
            V.op(lambda e, bct=bct: e.tensor_tensor(out=crawn[:], in0=bct[64:128, :],
                                                    in1=pb[3][0:64, 0:128],
                                                    op=ALU.mult))
            V.op(lambda e: e.tensor_copy(
                out=brawn8[:].rearrange("p (r c) -> p r c", r=8),
                in_=brawn[:].unsqueeze(1).broadcast_to([64, 8, 128])))
            V.op(lambda e: e.tensor_copy(
                out=crawn8[:].rearrange("p (r c) -> p r c", r=8),
                in_=crawn[:].unsqueeze(1).broadcast_to([64, 8, 128])), ve)
            PE.wait(ve)
            for half in range(2):
                rw = repl1w if half == 0 else repl2w
                rwc = repl1wc if half == 0 else repl2wc
                dst_b = vB1 if half == 0 else vB2
                dst_c = vC1 if half == 0 else vC2
                for piece in range(2):
                    PE.wait(ve)
                    PE.op(lambda e, rw=rw, piece=piece: e.matmul(
                        pb[2][:, 0:512], rw[:],
                        brawn8[:, piece * 512:(piece + 1) * 512],
                        start=True, stop=True), pe)
                    PE.op(lambda e, rwc=rwc, piece=piece: e.matmul(
                        pb[3][:, 0:512], rwc[:],
                        crawn8[:, piece * 512:(piece + 1) * 512],
                        start=True, stop=True), pe)
                    V.wait(pe)
                    V.op(lambda e, dst_b=dst_b, piece=piece, half=half: e.tensor_tensor(
                        out=dst_b[:, piece * 512:(piece + 1) * 512],
                        in0=bbv_s[:, half * 1024 + piece * 512:
                                  half * 1024 + (piece + 1) * 512],
                        in1=pb[2][:, 0:512], op=ALU.add))
                    V.op(lambda e, dst_c=dst_c, piece=piece, half=half: e.tensor_tensor(
                        out=dst_c[:, piece * 512:(piece + 1) * 512],
                        in0=cbv_s[:, half * 1024 + piece * 512:
                                  half * 1024 + (piece + 1) * 512],
                        in1=pb[3][:, 0:512], op=ALU.add), ve)
            V.wait(se)
            V.wait(pe)
            for (v1, v2) in ((vB1, vB2), (vC1, vC2)):
                V.op(lambda e, v1=v1: e.tensor_tensor(out=rta[:], in0=v1[:],
                                                      in1=cosV[:], op=ALU.mult))
                V.op(lambda e, v2=v2: e.tensor_tensor(out=rtb[:], in0=v2[:],
                                                      in1=sinV[:], op=ALU.mult))
                V.op(lambda e, v2=v2: e.tensor_tensor(out=v2[:], in0=v2[:],
                                                      in1=cosV[:], op=ALU.mult))
                V.op(lambda e, v1=v1: e.tensor_tensor(out=v1[:], in0=v1[:],
                                                      in1=sinV[:], op=ALU.mult))
                V.op(lambda e, v1=v1, v2=v2: e.tensor_tensor(out=v2[:], in0=v2[:],
                                                             in1=v1[:], op=ALU.add))
                V.op(lambda e, v1=v1: e.tensor_tensor(out=v1[:], in0=rta[:],
                                                      in1=rtb[:], op=ALU.subtract), ve)
            V.op(lambda e: e.tensor_tensor(
                out=wvsel[:].rearrange("p (r c) -> p r c", r=8),
                in0=wT[:].unsqueeze(1).broadcast_to([32, 8, 128]),
                in1=selj8s[:].unsqueeze(2).broadcast_to([32, 8, 128]),
                op=ALU.mult), ve)
            PE.wait(ve)
            for piece in range(2):
                PE.op(lambda e, piece=piece: e.matmul(
                    pb[2 + piece][:, 0:512], repl4bf[:],
                    wvsel[:, piece * 512:(piece + 1) * 512],
                    start=True, stop=True), pe)
            V.wait(pe)
            for piece in range(2):
                V.op(lambda e, piece=piece: e.tensor_tensor(
                    out=vBW1[:, piece * 512:(piece + 1) * 512],
                    in0=vB1[:, piece * 512:(piece + 1) * 512],
                    in1=pb[2 + piece][:, 0:512], op=ALU.mult))
                V.op(lambda e, piece=piece: e.tensor_tensor(
                    out=vBW2[:, piece * 512:(piece + 1) * 512],
                    in0=vB2[:, piece * 512:(piece + 1) * 512],
                    in1=pb[2 + piece][:, 0:512], op=ALU.mult), ve)
            for j in range(8):
                for half in range(2):
                    src = vBW1 if half == 0 else vBW2
                    bank = pb[2 + ((j * 2 + half) % 2)]
                    PE.wait(ve)
                    PE.op(lambda e, src=src, j=j, bank=bank: e.matmul(
                        bank[:, 0:128], src[:, j * 128:(j + 1) * 128], Ibf[:],
                        start=True, stop=True), pe)
                    V.wait(pe)
                    V.op(lambda e, j=j, half=half, bank=bank: e.tensor_copy(
                        out=bpp.rearrange("p (jj mm n) -> p jj mm n", jj=8, mm=4)
                            [:, j, :, half * 32:half * 32 + 32],
                        in_=bank[:, 0:128].rearrange("p (m k) -> p m k", m=4)), ve)

            cap_tr = [0, 0]; cap_d2 = [0, 0]; cap_g = [0, 0]
            cap_cb = [0, 0]; cap_yx = [0, 0]; cap_ysc = [0, 0]
            cap_d3v = [0, 0]; cap_ys = [0, 0]
            cap_prev_state = ve.count
            cap_ygw = ve.count
            for m in range(4):
                pbYx = pb[4 + (m % 2)]
                pbYs = pb[6 + (m % 2)]
                pm = slice(32 * m, 32 * m + 32)
                for jj in range(8):
                    h = 4 * jj + m
                    j = jj
                    dpar = jj % 2
                    cbank = pb[jj % 2]
                    V.wait(pe, cap_tr[dpar])
                    V.op(lambda e, h=h, dpar=dpar: e.tensor_tensor(
                        out=pre_t[dpar][:], in0=Mneg[:],
                        in1=packN[:, 64 + h:65 + h].broadcast_to([128, 128]),
                        op=ALU.add), ve)
                    cap_prev = ve.count
                    PE.wait(ve, cap_prev)
                    PE.wait(se, cap_d2[dpar])
                    PE.op(lambda e, dpar=dpar: e.transpose(
                        pb[2 + dpar][:, 0:128], pre_t[dpar][:], I32[:]), pe)
                    cap_tr[dpar] = pe.count
                    PE.wait(ve, cap_g[jj % 2])
                    PE.op(lambda e, pm=pm, j=j, cbank=cbank, m=m: e.matmul(
                        cbank[:, 0:128], vB1[pm, j * 128:(j + 1) * 128],
                        vC1[pm, j * 128:(j + 1) * 128], start=True, stop=False,
                        tile_position=(32 * m, 0)))
                    PE.op(lambda e, pm=pm, j=j, cbank=cbank, m=m: e.matmul(
                        cbank[:, 0:128], vB2[pm, j * 128:(j + 1) * 128],
                        vC2[pm, j * 128:(j + 1) * 128], start=False, stop=True,
                        tile_position=(32 * m, 0)), pe)
                    cap_cb[dpar] = pe.count
                    S.wait(pe, cap_tr[dpar])
                    S.wait(ve, cap_d3v[dpar])
                    S.op(lambda e, dpar=dpar, h=h: e.activation(
                        out=D2[dpar][:], in_=pb[2 + dpar][:, 0:128], func=AF.Exp,
                        bias=packN[:, h:h + 1]), se)
                    cap_d2[dpar] = se.count
                    V.wait(se, cap_d2[dpar])
                    V.op(lambda e, dpar=dpar, h=h: e.scalar_tensor_tensor(
                        out=D2[dpar][:], in0=I32[:], scalar=packN[:, 32 + h:33 + h],
                        in1=D2[dpar][:], op0=ALU.mult, op1=ALU.add))
                    V.wait(pe, cap_cb[dpar])
                    V.wait(pe, cap_yx[dpar])
                    V.op(lambda e, dpar=dpar, cbank=cbank: e.tensor_tensor(
                        out=G_bf[dpar][:], in0=D2[dpar][:], in1=cbank[:, 0:128],
                        op=ALU.mult), ve)
                    cap_g[dpar] = ve.count
                    cap_d3v[dpar] = ve.count
                    PE.wait(ve, max(cap_g[dpar], cap_prev_state))
                    if jj == 0:
                        PE.wait(se, cap_ysc[1])
                    PE.op(lambda e, dpar=dpar, h=h, jj=jj, pbYx=pbYx, par=par:
                          e.matmul(pbYx[:, jj * 64:(jj + 1) * 64], G_bf[dpar][:],
                                   Xsb[:, par * DI + h * 64: par * DI + (h + 1) * 64],
                                   start=True, stop=True, skip_group_check=True),
                          pe)
                    cap_yx[dpar] = pe.count
                    PE.op(lambda e, pm=pm, j=j, jj=jj, pbYs=pbYs, m=m: e.matmul(
                        pbYs[:, jj * 64:(jj + 1) * 64],
                        vC1[pm, j * 128:(j + 1) * 128],
                        state_bf[pm, j * 128:j * 128 + 64], start=True,
                        stop=False, tile_position=(32 * m, 0),
                        skip_group_check=True))
                    PE.op(lambda e, pm=pm, j=j, jj=jj, pbYs=pbYs, m=m: e.matmul(
                        pbYs[:, jj * 64:(jj + 1) * 64],
                        vC2[pm, j * 128:(j + 1) * 128],
                        state_bf[pm, j * 128 + 64:(j + 1) * 128], start=False,
                        stop=True, tile_position=(32 * m, 0),
                        skip_group_check=True), pe)
                    cap_ys[dpar] = pe.count
                    S.wait(pe, cap_ys[dpar])
                    if jj == 0:
                        S.wait(ve, cap_ygw)
                    S.op(lambda e, jj=jj, h=h, pbYs=pbYs: e.activation(
                        out=ysc[:, jj * 64:(jj + 1) * 64],
                        in_=pbYs[:, jj * 64:(jj + 1) * 64],
                        func=AF.Identity, scale=E_nat[:, h:h + 1]), se)
                    cap_ysc[0] = cap_ysc[1]
                    cap_ysc[1] = se.count
                V.wait(se)
                V.op(lambda e, pbYx=pbYx: e.tensor_tensor(
                    out=ysc[:], in0=ysc[:], in1=pbYx[:, 0:512], op=ALU.add))
                V.op(lambda e, m=m, par=par: e.tensor_tensor(
                    out=rmod[:, 0:512].rearrange("p (j q) -> p j q", j=8),
                    in0=Xsb[:, par * DI:(par + 1) * DI].rearrange(
                        "p (j hm q) -> p j hm q", j=8, hm=4)[:, :, m, :],
                    in1=dsk_s[:].rearrange("p (a b) -> p a b", a=4)[:, m, :]
                        .unsqueeze(2).broadcast_to([128, 8, 64]), op=ALU.mult))
                V.op(lambda e: e.tensor_tensor(
                    out=ysc[:], in0=ysc[:], in1=rmod[:, 0:512], op=ALU.add))
                V.op(lambda e, m=m: e.tensor_copy(
                    out=yg[:].rearrange("p (j hm q) -> p j hm q", j=8, hm=4)[:, :, m, :],
                    in_=ysc[:].rearrange("p (j c) -> p j c", j=8)), ve)
                cap_ygw = ve.count

            V.op(lambda e: e.tensor_tensor(out=yg[:], in0=yg[:], in1=sil,
                                           op=ALU.mult), ve)
            PE.wait(ve)
            for m in range(4):
                pm = slice(32 * m, 32 * m + 32)
                for j in range(8):
                    h = 4 * j + m
                    bank = pb[6] if j < 4 else pb[7]
                    jo = (j % 4) * 128
                    PE.op(lambda e, h=h, pm=pm, bank=bank, jo=jo, m=m, par=par:
                          e.matmul(bank[pm, jo:jo + 64],
                                   bpp[:, h * 64:h * 64 + 32],
                                   Xsb[:, par * DI + h * 64: par * DI + (h + 1) * 64],
                                   start=True, stop=True,
                                   tile_position=(0, 32 * m),
                                   skip_group_check=True))
                    PE.op(lambda e, h=h, pm=pm, bank=bank, jo=jo, m=m, par=par:
                          e.matmul(bank[pm, jo + 64:jo + 128],
                                   bpp[:, h * 64 + 32:(h + 1) * 64],
                                   Xsb[:, par * DI + h * 64: par * DI + (h + 1) * 64],
                                   start=True, stop=True,
                                   tile_position=(0, 32 * m),
                                   skip_group_check=True),
                          pe if (j in (3, 7)) else None)
            V.wait(pe)
            V.op(lambda e: e.tensor_tensor(
                out=state[:].rearrange("p (j c) -> p j c", j=8),
                in0=state[:].rearrange("p (j c) -> p j c", j=8),
                in1=dec4[:].unsqueeze(2).broadcast_to([128, 8, 128]), op=ALU.mult))
            V.op(lambda e: e.tensor_tensor(out=state[:, 0:512], in0=state[:, 0:512],
                                           in1=pb[6][:, 0:512], op=ALU.add))
            V.op(lambda e: e.tensor_tensor(out=state[:, 512:1024],
                                           in0=state[:, 512:1024],
                                           in1=pb[7][:, 0:512], op=ALU.add))
            V.op(lambda e: e.tensor_copy(out=state_bf[:], in_=state[:]), ve)

            V.op(lambda e: e.tensor_tensor(out=rmod[:], in0=yg[:, 0:1024],
                                           in1=yg[:, 0:1024], op=ALU.mult))
            V.op(lambda e: e.tensor_tensor(out=rln[:], in0=yg[:, 1024:2048],
                                           in1=yg[:, 1024:2048], op=ALU.mult))
            V.op(lambda e: e.tensor_reduce(out=cols[:, 0:1], in_=rmod[:], axis=AX.X,
                                           op=ALU.add))
            V.op(lambda e: e.tensor_reduce(out=cols[:, 1:2], in_=rln[:], axis=AX.X,
                                           op=ALU.add), ve)
            S.wait(ve)
            S.op(lambda e: e.activation(out=cols[:, 12:13], in_=cols[:, 0:1],
                                        func=AF.Identity, bias=cols[:, 1:2]))
            S.op(lambda e: e.activation(out=cols[:, 1:2], in_=cols[:, 12:13],
                                        func=AF.Sqrt, scale=1.0 / DI,
                                        bias=cns[:, 0:1]), se)
            V.wait(se)
            V.op(lambda e: e.reciprocal(out=cols[:, 13:14], in_=cols[:, 1:2]), ve)
            S.wait(ve)
            S.op(lambda e: e.activation(out=cols[:, 2:3], in_=cols[:, 13:14],
                                        func=AF.Identity), se)
            V.wait(se)
            V.op(lambda e: e.tensor_tensor(
                out=yg[:], in0=yg[:],
                in1=cols[:, 2:3].broadcast_to([128, DI]), op=ALU.mult))
            V.op(lambda e: e.tensor_tensor(out=yg[:], in0=yg[:], in1=nwrep_s[:],
                                           op=ALU.mult), ve)
            for k in range(16):
                bank = pb[2 + (k % 2)]
                ytd = vB1 if k < 8 else vB2
                ko = k % 8
                PE.wait(ve)
                PE.wait(se)
                PE.op(lambda e, k=k, bank=bank: e.matmul(
                    bank[:, 0:128], yg[:, k * 128:(k + 1) * 128], Ibf[:],
                    start=True, stop=True), pe)
                V.wait(pe)
                V.op(lambda e, ko=ko, bank=bank, ytd=ytd: e.tensor_copy(
                    out=ytd[:, ko * 128:(ko + 1) * 128],
                    in_=bank[:, 0:128]), ve)
            for k in range(16):
                ytd = vB1 if k < 8 else vB2
                ko = k % 8
                GQ.wait(pe)
                GQ.op(lambda e, k=k: e.dma_start(
                    out=wob[k % 2][:], in_=woutTbf_d[k * 128:(k + 1) * 128, :]),
                    gq, 16)
                PE.wait(gq)
                PE.wait(ve)
                for nh in range(2):
                    PE.op(lambda e, ko=ko, nh=nh, k=k, ytd=ytd: e.matmul(
                        pb[nh][:, 0:512], ytd[:, ko * 128:(ko + 1) * 128],
                        wob[k % 2][:, nh * 512:(nh + 1) * 512],
                        start=(k == 0), stop=(k == 15)), pe)
            SY.wait(ve)
            SY.op(lambda e, c=c, par=par: e.dma_start(
                out=usb[par][:], in_=u_d[c * 128:(c + 1) * 128, :]), sy, 16)
            V.wait(sy)
            V.wait(pe)
            V.op(lambda e, par=par: e.tensor_tensor(
                out=rln[:, 0:512], in0=usb[par][:, 0:512], in1=pb[0][:, 0:512],
                op=ALU.add))
            V.op(lambda e, par=par: e.tensor_tensor(
                out=rln[:, 512:1024], in0=usb[par][:, 512:1024], in1=pb[1][:, 0:512],
                op=ALU.add))
            V.op(lambda e: e.tensor_reduce(out=cols[:, 2:3], in_=rln[:], axis=AX.X,
                                           op=ALU.add), ve)
            S.wait(ve)
            S.op(lambda e: e.activation(out=cols[:, 4:5], in_=cols[:, 2:3],
                                        func=AF.Identity, scale=1.0 / DM), se)
            V.wait(se)
            V.op(lambda e: e.tensor_tensor(
                out=xsq[:], in0=rln[:],
                in1=cols[:, 4:5].broadcast_to([128, DM]), op=ALU.subtract))
            V.wait(sr)
            V.op(lambda e: e.tensor_tensor(out=xln[:], in0=xsq[:], in1=xsq[:],
                                           op=ALU.mult))
            V.op(lambda e: e.tensor_reduce(out=cols[:, 5:6], in_=xln[:], axis=AX.X,
                                           op=ALU.add), ve)
            S.wait(ve)
            S.op(lambda e: e.activation(out=cols[:, 8:9], in_=cols[:, 5:6],
                                        func=AF.Sqrt, scale=1.0 / DM,
                                        bias=cns[:, 0:1]), se)
            V.wait(se)
            V.op(lambda e: e.reciprocal(out=cols[:, 9:10], in_=cols[:, 8:9]), ve)
            S.wait(ve)
            S.op(lambda e: e.activation(out=cols[:, 10:11], in_=cols[:, 9:10],
                                        func=AF.Identity), se)
            V.wait(se)
            V.op(lambda e: e.tensor_tensor(
                out=xln[:], in0=xsq[:],
                in1=cols[:, 10:11].broadcast_to([128, DM]), op=ALU.mult), ve)
            SY.wait(ve)
            SY.op(lambda e, c=c: e.dma_start(
                out=r1scr_d[c * 128:(c + 1) * 128, :], in_=xln[:]), sr, 16)
            for k in range(8):
                bank = pb[2 + (k % 2)]
                PE.wait(ve)
                PE.op(lambda e, k=k, bank=bank: e.transpose(
                    bank[:, 0:128], xln[:, k * 128:(k + 1) * 128], I32[:]), pe)
                V.wait(pe)
                V.op(lambda e, bank=bank: e.tensor_copy(out=xt32[:],
                                                        in_=bank[:, 0:128]))
                V.op(lambda e, k=k, c=c, bank=bank: e.tensor_copy(
                    out=xT[:, k * L + c * 128:k * L + (c + 1) * 128],
                    in_=bank[:, 0:128]), ve)
                PE.wait(ve)
                PE.op(lambda e, k=k: e.matmul(
                    pb[4][0:8, 0:128], wr_s[:, k * 8:(k + 1) * 8], xt32[:],
                    start=(k == 0), stop=(k == 7)), pe if k == 7 else None)
            S.wait(pe)
            S.op(lambda e: e.activation(out=logT[:], in_=pb[4][0:8, 0:128],
                                        func=AF.Identity, bias=br_s[:]), se)
            PE.wait(se)
            PE.wait(ve)
            PE.op(lambda e: e.transpose(pb[5][:, 0:8], logT[:], I32[0:8, 0:8]), pe)
            V.wait(pe)
            V.op(lambda e: e.tensor_reduce(out=cols[:, 14:15], in_=pb[5][:, 0:8],
                                           axis=AX.X, op=ALU.max), ve)
            S.wait(ve)
            S.op(lambda e: e.activation(out=cols[:, 13:14], in_=cols[:, 14:15],
                                        func=AF.Identity), se)
            V.wait(se)
            V.op(lambda e: e.tensor_tensor(
                out=xt32[:, 8:16], in0=pb[5][:, 0:8],
                in1=cols[:, 13:14].broadcast_to([128, 8]), op=ALU.is_ge))
            V.op(lambda e: e.tensor_tensor(out=cols[:, 15:16], in0=cols[:, 13:14],
                                           in1=cns[:, 4:5], op=ALU.mult), ve)
            S.wait(ve)
            S.op(lambda e: e.activation(out=xt32[:, 0:8], in_=pb[5][:, 0:8],
                                        func=AF.Exp, bias=cols[:, 15:16],
                                        accum_out=cols[:, 11:12]), se)
            V.wait(se)
            V.op(lambda e: e.reciprocal(out=cols[:, 12:13], in_=cols[:, 11:12]), ve)
            S.wait(ve)
            S.op(lambda e: e.activation(out=cols[:, 10:11], in_=cols[:, 12:13],
                                        func=AF.Identity), se)
            V.wait(se)
            V.op(lambda e, c=c: e.tensor_tensor(
                out=gsel[:, c * 8:(c + 1) * 8], in0=xt32[:, 8:16],
                in1=cols[:, 10:11].broadcast_to([128, 8]), op=ALU.mult), ve)

        emit_wcasts(20)
        proj_stage(0)
        nssd = NCH if upto is None else min(NCH, upto)
        for c in range(nssd):
            if c + 1 < NCH:
                proj_stage(c + 1)
            ssd_stage(c)
            emit_wcasts(33)

        emit_wcasts(len(wcast_jobs))
        GQ.wait(gq)
        PE.wait(gq)
        if upto is None:
            for tg in range(4):
                V.op(lambda e: e.memset(acc[:], 0.0), ve)
                for e_i in range(E):
                    par_e = e_i % 2
                    SY.wait(se)
                    SY.op(lambda e, e_i=e_i, par_e=par_e: e.dma_start(
                        out=b1c_s[par_e][:], in_=b1_d[e_i]), sy, 16)
                    for ft in range(32):
                        par = ft % 4
                        QE, qs = (SY, sy) if ft % 2 == 0 else (GQ, gq)
                        QE.wait(pe, pe.count if ft == 0 else pe.count - 3)
                        QE.op(lambda e, e_i=e_i, ft=ft, par=par: e.dma_start(
                            out=w1bf4[par][:], in_=w1bf_d[e_i, ft]), qs, 16)
                        bank = pb[ft % 4]
                        PE.wait(qs)
                        PE.wait(se)
                        for k in range(8):
                            PE.op(lambda e, k=k, par=par, bank=bank, tg=tg: e.matmul(
                                bank[:, 0:512], w1bf4[par][:, k * 128:(k + 1) * 128],
                                xT[:, k * L + tg * 512:k * L + (tg + 1) * 512],
                                start=(k == 0), stop=(k == 7)),
                                pe if k == 7 else None)
                        S.wait(pe)
                        S.wait(sy)
                        S.op(lambda e, ft=ft, bank=bank, par_e=par_e: e.activation(
                            out=h1T[:, ft * 512:(ft + 1) * 512], in_=bank[:, 0:512],
                            func=AF.Relu, bias=b1c_s[par_e][:, ft:ft + 1]), se)
                    for fs in range(33):
                        par = fs % 4
                        QE, qs = (SY, sy) if fs % 2 == 0 else (GQ, gq)
                        QE.wait(pe, pe.count if fs == 0 else pe.count - 3)
                        QE.op(lambda e, e_i=e_i, fs=fs, par=par: e.dma_start(
                            out=w2bf4[par][:], in_=w2bf_d[e_i, fs]), qs, 16)
                        PE.wait(qs)
                        if fs == 0:
                            PE.wait(se)
                        for tt in range(4):
                            for nh in range(2):
                                PE.op(lambda e, fs=fs, tt=tt, nh=nh, par=par: e.matmul(
                                    pb[tt * 2 + nh][:, 0:512],
                                    h1T[:, fs * 512 + tt * 128:
                                        fs * 512 + (tt + 1) * 128],
                                    w2bf4[par][:, nh * 512:(nh + 1) * 512],
                                    start=(fs == 0), stop=(fs == 32)),
                                    pe if (tt == 3 and nh == 1) else None)
                    V.wait(pe)
                    for tt in range(4):
                        ci = tg * 4 + tt
                        for nh in range(2):
                            V.op(lambda e, tt=tt, nh=nh, ci=ci, e_i=e_i:
                                 e.scalar_tensor_tensor(
                                     out=acc[:, tt * DM + nh * 512:
                                            tt * DM + (nh + 1) * 512],
                                     in0=pb[tt * 2 + nh][:, 0:512],
                                     scalar=gsel[:, ci * 8 + e_i:ci * 8 + e_i + 1],
                                     in1=acc[:, tt * DM + nh * 512:
                                            tt * DM + (nh + 1) * 512],
                                     op0=ALU.mult, op1=ALU.add), ve)
                if e_i == E - 1:
                    for tt in range(4):
                        ci = tg * 4 + tt
                        par = tt % 2
                        SY.wait(ve)
                        SY.op(lambda e, ci=ci, par=par: e.dma_start(
                            out=usb[par][:], in_=r1scr_d[ci * 128:(ci + 1) * 128, :]),
                            sy, 16)
                        V.wait(sy)
                        V.op(lambda e, tt=tt, par=par: e.tensor_tensor(
                            out=rln[:], in0=usb[par][:],
                            in1=acc[:, tt * DM:(tt + 1) * DM], op=ALU.add))
                        V.op(lambda e: e.tensor_reduce(
                            out=cols[:, 2:3], in_=rln[:], axis=AX.X, op=ALU.add), ve)
                        S.wait(ve)
                        S.op(lambda e: e.activation(
                            out=cols[:, 4:5], in_=cols[:, 2:3], func=AF.Identity,
                            scale=1.0 / DM), se)
                        V.wait(se)
                        V.op(lambda e: e.tensor_tensor(
                            out=xsq[:], in0=rln[:],
                            in1=cols[:, 4:5].broadcast_to([128, DM]),
                            op=ALU.subtract))
                        V.op(lambda e: e.tensor_tensor(
                            out=w1st[0][:], in0=xsq[:, 0:512], in1=xsq[:, 0:512],
                            op=ALU.mult))
                        V.op(lambda e: e.tensor_tensor(
                            out=w1st[1][:], in0=xsq[:, 512:1024],
                            in1=xsq[:, 512:1024], op=ALU.mult))
                        V.op(lambda e: e.tensor_reduce(
                            out=cols[:, 5:6], in_=w1st[0][:], axis=AX.X, op=ALU.add))
                        V.op(lambda e: e.tensor_reduce(
                            out=cols[:, 6:7], in_=w1st[1][:], axis=AX.X, op=ALU.add),
                            ve)
                        S.wait(ve)
                        S.op(lambda e: e.activation(
                            out=cols[:, 7:8], in_=cols[:, 5:6], func=AF.Identity,
                            bias=cols[:, 6:7]))
                        S.op(lambda e: e.activation(
                            out=cols[:, 10:11], in_=cols[:, 7:8], func=AF.Sqrt,
                            scale=1.0 / DM, bias=cns[:, 0:1]), se)
                        V.wait(se)
                        V.op(lambda e: e.reciprocal(out=cols[:, 11:12],
                                                    in_=cols[:, 10:11]), ve)
                        S.wait(ve)
                        S.op(lambda e: e.activation(
                            out=cols[:, 12:13], in_=cols[:, 11:12],
                            func=AF.Identity), se)
                        V.wait(se)
                        V.wait(so)
                        V.op(lambda e, par=par: e.tensor_tensor(
                            out=xo32[par][:], in0=xsq[:],
                            in1=cols[:, 12:13].broadcast_to([128, DM]),
                            op=ALU.mult), ve)
                        SY.wait(ve)
                        SY.op(lambda e, ci=ci, par=par: e.dma_start(
                            out=out_d[ci * 128:(ci + 1) * 128, :], in_=xo32[par][:]),
                            so, 16)
            SY.wait(so)

        if taps:
            SY.wait(ve); SY.wait(se); SY.wait(pe); SY.wait(gq, gq.count)
            for tname, sname, shape in taps:
                t = SBUF[sname]
                td = dt("tap_" + tname, list(shape), t.dtype,
                        kind="ExternalOutput").ap()
                SY.op(lambda e, t=t, td=td, shape=shape: e.dma_start(
                    out=td, in_=t[0:shape[0], 0:shape[1]]), so, 16)
            SY.wait(so)

        with nc.Block() as block:
            @block.sync
            def _(eng):
                SY.replay(eng)

            @block.gpsimd
            def _(eng):
                GQ.replay(eng)

            @block.tensor
            def _(eng):
                PE.replay(eng)

            @block.vector
            def _(eng):
                V.replay(eng)

            @block.scalar
            def _(eng):
                S.replay(eng)

    return nc


def _host_inputs(inputs):
    u = np.asarray(inputs["L0_attn_u"], np.float32)
    ipw = np.asarray(inputs["L0_attn_in_proj_w"], np.float32)
    perm = np.concatenate([
        np.arange(0, 2048), np.arange(2048, 4096), np.arange(4096, 4224),
        np.arange(4224, 4256), np.arange(5280, 5312), np.arange(4256, 5280)])
    winT = np.ascontiguousarray(ipw[perm].T)
    woutT = np.ascontiguousarray(
        np.asarray(inputs["L0_attn_out_proj_w"], np.float32).T)
    W1 = np.asarray(inputs["L0_ffn_W1"], np.float32)
    w1r = np.ascontiguousarray(
        W1.reshape(E, 8, 128, 32, 128).transpose(0, 3, 1, 2, 4))
    W2 = np.asarray(inputs["L0_ffn_W2"], np.float32)
    b2 = np.asarray(inputs["L0_ffn_b2"], np.float32)
    w2p = np.zeros((E, 33, 128, DM), np.float32)
    w2p[:, :32] = W2.reshape(E, 32, 128, DM)
    w2p[:, 32, 0, :] = b2
    b1 = np.asarray(inputs["L0_ffn_b1"], np.float32)
    b1c = np.ascontiguousarray(b1.reshape(E, 32, 128).transpose(0, 2, 1))
    Bb = np.asarray(inputs["L0_attn_B_bias"], np.float32)
    Cb = np.asarray(inputs["L0_attn_C_bias"], np.float32)

    def vint(bias):
        outs = np.zeros((2, 128, 1024), np.float32)
        m2g, kg = np.divmod(np.arange(128), 32)
        for half in range(2):
            for j in range(8):
                outs[half, :, 128 * j:128 * (j + 1)] = \
                    bias[4 * j + m2g, half * 32 + kg][:, None]
        return outs

    nwrep = np.ascontiguousarray(np.broadcast_to(
        np.asarray(inputs["L0_attn_norm_w"], np.float32)[None, :], (128, DI)))
    ii, jjx = np.meshgrid(np.arange(128), np.arange(128), indexing="ij")
    mneg = np.where(ii < jjx, -1e30, 0.0).astype(np.float32)
    kk = np.arange(128) % 32
    shared = dict(
        winT=winT, woutT=woutT,
        wr=np.ascontiguousarray(np.asarray(inputs["L0_ffn_Wr"], np.float32)),
        br=np.asarray(inputs["L0_ffn_br"], np.float32).reshape(E, 1),
        w1=w1r, b1c=b1c, w2=w2p,
        dtb=np.asarray(inputs["L0_attn_dt_bias"], np.float32).reshape(H, 1),
        alog=np.asarray(inputs["L0_attn_A_log"], np.float32).reshape(H, 1),
        dsk4=np.ascontiguousarray(np.broadcast_to(
            np.asarray(inputs["L0_attn_D"], np.float32).reshape(-1)[
                (4 * np.arange(8)[None, :] + np.arange(4)[:, None])][None, :, :],
            (128, 4, 8))),
        bnw=np.asarray(inputs["L0_attn_B_norm_w"], np.float32).reshape(N, 1),
        cnw=np.asarray(inputs["L0_attn_C_norm_w"], np.float32).reshape(N, 1),
        bbv=vint(Bb), cbv=vint(Cb), nwrep=nwrep,
        ident=np.eye(128, dtype=np.float32), mneg=mneg,
        repl1=(np.arange(N)[:, None] == kk[None, :]).astype(np.float32),
        repl2=(np.arange(N)[:, None] == (32 + kk)[None, :]).astype(np.float32),
        repl4=((np.arange(H)[:, None] % 4) ==
               (np.arange(128)[None, :] // 32)).astype(np.float32),
        selj8=((np.arange(H)[:, None] // 4) ==
               np.arange(8)[None, :]).astype(np.float32),
    )
    in_maps = []
    for core in range(8):
        d = dict(shared)
        d["u"] = np.ascontiguousarray(u[core])
        in_maps.append(d)
    return in_maps


def kernel(**inputs):
    in_maps = _host_inputs(inputs)
    nc = build_program()
    res = run_bass_kernel_spmd(nc, in_maps, core_ids=list(range(8)))
    out = np.stack([np.asarray(r["out"], np.float32) for r in res.results], axis=0)
    return out


# version 07
# speedup vs baseline: 2.1662x; 1.0702x over previous
import math
import numpy as np
from contextlib import ExitStack

import concourse.bass as bass
import concourse.mybir as mybir
from concourse.bass_utils import run_bass_kernel_spmd

F32 = mybir.dt.float32
BF16 = mybir.dt.bfloat16
AF = mybir.ActivationFunctionType
ALU = mybir.AluOpType
AX = mybir.AxisListType

L = 2048
DM = 1024
H = 32
P = 64
N = 64
DI = H * P
NCH = 16
Q = 128
DPROJ = 5312
E = 8
DFF = 4096
TWO_PI = 2.0 * math.pi
MAGIC = 1.5 * (2.0 ** 23)
PI = math.pi
PIC = math.pi * (1.0 - 1e-6)

ZC0, XC0, BCC0, DLC0, THC0 = 0, 2048, 4096, 4224, 4288


class Sem:
    def __init__(self, handle):
        self.h = handle
        self.count = 0


class Emitter:
    def __init__(self, name):
        self.name = name
        self.items = []
        self._waited = {}

    def wait(self, sem, val=None):
        v = sem.count if val is None else val
        if self._waited.get(id(sem), -1) >= v or v <= 0:
            return
        self._waited[id(sem)] = v
        self.items.append(lambda e, s=sem, v=v: e.wait_ge(s.h, v))

    def op(self, fn, inc=None, amt=1):
        if inc is not None:
            inc.count += amt
            def run(e, fn=fn, inc=inc, amt=amt):
                fn(e).then_inc(inc.h, amt)
        else:
            def run(e, fn=fn):
                fn(e)
        self.items.append(run)

    def replay(self, eng):
        for it in self.items:
            it(eng)


def build_program(upto=None, taps=()):
    nc = bass.Bass("TRN2", target_bir_lowering=False)
    dt = nc.dram_tensor

    u_d = dt("u", [L, DM], F32, kind="ExternalInput").ap()
    winT_d = dt("winT", [DM, DPROJ], F32, kind="ExternalInput").ap()
    woutT_d = dt("woutT", [DI, DM], F32, kind="ExternalInput").ap()
    wr_d = dt("wr", [DM, E], F32, kind="ExternalInput").ap()
    br_d = dt("br", [E, 1], F32, kind="ExternalInput").ap()
    w1_d = dt("w1", [E, 32, 8, 128, 128], F32, kind="ExternalInput").ap()
    b1_d = dt("b1c", [E, 128, 32], F32, kind="ExternalInput").ap()
    w2_d = dt("w2", [E, 33, 128, DM], F32, kind="ExternalInput").ap()
    dtb_d = dt("dtb", [H, 1], F32, kind="ExternalInput").ap()
    alog_d = dt("alog", [H, 1], F32, kind="ExternalInput").ap()
    dsk_d = dt("dsk4", [128, 4, 8], F32, kind="ExternalInput").ap()
    bnw_d = dt("bnw", [N, 1], F32, kind="ExternalInput").ap()
    cnw_d = dt("cnw", [N, 1], F32, kind="ExternalInput").ap()
    bbv_d = dt("bbv", [2, 128, 1024], F32, kind="ExternalInput").ap()
    cbv_d = dt("cbv", [2, 128, 1024], F32, kind="ExternalInput").ap()
    nwrep_d = dt("nwrep", [128, DI], F32, kind="ExternalInput").ap()
    ident_d = dt("ident", [128, 128], F32, kind="ExternalInput").ap()
    mneg_d = dt("mneg", [128, 128], F32, kind="ExternalInput").ap()
    repl1_d = dt("repl1", [N, 128], F32, kind="ExternalInput").ap()
    repl2_d = dt("repl2", [N, 128], F32, kind="ExternalInput").ap()
    repl4_d = dt("repl4", [H, 128], F32, kind="ExternalInput").ap()
    selj8_d = dt("selj8", [H, 8], F32, kind="ExternalInput").ap()
    out_d = dt("out", [L, DM], F32, kind="ExternalOutput").ap()

    winTbf_d = dt("winTbf", [DM, DPROJ], BF16, kind="Internal").ap()
    woutTbf_d = dt("woutTbf", [DI, DM], BF16, kind="Internal").ap()
    r1scr_d = dt("r1scr", [L, DM], F32, kind="Internal").ap()
    w1bf_d = dt("w1bfd", [E, 32, 128, 1024], BF16, kind="Internal").ap()
    w2bf_d = dt("w2bfd", [E, 33, 128, DM], BF16, kind="Internal").ap()


    st = ExitStack()
    SBUF = {}

    def sb(n, sh, d=F32):
        t = st.enter_context(nc.sbuf_tensor(n, sh, d))
        SBUF[n] = t
        return t

    ps = lambda n, sh: st.enter_context(nc.psum_tensor(n, sh, F32))
    mksem = lambda n: Sem(st.enter_context(nc.semaphore(n)))

    with st:
        xT = sb("xT", [128, 8 * L], BF16)
        acc = sb("acc", [128, 4 * DM], BF16)
        h1T = sb("h1T", [128, 33 * 512], BF16)
        zsb = h1T[:, 0:4096]
        Xsb = h1T[:, 4096:8192]
        tcV = h1T[:, 8192:12288].bitcast(F32)
        bpp = h1T[:, 12288:14336]
        sil = h1T[:, 14336:16384]
        bcT = sb("bcT", [128, 2 * 128], F32)
        dtT = sb("dtT", [32, 2 * 128], F32)
        lamT = sb("lamT", [32, 2 * 128], F32)
        g_t = sb("g_t", [32, 2 * 128], F32)
        q_t = sb("q_t", [32, 2 * 128], F32)
        wgb = [sb(f"wgb{i}", [128, 8 * 256], BF16) for i in range(4)]
        wob = [sb(f"wob{i}", [128, DM], BF16) for i in range(2)]
        usb = [sb(f"usb{i}", [128, DM], F32) for i in range(2)]

        ghat = sb("ghat", [32, 128], F32)
        s_t = sb("s_t", [32, 128], F32)
        S_T = sb("S_T", [32, 128], F32)
        lng = sb("lng", [32, 128], F32)
        wT = sb("wT", [32, 128], F32)
        expd = sb("expd", [32, 128], F32)
        pack = sb("pack", [128, 128], F32)
        packN = sb("packN", [128, 128], F32)
        E_nat = sb("E_nat", [128, 32], F32)
        dec_c = sb("dec_c", [32, 1], F32)
        dec4i = sb("dec4i", [32, 8], F32)
        dec4 = sb("dec4", [128, 8], F32)
        wvsel = sb("wvsel", [32, 8 * 128], BF16)
        state = sb("state", [128, 1024], F32)
        state_bf = sb("state_bf", [128, 1024], BF16)
        sqT = sb("sqT", [128, 128], F32)
        invB = sb("invB", [1, 128], F32)
        invC = sb("invC", [1, 128], F32)
        brawn = sb("brawn", [64, 128], BF16)
        brawn8 = sb("brawn8", [64, 8 * 128], BF16)
        crawn = sb("crawn", [64, 128], BF16)
        crawn8 = sb("crawn8", [64, 8 * 128], BF16)
        vB1 = sb("vB1", [128, 1024], BF16)
        vB2 = sb("vB2", [128, 1024], BF16)
        vC1 = sb("vC1", [128, 1024], BF16)
        vC2 = sb("vC2", [128, 1024], BF16)
        rta = sb("rta", [128, 1024], BF16)
        rtb = sb("rtb", [128, 1024], BF16)
        vBW1 = rta
        vBW2 = rtb
        rmod = sb("rmod", [128, 1024], F32)
        cosV = sb("cosV", [128, 1024], BF16)
        sinV = sb("sinV", [128, 1024], BF16)
        pre_t = [sb(f"pre{i}", [128, 128], F32) for i in range(2)]
        D2 = [sb(f"D2_{i}", [128, 128], F32) for i in range(2)]
        G_bf = [sb(f"G_{i}", [128, 128], BF16) for i in range(2)]
        ysc = sb("ysc", [128, 512], F32)
        yg = sb("yg", [128, DI], BF16)
        I32 = sb("I32", [128, 128], F32)
        Ibf = sb("Ibf", [128, 128], BF16)
        Mneg = sb("Mneg", [128, 128], F32)
        rp1 = sb("rp1", [64, 128], F32)
        rp2 = sb("rp2", [64, 128], F32)
        repl1w = sb("repl1w", [64, 128], BF16)
        repl2w = sb("repl2w", [64, 128], BF16)
        repl1wc = sb("repl1wc", [64, 128], BF16)
        repl2wc = sb("repl2wc", [64, 128], BF16)
        repl4s = sb("repl4s", [32, 128], F32)
        repl4bf = sb("repl4bf", [32, 128], BF16)
        selj8s = sb("selj8s", [32, 8], F32)
        bbv_s = sb("bbv_s", [128, 2 * 1024], BF16)
        cbv_s = sb("cbv_s", [128, 2 * 1024], BF16)
        nwrep_s = sb("nwrep_s", [128, DI], BF16)
        onec = sb("onec", [128, 128], F32)
        a_col = sb("a_col", [32, 1], F32)
        acol_t = sb("acol_t", [32, 1], F32)
        dtb_s = sb("dtb_s", [32, 1], F32)
        dsk_s = sb("dsk_s", [128, 32], F32)
        bnw_s = sb("bnw_s", [64, 1], F32)
        cnw_s = sb("cnw_s", [64, 1], F32)
        wr_s = sb("wr_s", [128, 8 * 8], F32)
        br_s = sb("br_s", [8, 1], F32)
        zcol = sb("zcol", [32, 1], F32)
        rln = sb("rln", [128, DM], F32)
        xln = sb("xln", [128, DM], F32)
        carg = rln
        atmp = xln
        xsq = rmod
        cst32 = [rln, xln]
        cstbf_t = None
        xsq = sb("xsq", [128, DM], F32)
        xt32 = sb("xt32", [128, 128], F32)
        logT = sb("logT", [8, 128], F32)
        cols = sb("cols", [128, 16], F32)
        cns = sb("cns", [128, 6], F32)
        gsel = sb("gsel", [128, 8 * 16], F32)
        w1st = [sb(f"w1st{i}", [128, 1024], F32) for i in range(2)]
        w1bf = [sb(f"w1bf{i}", [128, 1024], BF16) for i in range(2)]
        w1bf4 = [sb(f"w1bf4_{i}", [128, 1024], BF16) for i in range(6)]
        w2bf4 = w1bf4
        b1c_s = [sb(f"b1c{i}", [128, 32], F32) for i in range(2)]
        xo32 = [xln, xln]

        pb = [ps(f"pb{i}", [128, 512]) for i in range(8)]

        for _n, _a in (("zsb", zsb), ("Xsb", Xsb), ("tcV", tcV),
                       ("bpp", bpp), ("sil", sil)):
            SBUF[_n] = _a
        sy = mksem("sy")
        sr = mksem("sr")
        gq = mksem("gq")
        pe = mksem("pe")
        ve = mksem("ve")
        se = mksem("se")
        so = mksem("so")

        SY = Emitter("sync"); GQ = Emitter("gpsimd"); PE = Emitter("tensor")
        V = Emitter("vector"); S = Emitter("scalar")

        for a, b in [(I32, ident_d), (Mneg, mneg_d), (repl4s, repl4_d),
                     (selj8s, selj8_d), (a_col, alog_d), (dtb_s, dtb_d),
                     (dsk_s, dsk_d.rearrange("p a b -> p (a b)")), (bnw_s, bnw_d), (cnw_s, cnw_d),
                     (br_s, br_d), (rp1, repl1_d),
                     (rp2, repl2_d)]:
            SY.op(lambda e, a=a, b=b: e.dma_start(out=a[:], in_=b), sy, 16)
        SY.op(lambda e: e.dma_start(out=wr_s[:].rearrange("p (k c) -> p k c", k=8),
                                    in_=wr_d.rearrange("(k p) c -> p k c", p=128)), sy, 16)

        V.wait(sy)
        V.op(lambda e: e.memset(onec[:], 1.0))
        V.op(lambda e: e.tensor_copy(out=Ibf[:], in_=I32[:]))
        V.op(lambda e: e.tensor_copy(out=repl4bf[:], in_=repl4s[:]))
        V.op(lambda e: e.tensor_scalar(out=repl1w[:], in0=rp1[:], scalar1=bnw_s[:],
                                       scalar2=None, op0=ALU.mult))
        V.op(lambda e: e.tensor_scalar(out=repl2w[:], in0=rp2[:], scalar1=bnw_s[:],
                                       scalar2=None, op0=ALU.mult))
        V.op(lambda e: e.tensor_scalar(out=repl1wc[:], in0=rp1[:], scalar1=cnw_s[:],
                                       scalar2=None, op0=ALU.mult))
        V.op(lambda e: e.tensor_scalar(out=repl2wc[:], in0=rp2[:], scalar1=cnw_s[:],
                                       scalar2=None, op0=ALU.mult))
        V.op(lambda e: e.memset(zcol[:], 0.0))
        V.op(lambda e: e.memset(cns[:, 0:1], 1e-5))
        V.op(lambda e: e.memset(cns[:, 1:2], 1.0))
        V.op(lambda e: e.memset(cns[:, 2:3], 1.0 / DI))
        V.op(lambda e: e.memset(cns[:, 3:4], 1.0 / DM))
        V.op(lambda e: e.memset(cns[:, 4:5], -1.0))
        V.op(lambda e: e.memset(state[:], 0.0))
        V.op(lambda e: e.memset(state_bf[:], 0.0))
        V.op(lambda e: e.memset(h1T[:, 32 * 512:33 * 512], 0.0))
        V.op(lambda e: e.memset(h1T[0:1, 32 * 512:32 * 512 + 512], 1.0), ve)
        S.wait(sy)
        S.op(lambda e: e.activation(out=acol_t[:], in_=a_col[:], func=AF.Exp), se)
        V.wait(se)
        V.op(lambda e: e.tensor_scalar(out=a_col[:], in0=acol_t[:], scalar1=-1.0,
                                       scalar2=None, op0=ALU.mult), ve)

        cast_jobs = []
        wpieces = [(i * 1024, 1024) for i in range(5)] + [(5120, 192)]
        for k in range(8):
            for s0, w in wpieces:
                cast_jobs.append((winT_d[k * 128:(k + 1) * 128, s0:s0 + w],
                                  winTbf_d[k * 128:(k + 1) * 128, s0:s0 + w],
                                  None, w))
        for k in range(16):
            cast_jobs.append((woutT_d[k * 128:(k + 1) * 128, :],
                              woutTbf_d[k * 128:(k + 1) * 128, :], None, DM))
        for half in range(2):
            cast_jobs.append((bbv_d[half], None,
                              bbv_s[:, half * 1024:(half + 1) * 1024], 1024))
            cast_jobs.append((cbv_d[half], None,
                              cbv_s[:, half * 1024:(half + 1) * 1024], 1024))
            cast_jobs.append((nwrep_d[:, half * 1024:(half + 1) * 1024], None,
                              nwrep_s[:, half * 1024:(half + 1) * 1024], 1024))

        def emit_cast_job(srcap, dstap, sdst, w, par):
            GQ.wait(ve)
            GQ.op(lambda e, s=srcap, w=w, par=par: e.dma_start(
                out=cst32[par][:, 0:w], in_=s), gq, 16)
            V.wait(gq)
            if sdst is None:
                V.op(lambda e, w=w, par=par: e.tensor_copy(
                    out=yg[:, par * 1024:par * 1024 + w],
                    in_=cst32[par][:, 0:w]), ve)
                GQ.wait(ve)
                GQ.op(lambda e, d=dstap, w=w, par=par: e.dma_start(
                    out=d, in_=yg[:, par * 1024:par * 1024 + w]), gq, 16)
            else:
                V.op(lambda e, sdst=sdst, w=w, par=par: e.tensor_copy(
                    out=sdst, in_=cst32[par][:, 0:w]), ve)

        for idx, job in enumerate(cast_jobs):
            emit_cast_job(*job, idx % 2)
        scratch_done = gq.count

        wcast_jobs = []
        for e_i in range(E):
            for ft in range(32):
                wcast_jobs.append((
                    ("w1", w1_d[e_i, ft].transpose([1, 0, 2])),
                    w1bf_d[e_i, ft], 1024))
            for fs in range(33):
                wcast_jobs.append((("w2", w2_d[e_i, fs]), w2bf_d[e_i, fs], 1024))
        wcast_pos = [0]

        wc_slots32 = [
            lambda: acc[:, 0:2048].bitcast(F32),
            lambda: acc[:, 2048:4096].bitcast(F32),
            lambda: w1st[0][:],
            lambda: w1st[1][:],
        ]
        wc_state = {"loaded": -1, "gq_cap": [0, 0, 0, 0],
                    "ve_slot": [0, 0, 0, 0], "st_cap": [0, 0]}

        def _wc_src(i):
            (kind, srcap), dstap, w = wcast_jobs[i]
            if kind == "w1":
                return srcap.rearrange("hp k fi -> hp (k fi)") if False else srcap,                     dstap, kind
            return srcap, dstap, kind

        def emit_wcasts(count):
            for _ in range(count):
                i = wcast_pos[0]
                if i >= len(wcast_jobs):
                    return
                wcast_pos[0] += 1
                while wc_state["loaded"] < min(i + 2, len(wcast_jobs) - 1):
                    li = wc_state["loaded"] + 1
                    wc_state["loaded"] = li
                    sl = li % 4
                    (kind, srcap), dstap, w = wcast_jobs[li]
                    GQ.wait(ve, wc_state["ve_slot"][li % 4])
                    if kind == "w1":
                        GQ.op(lambda e, s=srcap, sl=sl: e.dma_start(
                            out=wc_slots32[sl]().rearrange(
                                "p (k c) -> p k c", k=8), in_=s), gq, 16)
                    else:
                        GQ.op(lambda e, s=srcap, sl=sl: e.dma_start(
                            out=wc_slots32[sl](), in_=s), gq, 16)
                    wc_state["gq_cap"][sl] = gq.count
                sl = i % 4
                par = i % 2
                (kind, srcap), dstap, w = wcast_jobs[i]
                V.wait(gq, max(wc_state["gq_cap"][sl], wc_state["st_cap"][par]))
                V.op(lambda e, sl=sl, par=par: e.tensor_copy(
                    out=w1bf[par][:], in_=wc_slots32[sl]()), ve)
                wc_state["ve_slot"][sl] = ve.count
                GQ.wait(ve, ve.count)
                GQ.op(lambda e, d=dstap, par=par: e.dma_start(
                    out=d, in_=w1bf[par][:]), gq, 16)
                wc_state["st_cap"][par] = gq.count

        pe_tile_end = {}
        for t in range(16):
            par = t % 2
            if t >= 2:
                SY.wait(pe, pe_tile_end[t - 2])
            SY.op(lambda e, t=t, par=par: e.dma_start(
                out=usb[par][:], in_=u_d[t * 128:(t + 1) * 128, :]), sy, 16)
            PE.wait(sy)
            for k in range(8):
                bank = pb[k % 2]
                PE.wait(ve, ve.count - 1)
                PE.op(lambda e, k=k, par=par, bank=bank: e.transpose(
                    bank[:, 0:128], usb[par][:, k * 128:(k + 1) * 128], I32[:]), pe)
                V.wait(pe)
                V.op(lambda e, k=k, t=t, bank=bank: e.tensor_copy(
                    out=xT[:, k * L + t * 128: k * L + (t + 1) * 128],
                    in_=bank[:, 0:128]), ve)
            pe_tile_end[t] = pe.count

        PE.wait(gq, scratch_done)
        GQ.wait(gq, scratch_done)

        nat_groups = [(ZC0 + 256 * i, 256) for i in range(8)] + \
                     [(XC0 + 256 * i, 256) for i in range(8)]
        tt_groups = [(BCC0, 128), (DLC0, 64)] + [(THC0 + 128 * j, 128) for j in range(8)]

        proj_caps = [0, 0, 0, 0]

        def proj_stage(c):
            par = c % 2
            xcols = slice(c * 128, (c + 1) * 128)
            for gi, (c0, w) in enumerate(nat_groups + tt_groups):
                is_nat = gi < len(nat_groups)
                bank = pb[gi % 2]
                buf = wgb[gi % 4]
                QE, qs = (SY, sy) if gi % 2 == 0 else (GQ, gq)
                QE.wait(pe, proj_caps[gi % 4])
                QE.op(lambda e, c0=c0, w=w, buf=buf: e.dma_start(
                    out=buf[:, 0:8 * w].rearrange("p (k c) -> p k c", k=8),
                    in_=winTbf_d[:, c0:c0 + w].rearrange("(k p) c -> p k c", p=128)),
                    qs, 16)
                PE.wait(qs)
                PE.wait(ve)
                for k in range(8):
                    if is_nat:
                        PE.op(lambda e, k=k, w=w, buf=buf, bank=bank, xcols=xcols:
                              e.matmul(bank[:, 0:w],
                                       xT[:, k * L:(k + 1) * L][:, xcols],
                                       buf[:, k * w:(k + 1) * w],
                                       start=(k == 0), stop=(k == 7)),
                              pe if k == 7 else None)
                    else:
                        PE.op(lambda e, k=k, w=w, buf=buf, bank=bank, xcols=xcols:
                              e.matmul(bank[0:w, 0:128],
                                       buf[:, k * w:(k + 1) * w],
                                       xT[:, k * L:(k + 1) * L][:, xcols],
                                       start=(k == 0), stop=(k == 7)),
                              pe if k == 7 else None)
                proj_caps[gi % 4] = pe.count
                V.wait(pe)
                if is_nat:
                    if c0 < XC0:
                        off = c0 - ZC0
                        S.wait(pe)
                        S.op(lambda e, off=off, w=w, bank=bank, par=par: e.copy(
                            out=zsb[:, par * DI + off: par * DI + off + w],
                            in_=bank[:, 0:w]), se)
                        V.op(lambda e: e.engine_nop(), ve)
                    else:
                        off = c0 - XC0
                        V.op(lambda e, off=off, w=w, bank=bank, par=par: e.tensor_copy(
                            out=Xsb[:, par * DI + off: par * DI + off + w],
                            in_=bank[:, 0:w]), ve)
                elif c0 == BCC0:
                    V.op(lambda e, bank=bank, par=par: e.tensor_copy(
                        out=bcT[:, par * 128:(par + 1) * 128], in_=bank[:, 0:128]), ve)
                elif c0 == DLC0:
                    V.op(lambda e, bank=bank, par=par: e.tensor_copy(
                        out=dtT[:, par * 128:(par + 1) * 128], in_=bank[0:32, 0:128]))
                    V.op(lambda e, bank=bank, par=par: e.tensor_copy(
                        out=lamT[:, par * 128:(par + 1) * 128], in_=bank[32:64, 0:128]),
                        ve)
                else:
                    j = (c0 - THC0) // 128
                    if c == 0:
                        ini = 0.0
                    else:
                        prev = (c - 1) % 2
                        ini = tcV[:, prev * 1024 + j * 128 + 127:
                                  prev * 1024 + j * 128 + 128]
                    V.op(lambda e, bank=bank, par=par, j=j, ini=ini:
                         e.tensor_tensor_scan(
                             out=tcV[:, par * 1024 + j * 128: par * 1024 + (j + 1) * 128],
                             data0=onec[:], data1=bank[:, 0:128], initial=ini,
                             op0=ALU.mult, op1=ALU.add), ve)
            S.wait(ve)
            S.op(lambda e, par=par: e.activation(
                out=s_t[:], in_=dtT[:, par * 128:(par + 1) * 128], func=AF.Exp,
                bias=dtb_s[:]))
            S.op(lambda e, par=par: e.activation(
                out=dtT[:, par * 128:(par + 1) * 128], in_=s_t[:], func=AF.Ln,
                bias=cns[0:32, 1:2]))
            S.op(lambda e, par=par: e.activation(
                out=lamT[:, par * 128:(par + 1) * 128],
                in_=lamT[:, par * 128:(par + 1) * 128], func=AF.Sigmoid), se)
            V.wait(se)
            V.op(lambda e, par=par: e.tensor_tensor(
                out=g_t[:, par * 128:(par + 1) * 128],
                in0=lamT[:, par * 128:(par + 1) * 128],
                in1=dtT[:, par * 128:(par + 1) * 128], op=ALU.mult))
            V.op(lambda e, par=par: e.tensor_tensor(
                out=q_t[:, par * 128:(par + 1) * 128],
                in0=dtT[:, par * 128:(par + 1) * 128],
                in1=g_t[:, par * 128:(par + 1) * 128], op=ALU.subtract), ve)

        def ssd_stage(c):
            par = c % 2
            npar = (c + 1) % 2
            S.wait(ve)
            S.op(lambda e, par=par: e.activation(
                out=sil, in_=zsb[:, par * DI:(par + 1) * DI], func=AF.Silu), se)
            V.op(lambda e, par=par: e.tensor_tensor(
                out=ghat[:, 0:127], in0=g_t[:, par * 128:par * 128 + 127],
                in1=q_t[:, par * 128 + 1:par * 128 + 128], op=ALU.add))
            qn = zcol[:] if c == NCH - 1 else q_t[:, npar * 128:npar * 128 + 1]
            V.op(lambda e, qn=qn, par=par: e.tensor_tensor(
                out=ghat[:, 127:128], in0=g_t[:, par * 128 + 127:par * 128 + 128],
                in1=qn, op=ALU.add))
            V.op(lambda e, par=par: e.tensor_scalar(
                out=s_t[:], in0=dtT[:, par * 128:(par + 1) * 128], scalar1=a_col[:],
                scalar2=None, op0=ALU.mult))
            V.op(lambda e: e.tensor_tensor_scan(
                out=S_T[:], data0=onec[0:32, :], data1=s_t[:], initial=0.0,
                op0=ALU.mult, op1=ALU.add), ve)
            S.wait(ve)
            S.op(lambda e: e.activation(out=lng[:], in_=ghat[:], func=AF.Ln))
            S.op(lambda e: e.activation(out=expd[:], in_=S_T[:], func=AF.Exp,
                                        bias=S_T[:, 127:128], scale=-1.0))
            S.op(lambda e: e.activation(out=dec_c[:], in_=S_T[:, 127:128],
                                        func=AF.Exp), se)
            V.wait(se)
            V.op(lambda e: e.tensor_tensor(out=pack[0:32, :], in0=lng[:], in1=S_T[:],
                                           op=ALU.subtract))
            V.op(lambda e, par=par: e.tensor_tensor(
                out=pack[32:64, :], in0=g_t[:, par * 128:(par + 1) * 128],
                in1=ghat[:], op=ALU.subtract))
            V.op(lambda e: e.tensor_copy(out=pack[64:96, :], in_=S_T[:]))
            V.op(lambda e: e.tensor_tensor(out=wT[:], in0=ghat[:], in1=expd[:],
                                           op=ALU.mult))
            V.op(lambda e: e.tensor_scalar(out=dec4i[:], in0=selj8s[:],
                                           scalar1=dec_c[:], scalar2=None,
                                           op0=ALU.mult), ve)
            PE.wait(ve)
            PE.op(lambda e: e.transpose(pb[2][:, 0:128], pack[:], I32[:]), pe)
            PE.op(lambda e: e.matmul(pb[3][:, 0:8], repl4s[:], dec4i[:],
                                     start=True, stop=True), pe)
            V.wait(pe)
            V.op(lambda e: e.tensor_copy(out=packN[:], in_=pb[2][:, 0:128]))
            V.op(lambda e: e.tensor_copy(out=dec4[:], in_=pb[3][:, 0:8]), ve)
            S.wait(ve)
            S.op(lambda e: e.activation(out=E_nat[:], in_=packN[:, 64:96],
                                        func=AF.Exp), se)

            tcs = tcV[:, par * 1024:(par + 1) * 1024]
            V.op(lambda e, tcs=tcs: e.tensor_scalar(
                out=atmp[:], in0=tcs, scalar1=1.0 / TWO_PI, scalar2=MAGIC,
                op0=ALU.mult, op1=ALU.add))
            V.op(lambda e: e.tensor_scalar(
                out=atmp[:], in0=atmp[:], scalar1=MAGIC, scalar2=None,
                op0=ALU.subtract))
            V.op(lambda e, tcs=tcs: e.scalar_tensor_tensor(
                out=rmod[:], in0=atmp[:], scalar=-TWO_PI, in1=tcs,
                op0=ALU.mult, op1=ALU.add))
            V.op(lambda e: e.tensor_scalar(
                out=rmod[:], in0=rmod[:], scalar1=-PIC, scalar2=PIC,
                op0=ALU.max, op1=ALU.min))
            V.op(lambda e: e.tensor_scalar(
                out=atmp[:], in0=rmod[:], scalar1=PI / 2, scalar2=None, op0=ALU.is_gt))
            V.op(lambda e: e.tensor_scalar(
                out=carg[:], in0=rmod[:], scalar1=PI / 2, scalar2=None, op0=ALU.add))
            V.op(lambda e: e.scalar_tensor_tensor(
                out=carg[:], in0=atmp[:], scalar=-TWO_PI, in1=carg[:],
                op0=ALU.mult, op1=ALU.add))
            V.op(lambda e: e.tensor_scalar(
                out=carg[:], in0=carg[:], scalar1=-PIC, scalar2=PIC,
                op0=ALU.max, op1=ALU.min), ve)
            S.wait(ve)
            S.op(lambda e: e.activation(out=rmod[:], in_=rmod[:], func=AF.Sin))
            S.op(lambda e: e.activation(out=carg[:], in_=carg[:], func=AF.Sin), se)
            V.wait(se)
            V.op(lambda e: e.tensor_copy(out=sinV[:], in_=rmod[:]))
            V.op(lambda e: e.tensor_copy(out=cosV[:], in_=carg[:]), ve)

            bct = bcT[:, par * 128:(par + 1) * 128]
            V.op(lambda e, bct=bct: e.tensor_tensor(out=sqT[:], in0=bct, in1=bct,
                                                    op=ALU.mult), ve)
            PE.wait(ve)
            PE.op(lambda e: e.matmul(pb[2][0:1, 0:128], onec[0:64, 0:1], sqT[0:64, :],
                                     start=True, stop=True), pe)
            PE.op(lambda e: e.matmul(pb[3][0:1, 0:128], onec[64:128, 0:1],
                                     sqT[64:128, :], start=True, stop=True,
                                     tile_position=(64, 0)), pe)
            S.wait(pe)
            S.op(lambda e: e.activation(out=invB[:], in_=pb[2][0:1, 0:128],
                                        func=AF.Sqrt, scale=1.0 / 64.0, bias=cns[0:1, 0:1]))
            S.op(lambda e: e.activation(out=invC[:], in_=pb[3][0:1, 0:128],
                                        func=AF.Sqrt, scale=1.0 / 64.0, bias=cns[0:1, 0:1]), se)
            V.wait(se)
            V.op(lambda e: e.reciprocal(out=invB[:], in_=invB[:]))
            V.op(lambda e: e.reciprocal(out=invC[:], in_=invC[:]), ve)
            PE.wait(ve)
            PE.op(lambda e: e.matmul(pb[2][0:64, 0:128], onec[0:1, 0:64], invB[:],
                                     start=True, stop=True), pe)
            PE.op(lambda e: e.matmul(pb[3][0:64, 0:128], onec[0:1, 0:64], invC[:],
                                     start=True, stop=True), pe)
            V.wait(pe)
            V.op(lambda e, bct=bct: e.tensor_tensor(out=brawn[:], in0=bct[0:64, :],
                                                    in1=pb[2][0:64, 0:128],
                                                    op=ALU.mult))
            V.op(lambda e, bct=bct: e.tensor_tensor(out=crawn[:], in0=bct[64:128, :],
                                                    in1=pb[3][0:64, 0:128],
                                                    op=ALU.mult))
            V.op(lambda e: e.tensor_copy(
                out=brawn8[:].rearrange("p (r c) -> p r c", r=8),
                in_=brawn[:].unsqueeze(1).broadcast_to([64, 8, 128])))
            V.op(lambda e: e.tensor_copy(
                out=crawn8[:].rearrange("p (r c) -> p r c", r=8),
                in_=crawn[:].unsqueeze(1).broadcast_to([64, 8, 128])), ve)
            PE.wait(ve)
            for half in range(2):
                rw = repl1w if half == 0 else repl2w
                rwc = repl1wc if half == 0 else repl2wc
                dst_b = vB1 if half == 0 else vB2
                dst_c = vC1 if half == 0 else vC2
                for piece in range(2):
                    PE.wait(ve)
                    PE.op(lambda e, rw=rw, piece=piece: e.matmul(
                        pb[2][:, 0:512], rw[:],
                        brawn8[:, piece * 512:(piece + 1) * 512],
                        start=True, stop=True), pe)
                    PE.op(lambda e, rwc=rwc, piece=piece: e.matmul(
                        pb[3][:, 0:512], rwc[:],
                        crawn8[:, piece * 512:(piece + 1) * 512],
                        start=True, stop=True), pe)
                    V.wait(pe)
                    V.op(lambda e, dst_b=dst_b, piece=piece, half=half: e.tensor_tensor(
                        out=dst_b[:, piece * 512:(piece + 1) * 512],
                        in0=bbv_s[:, half * 1024 + piece * 512:
                                  half * 1024 + (piece + 1) * 512],
                        in1=pb[2][:, 0:512], op=ALU.add))
                    V.op(lambda e, dst_c=dst_c, piece=piece, half=half: e.tensor_tensor(
                        out=dst_c[:, piece * 512:(piece + 1) * 512],
                        in0=cbv_s[:, half * 1024 + piece * 512:
                                  half * 1024 + (piece + 1) * 512],
                        in1=pb[3][:, 0:512], op=ALU.add), ve)
            V.wait(se)
            V.wait(pe)
            for (v1, v2) in ((vB1, vB2), (vC1, vC2)):
                V.op(lambda e, v1=v1: e.tensor_tensor(out=rta[:], in0=v1[:],
                                                      in1=cosV[:], op=ALU.mult))
                V.op(lambda e, v2=v2: e.tensor_tensor(out=rtb[:], in0=v2[:],
                                                      in1=sinV[:], op=ALU.mult))
                V.op(lambda e, v2=v2: e.tensor_tensor(out=v2[:], in0=v2[:],
                                                      in1=cosV[:], op=ALU.mult))
                V.op(lambda e, v1=v1: e.tensor_tensor(out=v1[:], in0=v1[:],
                                                      in1=sinV[:], op=ALU.mult))
                V.op(lambda e, v1=v1, v2=v2: e.tensor_tensor(out=v2[:], in0=v2[:],
                                                             in1=v1[:], op=ALU.add))
                V.op(lambda e, v1=v1: e.tensor_tensor(out=v1[:], in0=rta[:],
                                                      in1=rtb[:], op=ALU.subtract), ve)
            V.op(lambda e: e.tensor_tensor(
                out=wvsel[:].rearrange("p (r c) -> p r c", r=8),
                in0=wT[:].unsqueeze(1).broadcast_to([32, 8, 128]),
                in1=selj8s[:].unsqueeze(2).broadcast_to([32, 8, 128]),
                op=ALU.mult), ve)
            PE.wait(ve)
            for piece in range(2):
                PE.op(lambda e, piece=piece: e.matmul(
                    pb[2 + piece][:, 0:512], repl4bf[:],
                    wvsel[:, piece * 512:(piece + 1) * 512],
                    start=True, stop=True), pe)
            V.wait(pe)
            for piece in range(2):
                V.op(lambda e, piece=piece: e.tensor_tensor(
                    out=vBW1[:, piece * 512:(piece + 1) * 512],
                    in0=vB1[:, piece * 512:(piece + 1) * 512],
                    in1=pb[2 + piece][:, 0:512], op=ALU.mult))
                V.op(lambda e, piece=piece: e.tensor_tensor(
                    out=vBW2[:, piece * 512:(piece + 1) * 512],
                    in0=vB2[:, piece * 512:(piece + 1) * 512],
                    in1=pb[2 + piece][:, 0:512], op=ALU.mult), ve)
            for j in range(8):
                for half in range(2):
                    src = vBW1 if half == 0 else vBW2
                    bank = pb[2 + ((j * 2 + half) % 2)]
                    PE.wait(ve)
                    PE.op(lambda e, src=src, j=j, bank=bank: e.matmul(
                        bank[:, 0:128], src[:, j * 128:(j + 1) * 128], Ibf[:],
                        start=True, stop=True), pe)
                    S.wait(pe)
                    S.op(lambda e, j=j, half=half, bank=bank: e.copy(
                        out=bpp.rearrange("p (jj mm n) -> p jj mm n", jj=8, mm=4)
                            [:, j, :, half * 32:half * 32 + 32],
                        in_=bank[:, 0:128].rearrange("p (m k) -> p m k", m=4)), se)
                    PE.wait(se)

            cap_tr = [0, 0]; cap_d2 = [0, 0]; cap_g = [0, 0]
            cap_cb = [0, 0]; cap_yx = [0, 0]; cap_ysc = [0, 0]
            cap_d3v = [0, 0]; cap_ys = [0, 0]
            cap_prev_state = ve.count
            cap_ygw = ve.count
            for m in range(4):
                pbYx = pb[4 + (m % 2)]
                pbYs = pb[6 + (m % 2)]
                pm = slice(32 * m, 32 * m + 32)
                for jj in range(8):
                    h = 4 * jj + m
                    j = jj
                    dpar = jj % 2
                    cbank = pb[jj % 2]
                    V.wait(pe, cap_tr[dpar])
                    V.op(lambda e, h=h, dpar=dpar: e.tensor_tensor(
                        out=pre_t[dpar][:], in0=Mneg[:],
                        in1=packN[:, 64 + h:65 + h].broadcast_to([128, 128]),
                        op=ALU.add), ve)
                    cap_prev = ve.count
                    PE.wait(ve, cap_prev)
                    PE.wait(se, cap_d2[dpar])
                    PE.op(lambda e, dpar=dpar: e.transpose(
                        pb[2 + dpar][:, 0:128], pre_t[dpar][:], I32[:]), pe)
                    cap_tr[dpar] = pe.count
                    PE.wait(ve, cap_g[jj % 2])
                    PE.op(lambda e, pm=pm, j=j, cbank=cbank, m=m: e.matmul(
                        cbank[:, 0:128], vB1[pm, j * 128:(j + 1) * 128],
                        vC1[pm, j * 128:(j + 1) * 128], start=True, stop=False,
                        tile_position=(32 * m, 0)))
                    PE.op(lambda e, pm=pm, j=j, cbank=cbank, m=m: e.matmul(
                        cbank[:, 0:128], vB2[pm, j * 128:(j + 1) * 128],
                        vC2[pm, j * 128:(j + 1) * 128], start=False, stop=True,
                        tile_position=(32 * m, 0)), pe)
                    cap_cb[dpar] = pe.count
                    S.wait(pe, cap_tr[dpar])
                    S.wait(ve, cap_d3v[dpar])
                    S.op(lambda e, dpar=dpar, h=h: e.activation(
                        out=D2[dpar][:], in_=pb[2 + dpar][:, 0:128], func=AF.Exp,
                        bias=packN[:, h:h + 1]), se)
                    cap_d2[dpar] = se.count
                    V.wait(se, cap_d2[dpar])
                    V.op(lambda e, dpar=dpar, h=h: e.scalar_tensor_tensor(
                        out=D2[dpar][:], in0=I32[:], scalar=packN[:, 32 + h:33 + h],
                        in1=D2[dpar][:], op0=ALU.mult, op1=ALU.add))
                    V.wait(pe, cap_cb[dpar])
                    V.wait(pe, cap_yx[dpar])
                    V.op(lambda e, dpar=dpar, cbank=cbank: e.tensor_tensor(
                        out=G_bf[dpar][:], in0=D2[dpar][:], in1=cbank[:, 0:128],
                        op=ALU.mult), ve)
                    cap_g[dpar] = ve.count
                    cap_d3v[dpar] = ve.count
                    PE.wait(ve, max(cap_g[dpar], cap_prev_state))
                    if jj == 0:
                        PE.wait(se, cap_ysc[1])
                    PE.op(lambda e, dpar=dpar, h=h, jj=jj, pbYx=pbYx, par=par:
                          e.matmul(pbYx[:, jj * 64:(jj + 1) * 64], G_bf[dpar][:],
                                   Xsb[:, par * DI + h * 64: par * DI + (h + 1) * 64],
                                   start=True, stop=True, skip_group_check=True),
                          pe)
                    cap_yx[dpar] = pe.count
                    PE.op(lambda e, pm=pm, j=j, jj=jj, pbYs=pbYs, m=m: e.matmul(
                        pbYs[:, jj * 64:(jj + 1) * 64],
                        vC1[pm, j * 128:(j + 1) * 128],
                        state_bf[pm, j * 128:j * 128 + 64], start=True,
                        stop=False, tile_position=(32 * m, 0),
                        skip_group_check=True))
                    PE.op(lambda e, pm=pm, j=j, jj=jj, pbYs=pbYs, m=m: e.matmul(
                        pbYs[:, jj * 64:(jj + 1) * 64],
                        vC2[pm, j * 128:(j + 1) * 128],
                        state_bf[pm, j * 128 + 64:(j + 1) * 128], start=False,
                        stop=True, tile_position=(32 * m, 0),
                        skip_group_check=True), pe)
                    cap_ys[dpar] = pe.count
                    S.wait(pe, cap_ys[dpar])
                    if jj == 0:
                        S.wait(ve, cap_ygw)
                    S.op(lambda e, jj=jj, h=h, pbYs=pbYs: e.activation(
                        out=ysc[:, jj * 64:(jj + 1) * 64],
                        in_=pbYs[:, jj * 64:(jj + 1) * 64],
                        func=AF.Identity, scale=E_nat[:, h:h + 1]), se)
                    cap_ysc[0] = cap_ysc[1]
                    cap_ysc[1] = se.count
                V.wait(se)
                V.op(lambda e, pbYx=pbYx: e.tensor_tensor(
                    out=ysc[:], in0=ysc[:], in1=pbYx[:, 0:512], op=ALU.add))
                V.op(lambda e, m=m, par=par: e.tensor_tensor(
                    out=rmod[:, 0:512].rearrange("p (j q) -> p j q", j=8),
                    in0=Xsb[:, par * DI:(par + 1) * DI].rearrange(
                        "p (j hm q) -> p j hm q", j=8, hm=4)[:, :, m, :],
                    in1=dsk_s[:].rearrange("p (a b) -> p a b", a=4)[:, m, :]
                        .unsqueeze(2).broadcast_to([128, 8, 64]), op=ALU.mult))
                V.op(lambda e: e.tensor_tensor(
                    out=ysc[:], in0=ysc[:], in1=rmod[:, 0:512], op=ALU.add))
                V.op(lambda e, m=m: e.tensor_copy(
                    out=yg[:].rearrange("p (j hm q) -> p j hm q", j=8, hm=4)[:, :, m, :],
                    in_=ysc[:].rearrange("p (j c) -> p j c", j=8)), ve)
                cap_ygw = ve.count

            V.op(lambda e: e.tensor_tensor(out=yg[:], in0=yg[:], in1=sil,
                                           op=ALU.mult), ve)
            PE.wait(ve)
            for m in range(4):
                pm = slice(32 * m, 32 * m + 32)
                for j in range(8):
                    h = 4 * j + m
                    bank = pb[6] if j < 4 else pb[7]
                    jo = (j % 4) * 128
                    PE.op(lambda e, h=h, pm=pm, bank=bank, jo=jo, m=m, par=par:
                          e.matmul(bank[pm, jo:jo + 64],
                                   bpp[:, h * 64:h * 64 + 32],
                                   Xsb[:, par * DI + h * 64: par * DI + (h + 1) * 64],
                                   start=True, stop=True,
                                   tile_position=(0, 32 * m),
                                   skip_group_check=True))
                    PE.op(lambda e, h=h, pm=pm, bank=bank, jo=jo, m=m, par=par:
                          e.matmul(bank[pm, jo + 64:jo + 128],
                                   bpp[:, h * 64 + 32:(h + 1) * 64],
                                   Xsb[:, par * DI + h * 64: par * DI + (h + 1) * 64],
                                   start=True, stop=True,
                                   tile_position=(0, 32 * m),
                                   skip_group_check=True),
                          pe if (j in (3, 7)) else None)
            V.wait(pe)
            V.op(lambda e: e.tensor_tensor(
                out=state[:].rearrange("p (j c) -> p j c", j=8),
                in0=state[:].rearrange("p (j c) -> p j c", j=8),
                in1=dec4[:].unsqueeze(2).broadcast_to([128, 8, 128]), op=ALU.mult))
            V.op(lambda e: e.tensor_tensor(out=state[:, 0:512], in0=state[:, 0:512],
                                           in1=pb[6][:, 0:512], op=ALU.add))
            V.op(lambda e: e.tensor_tensor(out=state[:, 512:1024],
                                           in0=state[:, 512:1024],
                                           in1=pb[7][:, 0:512], op=ALU.add))
            V.op(lambda e: e.tensor_copy(out=state_bf[:], in_=state[:]), ve)

            V.op(lambda e: e.tensor_tensor(out=rmod[:], in0=yg[:, 0:1024],
                                           in1=yg[:, 0:1024], op=ALU.mult))
            V.op(lambda e: e.tensor_tensor(out=rln[:], in0=yg[:, 1024:2048],
                                           in1=yg[:, 1024:2048], op=ALU.mult))
            V.op(lambda e: e.tensor_reduce(out=cols[:, 0:1], in_=rmod[:], axis=AX.X,
                                           op=ALU.add))
            V.op(lambda e: e.tensor_reduce(out=cols[:, 1:2], in_=rln[:], axis=AX.X,
                                           op=ALU.add), ve)
            S.wait(ve)
            S.op(lambda e: e.activation(out=cols[:, 12:13], in_=cols[:, 0:1],
                                        func=AF.Identity, bias=cols[:, 1:2]))
            S.op(lambda e: e.activation(out=cols[:, 1:2], in_=cols[:, 12:13],
                                        func=AF.Sqrt, scale=1.0 / DI,
                                        bias=cns[:, 0:1]), se)
            V.wait(se)
            V.op(lambda e: e.reciprocal(out=cols[:, 13:14], in_=cols[:, 1:2]), ve)
            S.wait(ve)
            S.op(lambda e: e.activation(out=cols[:, 2:3], in_=cols[:, 13:14],
                                        func=AF.Identity), se)
            V.wait(se)
            V.op(lambda e: e.tensor_tensor(
                out=yg[:], in0=yg[:],
                in1=cols[:, 2:3].broadcast_to([128, DI]), op=ALU.mult))
            V.op(lambda e: e.tensor_tensor(out=yg[:], in0=yg[:], in1=nwrep_s[:],
                                           op=ALU.mult), ve)
            for k in range(16):
                bank = pb[2 + (k % 2)]
                ytd = vB1 if k < 8 else vB2
                ko = k % 8
                PE.wait(ve)
                PE.wait(se)
                PE.op(lambda e, k=k, bank=bank: e.matmul(
                    bank[:, 0:128], yg[:, k * 128:(k + 1) * 128], Ibf[:],
                    start=True, stop=True), pe)
                S.wait(pe)
                S.op(lambda e, ko=ko, bank=bank, ytd=ytd: e.copy(
                    out=ytd[:, ko * 128:(ko + 1) * 128],
                    in_=bank[:, 0:128]), se)
            for k in range(16):
                ytd = vB1 if k < 8 else vB2
                ko = k % 8
                GQ.wait(pe)
                GQ.op(lambda e, k=k: e.dma_start(
                    out=wob[k % 2][:], in_=woutTbf_d[k * 128:(k + 1) * 128, :]),
                    gq, 16)
                PE.wait(gq)
                PE.wait(ve)
                PE.wait(se)
                for nh in range(2):
                    PE.op(lambda e, ko=ko, nh=nh, k=k, ytd=ytd: e.matmul(
                        pb[nh][:, 0:512], ytd[:, ko * 128:(ko + 1) * 128],
                        wob[k % 2][:, nh * 512:(nh + 1) * 512],
                        start=(k == 0), stop=(k == 15)), pe)
            SY.wait(ve)
            SY.op(lambda e, c=c, par=par: e.dma_start(
                out=usb[par][:], in_=u_d[c * 128:(c + 1) * 128, :]), sy, 16)
            V.wait(sy)
            V.wait(pe)
            V.op(lambda e, par=par: e.tensor_tensor(
                out=rln[:, 0:512], in0=usb[par][:, 0:512], in1=pb[0][:, 0:512],
                op=ALU.add))
            V.op(lambda e, par=par: e.tensor_tensor(
                out=rln[:, 512:1024], in0=usb[par][:, 512:1024], in1=pb[1][:, 0:512],
                op=ALU.add))
            V.op(lambda e: e.tensor_reduce(out=cols[:, 2:3], in_=rln[:], axis=AX.X,
                                           op=ALU.add), ve)
            S.wait(ve)
            S.op(lambda e: e.activation(out=cols[:, 4:5], in_=cols[:, 2:3],
                                        func=AF.Identity, scale=1.0 / DM), se)
            V.wait(se)
            V.op(lambda e: e.tensor_tensor(
                out=xsq[:], in0=rln[:],
                in1=cols[:, 4:5].broadcast_to([128, DM]), op=ALU.subtract))
            V.wait(sr)
            V.op(lambda e: e.tensor_tensor(out=xln[:], in0=xsq[:], in1=xsq[:],
                                           op=ALU.mult))
            V.op(lambda e: e.tensor_reduce(out=cols[:, 5:6], in_=xln[:], axis=AX.X,
                                           op=ALU.add), ve)
            S.wait(ve)
            S.op(lambda e: e.activation(out=cols[:, 8:9], in_=cols[:, 5:6],
                                        func=AF.Sqrt, scale=1.0 / DM,
                                        bias=cns[:, 0:1]), se)
            V.wait(se)
            V.op(lambda e: e.reciprocal(out=cols[:, 9:10], in_=cols[:, 8:9]), ve)
            S.wait(ve)
            S.op(lambda e: e.activation(out=cols[:, 10:11], in_=cols[:, 9:10],
                                        func=AF.Identity), se)
            V.wait(se)
            V.op(lambda e: e.tensor_tensor(
                out=xln[:], in0=xsq[:],
                in1=cols[:, 10:11].broadcast_to([128, DM]), op=ALU.mult), ve)
            SY.wait(ve)
            SY.op(lambda e, c=c: e.dma_start(
                out=r1scr_d[c * 128:(c + 1) * 128, :], in_=xln[:]), sr, 16)
            for k in range(8):
                bank = pb[2 + (k % 2)]
                PE.wait(ve)
                PE.op(lambda e, k=k, bank=bank: e.transpose(
                    bank[:, 0:128], xln[:, k * 128:(k + 1) * 128], I32[:]), pe)
                S.wait(pe)
                S.op(lambda e, bank=bank: e.copy(out=xt32[:],
                                                 in_=bank[:, 0:128]))
                S.op(lambda e, k=k, c=c, bank=bank: e.copy(
                    out=xT[:, k * L + c * 128:k * L + (c + 1) * 128],
                    in_=bank[:, 0:128]), se)
                PE.wait(se)
                PE.op(lambda e, k=k: e.matmul(
                    pb[4][0:8, 0:128], wr_s[:, k * 8:(k + 1) * 8], xt32[:],
                    start=(k == 0), stop=(k == 7)), pe if k == 7 else None)
            S.wait(pe)
            S.op(lambda e: e.activation(out=logT[:], in_=pb[4][0:8, 0:128],
                                        func=AF.Identity, bias=br_s[:]), se)
            PE.wait(se)
            PE.wait(ve)
            PE.op(lambda e: e.transpose(pb[5][:, 0:8], logT[:], I32[0:8, 0:8]), pe)
            V.wait(pe)
            V.op(lambda e: e.tensor_reduce(out=cols[:, 14:15], in_=pb[5][:, 0:8],
                                           axis=AX.X, op=ALU.max), ve)
            S.wait(ve)
            S.op(lambda e: e.activation(out=cols[:, 13:14], in_=cols[:, 14:15],
                                        func=AF.Identity), se)
            V.wait(se)
            V.op(lambda e: e.tensor_tensor(
                out=xt32[:, 8:16], in0=pb[5][:, 0:8],
                in1=cols[:, 13:14].broadcast_to([128, 8]), op=ALU.is_ge))
            V.op(lambda e: e.tensor_tensor(out=cols[:, 15:16], in0=cols[:, 13:14],
                                           in1=cns[:, 4:5], op=ALU.mult), ve)
            S.wait(ve)
            S.op(lambda e: e.activation(out=xt32[:, 0:8], in_=pb[5][:, 0:8],
                                        func=AF.Exp, bias=cols[:, 15:16],
                                        accum_out=cols[:, 11:12]), se)
            V.wait(se)
            V.op(lambda e: e.reciprocal(out=cols[:, 12:13], in_=cols[:, 11:12]), ve)
            S.wait(ve)
            S.op(lambda e: e.activation(out=cols[:, 10:11], in_=cols[:, 12:13],
                                        func=AF.Identity), se)
            V.wait(se)
            V.op(lambda e, c=c: e.tensor_tensor(
                out=gsel[:, c * 8:(c + 1) * 8], in0=xt32[:, 8:16],
                in1=cols[:, 10:11].broadcast_to([128, 8]), op=ALU.mult), ve)

        emit_wcasts(20)
        proj_stage(0)
        nssd = NCH if upto is None else min(NCH, upto)
        for c in range(nssd):
            if c + 1 < NCH:
                proj_stage(c + 1)
            ssd_stage(c)
            emit_wcasts(33)

        emit_wcasts(len(wcast_jobs))
        GQ.wait(gq)
        PE.wait(gq)
        if upto is None:
            for tg in range(4):
                V.op(lambda e: e.memset(acc[:], 0.0), ve)
                for e_i in range(E):
                    par_e = e_i % 2
                    SY.wait(se)
                    SY.op(lambda e, e_i=e_i, par_e=par_e: e.dma_start(
                        out=b1c_s[par_e][:], in_=b1_d[e_i]), sy, 16)
                    m1_base = pe.count
                    m1off = {1: -1, 2: 0, 3: -5, 4: -4, 5: -3}
                    se_caps = []
                    for ft in range(32):
                        par = ft % 6
                        QE, qs = (SY, sy) if ft % 2 == 0 else (GQ, gq)
                        if ft == 0:
                            QE.wait(pe, pe.count)
                        elif ft < 6:
                            QE.wait(pe, m1_base + m1off[ft])
                        else:
                            QE.wait(pe, pe.count - 5)
                        QE.op(lambda e, e_i=e_i, ft=ft, par=par: e.dma_start(
                            out=w1bf4[par][:], in_=w1bf_d[e_i, ft]), qs, 16)
                        bank = pb[ft % 4]
                        PE.wait(qs)
                        if ft < 4:
                            PE.wait(se)
                        else:
                            PE.wait(se, se_caps[ft - 4])
                        for k in range(8):
                            PE.op(lambda e, k=k, par=par, bank=bank, tg=tg: e.matmul(
                                bank[:, 0:512], w1bf4[par][:, k * 128:(k + 1) * 128],
                                xT[:, k * L + tg * 512:k * L + (tg + 1) * 512],
                                start=(k == 0), stop=(k == 7)),
                                pe if k == 7 else None)
                        S.wait(pe)
                        S.wait(sy)
                        S.op(lambda e, ft=ft, bank=bank, par_e=par_e: e.activation(
                            out=h1T[:, ft * 512:(ft + 1) * 512], in_=bank[:, 0:512],
                            func=AF.Relu, bias=b1c_s[par_e][:, ft:ft + 1]), se)
                        se_caps.append(se.count)
                    m2_base = pe.count
                    m2off = {1: 0, 2: -5, 3: -4, 4: -3, 5: -2}
                    for fs in range(33):
                        par = fs % 6
                        QE, qs = (SY, sy) if fs % 2 == 0 else (GQ, gq)
                        if fs == 0:
                            QE.wait(pe, pe.count)
                        elif fs < 6:
                            QE.wait(pe, m2_base + m2off[fs])
                        else:
                            QE.wait(pe, pe.count - 5)
                        QE.op(lambda e, e_i=e_i, fs=fs, par=par: e.dma_start(
                            out=w2bf4[par][:], in_=w2bf_d[e_i, fs]), qs, 16)
                        PE.wait(qs)
                        if fs == 0:
                            PE.wait(se)
                        for tt in range(4):
                            for nh in range(2):
                                PE.op(lambda e, fs=fs, tt=tt, nh=nh, par=par: e.matmul(
                                    pb[tt * 2 + nh][:, 0:512],
                                    h1T[:, fs * 512 + tt * 128:
                                        fs * 512 + (tt + 1) * 128],
                                    w2bf4[par][:, nh * 512:(nh + 1) * 512],
                                    start=(fs == 0), stop=(fs == 32)),
                                    pe if (tt == 3 and nh == 1) else None)
                    V.wait(pe)
                    for tt in range(4):
                        ci = tg * 4 + tt
                        for nh in range(2):
                            V.op(lambda e, tt=tt, nh=nh, ci=ci, e_i=e_i:
                                 e.scalar_tensor_tensor(
                                     out=acc[:, tt * DM + nh * 512:
                                            tt * DM + (nh + 1) * 512],
                                     in0=pb[tt * 2 + nh][:, 0:512],
                                     scalar=gsel[:, ci * 8 + e_i:ci * 8 + e_i + 1],
                                     in1=acc[:, tt * DM + nh * 512:
                                            tt * DM + (nh + 1) * 512],
                                     op0=ALU.mult, op1=ALU.add), ve)
                if e_i == E - 1:
                    for tt in range(4):
                        ci = tg * 4 + tt
                        par = tt % 2
                        SY.wait(ve)
                        SY.op(lambda e, ci=ci, par=par: e.dma_start(
                            out=usb[par][:], in_=r1scr_d[ci * 128:(ci + 1) * 128, :]),
                            sy, 16)
                        V.wait(sy)
                        V.op(lambda e, tt=tt, par=par: e.tensor_tensor(
                            out=rln[:], in0=usb[par][:],
                            in1=acc[:, tt * DM:(tt + 1) * DM], op=ALU.add))
                        V.op(lambda e: e.tensor_reduce(
                            out=cols[:, 2:3], in_=rln[:], axis=AX.X, op=ALU.add), ve)
                        S.wait(ve)
                        S.op(lambda e: e.activation(
                            out=cols[:, 4:5], in_=cols[:, 2:3], func=AF.Identity,
                            scale=1.0 / DM), se)
                        V.wait(se)
                        V.op(lambda e: e.tensor_tensor(
                            out=xsq[:], in0=rln[:],
                            in1=cols[:, 4:5].broadcast_to([128, DM]),
                            op=ALU.subtract))
                        V.op(lambda e: e.tensor_tensor(
                            out=w1st[0][:, 0:512], in0=xsq[:, 0:512],
                            in1=xsq[:, 0:512], op=ALU.mult))
                        V.op(lambda e: e.tensor_tensor(
                            out=w1st[0][:, 512:1024], in0=xsq[:, 512:1024],
                            in1=xsq[:, 512:1024], op=ALU.mult))
                        V.op(lambda e: e.tensor_reduce(
                            out=cols[:, 5:6], in_=w1st[0][:, 0:512], axis=AX.X,
                            op=ALU.add))
                        V.op(lambda e: e.tensor_reduce(
                            out=cols[:, 6:7], in_=w1st[0][:, 512:1024], axis=AX.X,
                            op=ALU.add),
                            ve)
                        S.wait(ve)
                        S.op(lambda e: e.activation(
                            out=cols[:, 7:8], in_=cols[:, 5:6], func=AF.Identity,
                            bias=cols[:, 6:7]))
                        S.op(lambda e: e.activation(
                            out=cols[:, 10:11], in_=cols[:, 7:8], func=AF.Sqrt,
                            scale=1.0 / DM, bias=cns[:, 0:1]), se)
                        V.wait(se)
                        V.op(lambda e: e.reciprocal(out=cols[:, 11:12],
                                                    in_=cols[:, 10:11]), ve)
                        S.wait(ve)
                        S.op(lambda e: e.activation(
                            out=cols[:, 12:13], in_=cols[:, 11:12],
                            func=AF.Identity), se)
                        V.wait(se)
                        V.wait(so)
                        V.op(lambda e, par=par: e.tensor_tensor(
                            out=xo32[par][:], in0=xsq[:],
                            in1=cols[:, 12:13].broadcast_to([128, DM]),
                            op=ALU.mult), ve)
                        SY.wait(ve)
                        SY.op(lambda e, ci=ci, par=par: e.dma_start(
                            out=out_d[ci * 128:(ci + 1) * 128, :], in_=xo32[par][:]),
                            so, 16)
            SY.wait(so)

        if taps:
            SY.wait(ve); SY.wait(se); SY.wait(pe); SY.wait(gq, gq.count)
            for tname, sname, shape in taps:
                t = SBUF[sname]
                td = dt("tap_" + tname, list(shape), t.dtype,
                        kind="ExternalOutput").ap()
                SY.op(lambda e, t=t, td=td, shape=shape: e.dma_start(
                    out=td, in_=t[0:shape[0], 0:shape[1]]), so, 16)
            SY.wait(so)

        with nc.Block() as block:
            @block.sync
            def _(eng):
                SY.replay(eng)

            @block.gpsimd
            def _(eng):
                GQ.replay(eng)

            @block.tensor
            def _(eng):
                PE.replay(eng)

            @block.vector
            def _(eng):
                V.replay(eng)

            @block.scalar
            def _(eng):
                S.replay(eng)

    return nc


def _host_inputs(inputs):
    u = np.asarray(inputs["L0_attn_u"], np.float32)
    ipw = np.asarray(inputs["L0_attn_in_proj_w"], np.float32)
    perm = np.concatenate([
        np.arange(0, 2048), np.arange(2048, 4096), np.arange(4096, 4224),
        np.arange(4224, 4256), np.arange(5280, 5312), np.arange(4256, 5280)])
    winT = np.ascontiguousarray(ipw[perm].T)
    woutT = np.ascontiguousarray(
        np.asarray(inputs["L0_attn_out_proj_w"], np.float32).T)
    W1 = np.asarray(inputs["L0_ffn_W1"], np.float32)
    w1r = np.ascontiguousarray(
        W1.reshape(E, 8, 128, 32, 128).transpose(0, 3, 1, 2, 4))
    W2 = np.asarray(inputs["L0_ffn_W2"], np.float32)
    b2 = np.asarray(inputs["L0_ffn_b2"], np.float32)
    w2p = np.zeros((E, 33, 128, DM), np.float32)
    w2p[:, :32] = W2.reshape(E, 32, 128, DM)
    w2p[:, 32, 0, :] = b2
    b1 = np.asarray(inputs["L0_ffn_b1"], np.float32)
    b1c = np.ascontiguousarray(b1.reshape(E, 32, 128).transpose(0, 2, 1))
    Bb = np.asarray(inputs["L0_attn_B_bias"], np.float32)
    Cb = np.asarray(inputs["L0_attn_C_bias"], np.float32)

    def vint(bias):
        outs = np.zeros((2, 128, 1024), np.float32)
        m2g, kg = np.divmod(np.arange(128), 32)
        for half in range(2):
            for j in range(8):
                outs[half, :, 128 * j:128 * (j + 1)] = \
                    bias[4 * j + m2g, half * 32 + kg][:, None]
        return outs

    nwrep = np.ascontiguousarray(np.broadcast_to(
        np.asarray(inputs["L0_attn_norm_w"], np.float32)[None, :], (128, DI)))
    ii, jjx = np.meshgrid(np.arange(128), np.arange(128), indexing="ij")
    mneg = np.where(ii < jjx, -1e30, 0.0).astype(np.float32)
    kk = np.arange(128) % 32
    shared = dict(
        winT=winT, woutT=woutT,
        wr=np.ascontiguousarray(np.asarray(inputs["L0_ffn_Wr"], np.float32)),
        br=np.asarray(inputs["L0_ffn_br"], np.float32).reshape(E, 1),
        w1=w1r, b1c=b1c, w2=w2p,
        dtb=np.asarray(inputs["L0_attn_dt_bias"], np.float32).reshape(H, 1),
        alog=np.asarray(inputs["L0_attn_A_log"], np.float32).reshape(H, 1),
        dsk4=np.ascontiguousarray(np.broadcast_to(
            np.asarray(inputs["L0_attn_D"], np.float32).reshape(-1)[
                (4 * np.arange(8)[None, :] + np.arange(4)[:, None])][None, :, :],
            (128, 4, 8))),
        bnw=np.asarray(inputs["L0_attn_B_norm_w"], np.float32).reshape(N, 1),
        cnw=np.asarray(inputs["L0_attn_C_norm_w"], np.float32).reshape(N, 1),
        bbv=vint(Bb), cbv=vint(Cb), nwrep=nwrep,
        ident=np.eye(128, dtype=np.float32), mneg=mneg,
        repl1=(np.arange(N)[:, None] == kk[None, :]).astype(np.float32),
        repl2=(np.arange(N)[:, None] == (32 + kk)[None, :]).astype(np.float32),
        repl4=((np.arange(H)[:, None] % 4) ==
               (np.arange(128)[None, :] // 32)).astype(np.float32),
        selj8=((np.arange(H)[:, None] // 4) ==
               np.arange(8)[None, :]).astype(np.float32),
    )
    in_maps = []
    for core in range(8):
        d = dict(shared)
        d["u"] = np.ascontiguousarray(u[core])
        in_maps.append(d)
    return in_maps


def kernel(**inputs):
    in_maps = _host_inputs(inputs)
    nc = build_program()
    res = run_bass_kernel_spmd(nc, in_maps, core_ids=list(range(8)))
    out = np.stack([np.asarray(r["out"], np.float32) for r in res.results], axis=0)
    return out
